# version 20
# baseline (speedup 1.0000x reference)
import numpy as np
import ml_dtypes
from contextlib import ExitStack
import concourse.bass as bass
import concourse.mybir as mybir
from concourse.bass_utils import run_bass_kernel_spmd

F32 = mybir.dt.float32
BF16 = mybir.dt.bfloat16
I32 = mybir.dt.int32
AF = mybir.ActivationFunctionType
ALU = mybir.AluOpType
AX = mybir.AxisListType.X
BF = ml_dtypes.bfloat16

T = 2048
D = 1024
NDT = 8
NTC = 4
H = 16
DH = 64
EPS_LN = 1e-5

ENGS = ("sync", "gpsimd", "tensor", "vector", "scalar")


class Sem:
    __slots__ = ("h", "n")

    def __init__(self, h):
        self.h = h
        self.n = 0


class Prog:

    def __init__(self):
        self.nc = bass.Bass("TRN2", target_bir_lowering=False)
        self.stack = ExitStack()
        self.scopes = []
        self.names = set()
        self.q = {e: [] for e in ENGS}
        self.sD = self.sem("sD")
        self.sW = self.sem("sW")
        self.sT = self.sem("sT")
        self.sV = self.sem("sV")
        self.sS = self.sem("sS")
        self.sG = self.sem("sG")

    def _uniq(self, name):
        n, k = name, 0
        while n in self.names:
            k += 1
            n = f"{name}_{k}"
        self.names.add(n)
        return n

    def _enter(self, cm):
        st_ = self.scopes[-1] if self.scopes else self.stack
        return st_.enter_context(cm)

    def push_scope(self):
        self.scopes.append(ExitStack())

    def pop_scope(self):
        self.scopes.pop().close()

    def sem(self, name):
        return Sem(self.stack.enter_context(self.nc.semaphore(self._uniq(name))))

    def sb(self, name, shape, dt=F32):
        return self._enter(self.nc.sbuf_tensor(self._uniq(name), shape, dt))

    def ps(self, name, shape, dt=F32):
        return self.stack.enter_context(self.nc.psum_tensor(name, shape, dt))

    def inp(self, name, shape, dt=F32):
        return self.nc.dram_tensor(name, shape, dt, kind="ExternalInput").ap()

    def outp(self, name, shape, dt=F32):
        return self.nc.dram_tensor(name, shape, dt, kind="ExternalOutput").ap()

    def op(self, eng, fn, inc=None, k=1):
        if inc is not None:
            inc.n += k
            self.q[eng].append(lambda e, fn=fn, s=inc.h, k=k: fn(e).then_inc(s, k))
        else:
            self.q[eng].append(fn)

    def wait(self, eng, sem, val=None):
        v = sem.n if val is None else val
        if v > 0:
            self.q[eng].append(lambda e, s=sem.h, v=v: e.wait_ge(s, v))

    def dma(self, queue, out, in_, inc=None):
        s = inc if inc is not None else (self.sD if queue == "sync" else self.sW)
        self.op(queue, lambda e, o=out, i=in_: e.dma_start(out=o, in_=i), inc=s, k=16)

    def finish(self):
        block = self.stack.enter_context(self.nc.Block())
        for e in ENGS:
            fns = list(self.q[e])
            if fns:
                getattr(block, e)(lambda eng, fns=fns: [f(eng) for f in fns])
        self.stack.close()
        return self.nc


class PS:
    __slots__ = ("t", "ready", "free")

    def __init__(self, t):
        self.t = t
        self.ready = None
        self.free = None


class St:

    def __init__(self, p: Prog, with_x=True):
        self.p = p
        self.X = [p.sb(f"X{i}", [128, T], F32) for i in range(NDT)] if with_x else None
        self.XN = [p.sb(f"XN{i}", [128, T], BF16) for i in range(NDT)]
        self.scr = p.sb("scr", [128, 512], F32)
        self.scr_bf = p.sb("scr_bf", [128, 512], BF16)
        self.rows = p.sb("rows", [1, 4 * 512], F32)
        self.ones_row = p.sb("ones_row", [1, 128], F32)
        self.ones_col = p.sb("ones_col", [128, 2], BF16)
        self.ind2 = p.sb("ind2", [2, 128], BF16)
        self.eps = p.sb("eps", [1, 8], F32)
        self.psA = PS(p.ps("psA", [128, 512], F32))
        self.psB = PS(p.ps("psB", [128, 512], F32))
        self.psC = PS(p.ps("psC", [128, 512], F32))
        self.psD = PS(p.ps("psD", [128, 512], F32))
        self.psE = PS(p.ps("psE", [128, 512], F32))
        self.psF = PS(p.ps("psF", [128, 512], F32))
        self.psO = PS(p.ps("psO", [128, 1024], F32))

    def load_consts(self, c_onesf, c_onesbf, c_eps, c_ind2):
        p = self.p
        p.dma("sync", self.ones_row[:], c_onesf)
        p.dma("sync", self.ones_col[:], c_onesbf)
        p.dma("sync", self.eps[:], c_eps)
        p.dma("sync", self.ind2[:], c_ind2)
        p.nc.const_aps.register(F32, EPS_LN, self.eps[:, 0:1])

    def tfree(self, ps):
        if ps.free is not None:
            self.p.wait("tensor", ps.free[0], ps.free[1])
            ps.free = None


def const_inputs():
    ind2 = np.zeros((2, 128), BF)
    ind2[0, 0:64] = 1.0
    ind2[1, 64:128] = 1.0
    return {
        "c_onesf": np.ones((1, 128), np.float32),
        "c_onesbf": np.ones((128, 2), BF),
        "c_eps": np.array([[EPS_LN, 0, float(1.0 / np.sqrt(2.0)), 0, 0, 0, 0, 0]],
                          np.float32),
        "c_ind2": ind2,
    }


def declare_const_inputs(p: Prog):
    return (p.inp("c_onesf", [1, 128], F32),
            p.inp("c_onesbf", [128, 2], BF16),
            p.inp("c_eps", [1, 8], F32),
            p.inp("c_ind2", [2, 128], BF16))



def emit_preln(st: St, xn_free=None, x_hbm=None, xstr=None, row_stash=None,
               xn_tcs=None):
    p = st.p
    XN = st.XN
    if xn_free is not None:
        p.wait("vector", xn_free[0], xn_free[1])
    if x_hbm is None:
        for dt in range(NDT):
            p.op("vector", lambda v, a=XN[dt], b=st.X[dt]: v.tensor_copy(a[:], b[:]),
                 inc=p.sV)
    xcopy_done = p.sV.n
    for tc in range(NTC):
        sl = slice(tc * 512, (tc + 1) * 512)
        if x_hbm is not None:
            p.wait("sync", p.sV)
            for dt in range(NDT):
                p.dma("sync", xstr[dt][:], x_hbm[dt * 128:(dt + 1) * 128, sl])
            xs_ready = p.sD.n
            p.wait("vector", p.sD, xs_ready)
            for dt in range(NDT):
                p.op("vector", lambda v, a=XN[dt][:, sl], b=xstr[dt][:]:
                     v.tensor_copy(a, b), inc=p.sV)
            xcopy_done = p.sV.n
            p.wait("scalar", p.sD, xs_ready)
        Xsl = (lambda dt, sl=sl: st.X[dt][:, sl]) if x_hbm is None else \
              (lambda dt: xstr[dt][:])
        psA, psB, psC, psDp = st.psA, st.psB, st.psC, st.psD
        r = st.rows
        st.tfree(psA)
        p.wait("tensor", p.sV, xcopy_done)
        for dt in range(NDT):
            p.op("tensor", lambda t, o=psA.t[0:1, :], l=st.ones_col[:, 0:1],
                 r_=XN[dt][:, sl], f=(dt == 0), la=(dt == NDT - 1):
                 t.matmul(o, l, r_, start=f, stop=la),
                 inc=(p.sT if dt == NDT - 1 else None))
        psA.ready = (p.sT, p.sT.n)
        st.tfree(psB)
        for dt in range(NDT):
            if dt > 0:
                p.wait("scalar", p.sT)
            p.op("scalar", lambda s, a=st.scr_bf[:], b=Xsl(dt):
                 s.activation(out=a, in_=b, func=AF.Square), inc=p.sS)
            p.wait("tensor", p.sS)
            p.op("tensor", lambda t, o=psB.t[0:1, :], l=st.ones_col[:, 0:1],
                 r_=st.scr_bf[:], f=(dt == 0), la=(dt == NDT - 1):
                 t.matmul(o, l, r_, start=f, stop=la), inc=p.sT)
        psB.ready = (p.sT, p.sT.n)
        def rr(k):
            return r[0:1, k * 512:(k + 1) * 512]
        p.wait("vector", *psA.ready)
        p.wait("vector", *psB.ready)
        p.op("vector", lambda v, o=rr(0), i=psA.t[0:1, :]:
             v.tensor_scalar_mul(o, i, 1.0 / D))
        p.op("vector", lambda v, o=rr(3), i=psB.t[0:1, :]:
             v.tensor_scalar_mul(o, i, 1.0 / D), inc=p.sV)
        psA.free = (p.sV, p.sV.n)
        psB.free = (p.sV, p.sV.n)
        p.op("vector", lambda v, o=rr(1), a=rr(0):
             v.tensor_tensor(o, a, a, ALU.mult))
        p.op("vector", lambda v, o=rr(3), a=rr(3), b=rr(1):
             v.tensor_tensor(o, a, b, ALU.subtract), inc=p.sV)
        if row_stash is not None:
            stash_done = row_stash(tc, p.sV.n)
            p.wait("scalar", p.sT, stash_done)
        p.wait("scalar", p.sV)
        p.op("scalar", lambda s, o=rr(3), i=rr(3):
             s.activation(out=o, in_=i, func=AF.Ln, bias=st.eps[0:1, 0:1]))
        p.op("scalar", lambda s, o=rr(1), i=rr(3):
             s.activation(out=o, in_=i, func=AF.Exp, scale=-0.5), inc=p.sS)
        p.wait("vector", p.sS)
        p.op("vector", lambda v, o=rr(2), a=rr(0), b=rr(1):
             v.scalar_tensor_tensor(out=o, in0=a, scalar=-1.0, in1=b,
                                    op0=ALU.mult, op1=ALU.mult), inc=p.sV)
        rows_ready = p.sV.n
        for ps, row in ((psC, rr(1)), (psDp, rr(2))):
            st.tfree(ps)
            p.wait("tensor", p.sV, rows_ready)
            p.op("tensor", lambda t, o=ps.t[:, :], l=st.ones_row[:], r_=row:
                 t.matmul(o, l, r_, start=True, stop=True), inc=p.sT)
            ps.ready = (p.sT, p.sT.n)
        p.wait("vector", *psC.ready)
        p.wait("vector", *psDp.ready)
        for dt in range(NDT):
            p.op("vector", lambda v, x=Xsl(dt), bc=psC.t[:, :]:
                 v.tensor_tensor(st.scr[:], x, bc, ALU.mult))
            p.op("vector", lambda v, o=XN[dt][:, sl], bc=psDp.t[:, :]:
                 v.tensor_tensor(o, st.scr[:], bc, ALU.add), inc=p.sV)
        psC.free = (p.sV, p.sV.n)
        psDp.free = (p.sV, p.sV.n)
        if xn_tcs is not None:
            xn_tcs.append(p.sV.n)
    return p.sV.n



def emit_ffn0(st: St, wgT, wuT, wdTt, xn_ready):
    xn_tc = xn_ready if isinstance(xn_ready, list) else [xn_ready] * 4
    """wgT, wuT: HBM [1024, 3072] int32 (host-transposed). wdTt: [24, 8, 128, 128] int32."""
    p = st.p
    NH = 24
    wg_sb = [p.sb(f"wg{i}", [128, 1024], BF16) for i in range(NDT)]
    wu_sb = [p.sb(f"wu{i}", [128, 1024], BF16) for i in range(NDT)]
    hid = [p.sb(f"hid{i}", [128, 1024], BF16) for i in range(NH)]
    wd_bb = [p.sb("wd", [128, NH * 128], BF16), p.sb("wd2", [128, NH * 128], BF16)]
    silu2 = [p.sb("silu_scr", [128, 512], F32), p.sb("silu_scr2", [128, 512], F32)]
    silu_stt = [0, 0]
    s = 1.0 / 127.0

    units = [(th_, g_, h_) for th_ in range(2) for g_ in range(3) for h_ in range(2)]
    w_rdy = {}
    half_free = [0, 0]

    def stream_unit(ui):
        th_, g_, h_ = units[ui]
        p.wait("gpsimd", p.sT, half_free[h_])
        csl = slice(h_ * 512, (h_ + 1) * 512)
        gsl_ = slice(g_ * 1024 + h_ * 512, g_ * 1024 + (h_ + 1) * 512)
        for dt in range(NDT):
            p.dma("gpsimd", wg_sb[dt][:, csl], wgT[dt * 128:(dt + 1) * 128, gsl_])
            p.dma("gpsimd", wu_sb[dt][:, csl], wuT[dt * 128:(dt + 1) * 128, gsl_])
        w_rdy[ui] = p.sW.n

    stream_unit(0)
    stream_unit(1)
    for th in range(2):
        for grp in range(3):
            ubase = (th * 3 + grp) * 2
            for hi in range(8):
                ht = grp * 8 + hi
                for tc2 in range(2):
                    sl = slice(th * 1024 + tc2 * 512, th * 1024 + (tc2 + 1) * 512)
                    sl2 = slice(tc2 * 512, (tc2 + 1) * 512)
                    pg = st.psA if (hi * 2 + tc2) % 2 == 0 else st.psC
                    pu = st.psB if (hi * 2 + tc2) % 2 == 0 else st.psD
                    st.tfree(pg)
                    if hi == 0 and tc2 == 0:
                        p.wait("tensor", p.sW, w_rdy[ubase])
                    if hi == 4 and tc2 == 0:
                        p.wait("tensor", p.sW, w_rdy[ubase + 1])
                    if grp == 0 and hi == 0:
                        p.wait("tensor", p.sV, xn_tc[th * 2 + tc2])
                    for dt in range(NDT):
                        p.op("tensor", lambda t, o=pg.t[:, :],
                             l=wg_sb[dt][:, hi * 128:(hi + 1) * 128],
                             r_=st.XN[dt][:, sl], f=(dt == 0), la=(dt == NDT - 1):
                             t.matmul(o, l, r_, start=f, stop=la),
                             inc=(p.sT if dt == NDT - 1 else None))
                    pg.ready = (p.sT, p.sT.n)
                    st.tfree(pu)
                    for dt in range(NDT):
                        p.op("tensor", lambda t, o=pu.t[:, :],
                             l=wu_sb[dt][:, hi * 128:(hi + 1) * 128],
                             r_=st.XN[dt][:, sl], f=(dt == 0), la=(dt == NDT - 1):
                             t.matmul(o, l, r_, start=f, stop=la),
                             inc=(p.sT if dt == NDT - 1 else None))
                    pu.ready = (p.sT, p.sT.n)
                    if tc2 == 1 and hi in (3, 7):
                        hh_ = 0 if hi == 3 else 1
                        half_free[hh_] = p.sT.n
                        if ubase + 2 + hh_ < len(units):
                            stream_unit(ubase + 2 + hh_)
                    sil_ = silu2[(hi * 2 + tc2) % 2]
                    p.wait("scalar", *pg.ready)
                    if silu_stt[(hi * 2 + tc2) % 2]:
                        p.wait("scalar", p.sV, silu_stt[(hi * 2 + tc2) % 2])
                    p.op("scalar", lambda sc, o=sil_[:], i=pg.t[:, :]:
                         sc.activation(out=o, in_=i, func=AF.Silu, scale=s), inc=p.sS)
                    pg.free = (p.sS, p.sS.n)
                    p.wait("vector", p.sS)
                    p.wait("vector", *pu.ready)
                    p.op("vector", lambda v, o=hid[ht][:, sl2], i=pu.t[:, :], sl_=sil_:
                         v.scalar_tensor_tensor(out=o, in0=i, scalar=s, in1=sl_[:],
                                                op0=ALU.mult, op1=ALU.mult), inc=p.sV)
                    pu.free = (p.sV, p.sV.n)
                    silu_stt[(hi * 2 + tc2) % 2] = p.sV.n
        hid_ready = p.sV.n
        wd_rdy = [0, 0]
        wd_free = [p.sT.n, p.sT.n]

        def wfetch(dt_):
            p.wait("gpsimd", p.sT, wd_free[dt_ % 2])
            p.dma("gpsimd", wd_bb[dt_ % 2][:].rearrange("p (h c) -> p h c", h=NH),
                  wdTt[:, dt_].rearrange("h p c -> p h c"))
            wd_rdy[dt_ % 2] = p.sW.n

        wfetch(0)
        wfetch(1)
        for dt in range(NDT):
            wd_sb = wd_bb[dt % 2]
            wd_ready = wd_rdy[dt % 2]
            for tc2 in range(2):
                sl = slice(th * 1024 + tc2 * 512, th * 1024 + (tc2 + 1) * 512)
                sl2 = slice(tc2 * 512, (tc2 + 1) * 512)
                pe = st.psE if tc2 % 2 == 0 else st.psF
                st.tfree(pe)
                p.wait("tensor", p.sW, wd_ready)
                if dt == 0 and tc2 == 0:
                    p.wait("tensor", p.sV, hid_ready)
                for ht in range(NH):
                    p.op("tensor", lambda t, o=pe.t[:, :],
                         l=wd_sb[:, ht * 128:(ht + 1) * 128], r_=hid[ht][:, sl2],
                         f=(ht == 0), la=(ht == NH - 1):
                         t.matmul(o, l, r_, start=f, stop=la),
                         inc=(p.sT if ht == NH - 1 else None))
                pe.ready = (p.sT, p.sT.n)
                p.wait("vector", *pe.ready)
                p.op("vector", lambda v, x=st.X[dt][:, sl], i=pe.t[:, :]:
                     v.scalar_tensor_tensor(out=x, in0=i, scalar=s, in1=x,
                                            op0=ALU.mult, op1=ALU.add), inc=p.sV)
                pe.free = (p.sV, p.sV.n)
            wd_free[dt % 2] = p.sT.n
            if dt + 2 < NDT:
                wfetch(dt + 2)
    return p.sV.n



def launch_prologue(p: Prog, stt: St, xT):
    qs_ = ("sync", "scalar", "gpsimd")
    for dt in range(NDT):
        p.dma(qs_[dt % 3], stt.X[dt][:], xT[dt * 128:(dt + 1) * 128, :],
              inc=p.sD)
    ld = p.sD.n
    for e in ("vector", "scalar", "tensor", "gpsimd"):
        p.wait(e, p.sD, ld)


def launch_epilogue(p: Prog, stt: St, yT, done):
    sOut = p.sem("sOut")
    qs_ = ("sync", "scalar", "gpsimd")
    for q in qs_:
        p.wait(q, done[0], done[1])
    for dt in range(NDT):
        p.dma(qs_[dt % 3], yT[dt * 128:(dt + 1) * 128, :], stt.X[dt][:], inc=sOut)
    p.wait("sync", sOut)
    return (sOut, sOut.n)


def build_ffn0_launch():
    p = Prog()
    xT = p.inp("xT", [D, T], F32)
    wgT = p.inp("wgT", [D, 3072], BF16)
    wuT = p.inp("wuT", [D, 3072], BF16)
    wdTt = p.inp("wdTt", [24, 8, 128, 128], BF16)
    consts = declare_const_inputs(p)
    yT = p.outp("yT", [D, T], F32)
    stt = St(p)
    stt.load_consts(*consts)
    launch_prologue(p, stt, xT)
    xn_tcs = []
    emit_preln(stt, xn_tcs=xn_tcs)
    emit_ffn0(stt, wgT, wuT, wdTt, xn_tcs)
    launch_epilogue(p, stt, yT, (p.sV, p.sV.n))
    return p.finish()


def prep_ffn0_weights(w_gate, w_up, w_down):
    return {
        "wgT": np.ascontiguousarray(w_gate.T).astype(BF),
        "wuT": np.ascontiguousarray(w_up.T).astype(BF),
        "wdTt": np.ascontiguousarray(
            w_down.T.reshape(24, 128, 8, 128).transpose(0, 2, 1, 3)).astype(BF),
    }



def emit_attn_a1(st: St, wqkP, wv, wout, qsb, ksb, mask4, ones_bf, xn_ready,
                 resid_stream):
    xn_tc = xn_ready if isinstance(xn_ready, list) else [xn_ready] * 4
    import os
    PH = int(os.environ.get("A1PHASE", "5"))
    NHP = int(os.environ.get("A1HP", "8"))
    NEC = int(os.environ.get("A1EC", "2"))
    NTCL = int(os.environ.get("A1TC", "4"))
    p = st.p
    qn = p.sb("a1qn", [128, T], BF16)
    kn = p.sb("a1kn", [128, T], BF16)
    vx = [p.sb(f"a1v{i}", [128, H * 65], BF16) for i in range(16)]
    et = [p.sb(f"a1e{i}", [128, 512], BF16) for i in range(16)]
    oT = st.XN
    wbuf = [p.sb(f"a1w{i}", [128, 1024], BF16) for i in range(NDT)]
    qsv = p.sb("a1qs", [128, 2], F32)
    ksv = p.sb("a1ks", [128, 2], F32)
    msk = p.sb("a1msk", [128, 4 * 512], BF16)
    sqb = p.sb("a1sq", [128, 512], BF16)
    prows = p.sb("a1pr", [2, 3 * 512], F32)
    prb = p.sb("a1prb", [2, 4 * 512], BF16)
    prb_free = [0, 0]
    idact = [0]
    ln_pend = []
    hofA = p.sb("a1hA", [128, 2], BF16)
    onb = p.sb("a1onb", [1, 128], BF16)
    recb = p.sb("a1recb", [1, 2 * 512], BF16)
    rec_prev = [0]
    oTs = [p.sb(f"a1oT{i}", [128, T], BF16) for i in range(8)]
    xres = [p.sb(f"a1xr{i}", [128, 512], F32) for i in range(2)]
    p.dma("sync", qsv[:], qsb)
    p.dma("sync", ksv[:], ksb)
    p.dma("sync", onb[:], ones_bf)
    for m in range(4):
        p.dma("sync", msk[:, m * 512:(m + 1) * 512], mask4[m])
    consts_ld = p.sD.n
    p.op("vector", lambda v: v.memset(hofA[:, :], 0.0))
    p.op("vector", lambda v: v.memset(hofA[0:64, 0:1], 1.0))
    p.op("vector", lambda v: v.memset(hofA[64:128, 1:2], 1.0), inc=p.sV)
    hof_ready = p.sV.n

    for dt in range(NDT):
        p.dma("gpsimd", wbuf[dt][:], wv[dt * 128:(dt + 1) * 128, :])
    wv_ready = p.sW.n
    for tt in range(16):
        p.op("vector", lambda v, o=vx[tt]: v.memset(
            o[:].rearrange("p (h j) -> p h j", h=H)[:, :, 64:65], 1.0),
            inc=(p.sV if tt == 15 else None))
    vones_ready = p.sV.n
    for tt in range(16):
        for ec in range(2):
            pp = st.psA if (tt * 2 + ec) % 2 == 0 else st.psB
            st.tfree(pp)
            p.wait("tensor", p.sW, wv_ready)
            if ec == 0:
                p.wait("tensor", p.sV, xn_tc[tt // 4])
            for dt in range(NDT):
                p.op("tensor", lambda t, o=pp.t[:, :],
                     l=st.XN[dt][:, tt * 128:(tt + 1) * 128],
                     r_=wbuf[dt][:, ec * 512:(ec + 1) * 512],
                     f=(dt == 0), la=(dt == NDT - 1):
                     t.matmul(o, l, r_, start=f, stop=la),
                     inc=(p.sT if dt == NDT - 1 else None))
            pp.ready = (p.sT, p.sT.n)
            p.wait("vector", *pp.ready)
            if tt == 0 and ec == 0:
                p.wait("vector", p.sV, vones_ready)
            p.op("vector", lambda v,
                 o=vx[tt][:].rearrange("p (h j) -> p h j", h=H)[:, ec * 8:(ec + 1) * 8, 0:64],
                 i=pp.t[:, :].rearrange("p (h j) -> p h j", j=64):
                 v.tensor_copy(o, i), inc=p.sV)
            pp.free = (p.sV, p.sV.n)
    v_done = p.sV.n

    for hp in range(NHP if PH >= 2 else 0):
        for ecl, dst, svec in ((0, qn, qsv), (1, kn, ksv))[:NEC]:
            ec = hp * 2 + ecl
            p.wait("gpsimd", p.sT)
            for dt in range(NDT):
                p.dma("gpsimd", wbuf[dt][:, 0:128],
                      wqkP[dt * 128:(dt + 1) * 128, ec * 128:(ec + 1) * 128])
            w_ready = p.sW.n
            for tc in range(NTCL):
                sl = slice(tc * 512, (tc + 1) * 512)
                pp = st.psA if tc % 2 == 0 else st.psB
                st.tfree(pp)
                p.wait("tensor", p.sW, w_ready)
                if hp == 0 and ecl == 0 and tc == 0:
                    p.wait("tensor", p.sV, hof_ready)
                for dt in range(NDT):
                    p.op("tensor", lambda t, o=pp.t[:, :], l=wbuf[dt][:, 0:128],
                         r_=st.XN[dt][:, sl], f=(dt == 0), la=(dt == NDT - 1):
                         t.matmul(o, l, r_, start=f, stop=la),
                         inc=(p.sT if dt == NDT - 1 else None))
                pp.ready = (p.sT, p.sT.n)
                p.wait("vector", *pp.ready)
                p.op("vector", lambda v, o=dst[:, sl], i=pp.t[:, :]:
                     v.tensor_copy(o, i), inc=p.sV)
                raw_done = p.sV.n
                p.wait("scalar", p.sV, raw_done)
                p.wait("scalar", p.sT)
                p.op("scalar", lambda s, o=sqb[:], i=pp.t[:, :]:
                     s.activation(out=o, in_=i, func=AF.Square), inc=p.sS)
                pp.free = (p.sS, p.sS.n)
                pc, pd = st.psC, st.psD
                st.tfree(pc)
                p.wait("tensor", p.sV, raw_done)
                p.op("tensor", lambda t, o=pc.t[0:2, :], l=hofA[:, :], r_=dst[:, sl]:
                     t.matmul(o, l, r_, start=True, stop=True), inc=p.sT)
                pc.ready = (p.sT, p.sT.n)
                st.tfree(pd)
                p.wait("tensor", p.sS)
                p.op("tensor", lambda t, o=pd.t[0:2, :], l=hofA[:, :], r_=sqb[:]:
                     t.matmul(o, l, r_, start=True, stop=True), inc=p.sT)
                pd.ready = (p.sT, p.sT.n)
                mu = prows[:, 0:512]
                v2 = prows[:, 512:1024]
                Bv = prows[:, 1024:1536]
                p.wait("vector", *pc.ready)
                p.wait("vector", *pd.ready)
                p.op("vector", lambda v, o=mu, i=pc.t[0:2, :]:
                     v.tensor_scalar_mul(o, i, 1.0 / 64.0))
                p.op("vector", lambda v, o=v2, i=pd.t[0:2, :]:
                     v.tensor_scalar_mul(o, i, 1.0 / 64.0), inc=p.sV)
                pc.free = pd.free = (p.sV, p.sV.n)
                p.op("vector", lambda v, o=Bv, a=mu: v.tensor_tensor(o, a, a, ALU.mult))
                p.op("vector", lambda v, o=v2, a=Bv, b=v2: v.scalar_tensor_tensor(
                     out=o, in0=a, scalar=-1.0, in1=b, op0=ALU.mult, op1=ALU.add))
                p.op("vector", lambda v, o=v2, a=v2:
                     v.tensor_scalar_add(o, a, 1e-6), inc=p.sV)
                p.wait("scalar", p.sV)
                p.op("scalar", lambda s, o=v2, i=v2: s.activation(out=o, in_=i, func=AF.Ln))
                p.op("scalar", lambda s, o=v2, i=v2:
                     s.activation(out=o, in_=i, func=AF.Exp, scale=-0.5), inc=p.sS)
                par = (hp * 2 * NTCL + ecl * NTCL + tc) % 2
                pb0_ = par * 1024
                p.wait("vector", p.sT, prb_free[par])
                p.wait("vector", p.sS)
                p.op("vector", lambda v, o=Bv, a=mu, b=v2: v.scalar_tensor_tensor(
                     out=o, in0=a, scalar=-1.0, in1=b, op0=ALU.mult, op1=ALU.mult))
                p.op("vector", lambda v, o=prb[:, pb0_:pb0_ + 512], i=v2:
                     v.tensor_copy(o, i))
                p.op("vector", lambda v, o=prb[:, pb0_ + 512:pb0_ + 1024], i=Bv:
                     v.tensor_copy(o, i), inc=p.sV)
                prows_ready = p.sV.n

                def ln_tail(dst_=dst, sl_=sl, svec_=svec, par_=par,
                            prv_=prows_ready):
                    pb_ = par_ * 1024
                    pe, pf = st.psE, st.psF
                    for ps, rbase in ((pe, pb_), (pf, pb_ + 512)):
                        st.tfree(ps)
                        p.wait("tensor", p.sV, prv_)
                        p.op("tensor", lambda t, o=ps.t[:, :], l=st.ind2[:, :],
                             r_=prb[:, rbase:rbase + 512]:
                             t.matmul(o, l, r_, start=True, stop=True), inc=p.sT)
                        ps.ready = (p.sT, p.sT.n)
                    prb_free[par_] = p.sT.n
                    p.wait("vector", *pe.ready)
                    p.wait("vector", *pf.ready)
                    if idact[0]:
                        p.wait("vector", p.sS, idact[0])
                    p.op("vector", lambda v, o=st.scr[:], a=dst_[:, sl_],
                         bc=pe.t[:, :]:
                         v.tensor_tensor(o, a, bc, ALU.mult), inc=p.sV)
                    pe.free = (p.sV, p.sV.n)
                    p.op("vector", lambda v, o=st.scr[:], a=st.scr[:], bc=pf.t[:, :]:
                         v.tensor_tensor(o, a, bc, ALU.add), inc=p.sV)
                    pf.free = (p.sV, p.sV.n)
                    p.wait("scalar", p.sV)
                    p.op("scalar", lambda s, o=dst_[:, sl_], i=st.scr[:], sv=svec_:
                         s.activation(out=o, in_=i, func=AF.Identity,
                                      scale=sv[:, 0:1], bias=sv[:, 1:2]), inc=p.sS)
                    idact[0] = p.sS.n

                ln_pend.append(ln_tail)
                if len(ln_pend) >= 2:
                    ln_pend.pop(0)()
        while ln_pend:
            ln_pend.pop(0)()
        qk_done = p.sS.n
        if hp == NHP - 1:
            wout_can_load = p.sT.n
        if hp == 0:
            p.wait("scalar", p.sD, consts_ld)
            p.wait("vector", p.sD, consts_ld)
        for qc in range(NTC if PH >= 3 else 0):
            qsl = slice(qc * 512, (qc + 1) * 512)
            nkt = 4 * qc + 4
            for hh, po in ((0, st.psE), (1, st.psF)):
                hs = slice(hh * 64, (hh + 1) * 64)
                tpos = (64 * hh, 0)
                for kt in range(nkt):
                    pp = [st.psA, st.psB, st.psC, st.psD][kt % 4]
                    st.tfree(pp)
                    if kt == 0 and hh == 0 and qc == 0:
                        p.wait("tensor", p.sS, qk_done)
                    p.op("tensor", lambda t, o=pp.t[:, :], l=kn[hs, kt * 128:(kt + 1) * 128],
                         r_=qn[hs, qsl], tp=tpos:
                         t.matmul(o, l, r_, start=True, stop=True, tile_position=tp),
                         inc=p.sT)
                    pp.ready = (p.sT, p.sT.n)
                    p.wait("scalar", *pp.ready)
                    p.op("scalar", lambda s, o=et[kt][:], i=pp.t[:, :]:
                         s.activation(out=o, in_=i, func=AF.Exp, scale=1.0 / 64.0),
                         inc=p.sS)
                    pp.free = (p.sS, p.sS.n)
                p.wait("vector", p.sS)
                for kt in range(4 * qc, nkt):
                    mi = (kt * 128 - qc * 512) // 128
                    p.op("vector", lambda v, o=et[kt][:], m=msk[:, mi * 512:(mi + 1) * 512]:
                         v.tensor_tensor(o, o, m, ALU.mult),
                         inc=(p.sV if kt == nkt - 1 else None))
                mask_done = p.sV.n
                if PH < 4:
                    continue
                h = hp * 2 + hh
                st.tfree(po)
                p.wait("tensor", p.sV, mask_done)
                if hp == 0 and qc == 0 and hh == 0:
                    p.wait("tensor", p.sV, v_done)
                for kt in range(nkt):
                    p.op("tensor", lambda t, o=po.t[0:65, :],
                         l=vx[kt][:, h * 65:(h + 1) * 65], r_=et[kt][:],
                         f=(kt == 0), la=(kt == nkt - 1):
                         t.matmul(o, l, r_, start=f, stop=la),
                         inc=(p.sT if kt == nkt - 1 else None))
                po.ready = (p.sT, p.sT.n)
            if PH < 4:
                continue
            p.wait("vector", *st.psE.ready)
            p.wait("vector", *st.psF.ready)
            if rec_prev[0]:
                p.wait("vector", p.sS, rec_prev[0])
            for hh, po in ((0, st.psE), (1, st.psF)):
                p.op("vector", lambda v, o=st.rows[0:1, hh * 512:(hh + 1) * 512],
                     i=po.t[64:65, :]: v.tensor_copy(o, i))
                p.op("vector", lambda v, o=oTs[hp][hh * 64:(hh + 1) * 64, qsl],
                     i=po.t[0:64, :]: v.tensor_copy(o, i), inc=p.sV)
            recs_done = p.sV.n
            p.wait("scalar", p.sV, recs_done)
            p.op("scalar", lambda s, o=st.rows[0:1, 0:1024], i=st.rows[0:1, 0:1024]:
                 s.activation(out=o, in_=i, func=AF.Ln))
            p.op("scalar", lambda s, o=recb[0:1, 0:1024], i=st.rows[0:1, 0:1024]:
                 s.activation(out=o, in_=i, func=AF.Exp, scale=-1.0), inc=p.sS)
            rec_prev[0] = p.sS.n
            st.psE.free = st.psF.free = (p.sV, recs_done)
            pr = st.psO
            st.tfree(pr)
            p.wait("tensor", p.sS, rec_prev[0])
            p.op("tensor", lambda t, o=pr.t[0:64, 0:512], l=onb[:, 0:64],
                 r_=recb[0:1, 0:512]: t.matmul(o, l, r_, start=True, stop=True))
            p.op("tensor", lambda t, o=pr.t[64:128, 0:512], l=onb[:, 0:64],
                 r_=recb[0:1, 512:1024]: t.matmul(o, l, r_, start=True, stop=True),
                 inc=p.sT)
            pr.ready = (p.sT, p.sT.n)
            p.wait("vector", *pr.ready)
            p.op("vector", lambda v, o=oTs[hp][:, qsl], bc=pr.t[:, 0:512]:
                 v.tensor_tensor(o, o, bc, ALU.mult), inc=p.sV)
            pr.free = (p.sV, p.sV.n)
    att_done = p.sV.n

    if PH < 5:
        sOut0 = p.sem("a1sOut0")
        p.wait("sync", p.sV)
        p.wait("sync", p.sS)
        for dt in range(NDT):
            p.dma("sync", resid_stream[1][dt * 128:(dt + 1) * 128, 0:512],
                  st.scr[:], inc=sOut0)
        p.wait("sync", sOut0)
        return
    p.wait("gpsimd", p.sT, wout_can_load)
    for dt in range(NDT):
        p.dma("gpsimd", wbuf[dt][:], wout[dt * 128:(dt + 1) * 128, :])
    wo_ready = p.sW.n
    xin, yout = resid_stream
    sOut = p.sem("a1sOut")
    n_out = 0
    for dch in range(NDT):
        for tc in range(NTC):
            sl = slice(tc * 512, (tc + 1) * 512)
            pp = st.psE if tc % 2 == 0 else st.psF
            xr = xres[tc % 2]
            p.wait("sync", p.sV)
            p.wait("sync", sOut, 16 * max(0, n_out - 1))
            n_out += 1
            p.dma("sync", xr[:], xin[dch * 128:(dch + 1) * 128, sl])
            xr_ready = p.sD.n
            st.tfree(pp)
            p.wait("tensor", p.sW, wo_ready)
            if dch == 0 and tc == 0:
                p.wait("tensor", p.sV, att_done)
            for ei in range(8):
                p.op("tensor", lambda t, o=pp.t[:, :],
                     l=wbuf[ei][:, dch * 128:(dch + 1) * 128],
                     r_=oTs[ei][:, sl], f=(ei == 0), la=(ei == 7):
                     t.matmul(o, l, r_, start=f, stop=la),
                     inc=(p.sT if ei == 7 else None))
            pp.ready = (p.sT, p.sT.n)
            p.wait("vector", *pp.ready)
            p.wait("vector", p.sD, xr_ready)
            p.op("vector", lambda v, o=xr[:], i=pp.t[:, :]:
                 v.scalar_tensor_tensor(out=o, in0=i, scalar=1.0 / 1024.0, in1=o,
                                        op0=ALU.mult, op1=ALU.add), inc=p.sV)
            pp.free = (p.sV, p.sV.n)
            p.wait("scalar", p.sV)
            p.dma("scalar", yout[dch * 128:(dch + 1) * 128, sl], xr[:], inc=sOut)
    p.wait("sync", sOut)
    return (sOut, sOut.n)



def emit_attn_b0(st: St, wq0, wq1, wk0, wk1, wv, wo, lvec, b3l, b3r, dmask5,
                 btbl, lnwP, lnbP, ones_bf, xn_ready, resid_stream):
    xn_tc = xn_ready if isinstance(xn_ready, list) else [xn_ready] * 4
    p = st.p
    qs = [p.sb(f"b0q{i}", [128, T], BF16) for i in range(2)]
    ks = [p.sb(f"b0k{i}", [128, T], BF16) for i in range(2)]
    vx = [p.sb(f"b0v{i}", [128, H * 65], BF16) for i in range(16)]
    oTs = [p.sb(f"b0oT{i}", [128, T], BF16) for i in range(8)]
    wbuf = [p.sb(f"b0w{i}", [128, 2048], BF16) for i in range(NDT)]
    et = [wbuf[i // 4][:, (i % 4) * 512:(i % 4 + 1) * 512] for i in range(32)]
    dmk = p.sb("b0dmk", [128, 4 * 512], F32)
    btb = p.sb("b0btb", [128, 320], F32)
    lnw = p.sb("b0lnw", [128, 8], F32)
    lnb = p.sb("b0lnb", [128, 8], F32)
    lvs = p.sb("b0lv", [64, 4], F32)
    onb = p.sb("b0onb", [1, 128], BF16)
    upair = p.sb("b0up", [128, 512], BF16)
    o1b = p.sb("b0o1b", [64, 512], BF16)
    rowsb = p.sb("b0rb", [1, 512], BF16)
    prows = p.sb("b0pr", [2, 3 * 512], F32)
    prb = p.sb("b0prb", [2, 2 * 512], BF16)
    sqb = p.sb("b0sq", [128, 512], BF16)
    hofA = p.sb("b0hA", [128, 2], BF16)
    lamv = p.sb("b0lam", [1, 2], F32)
    xres = [p.sb(f"b0xr{i}", [128, 512], F32) for i in range(2)]
    scrS = [st.scr, p.sb("b0scr2", [128, 512], F32)]
    scrU = p.sb("b0scr3", [128, 512], F32)
    zrow = st.rows
    exp_done = [0, 0]
    b3ls = p.sb("b0b3l", [67, 16 * 128], BF16)
    b3rs = p.sb("b0b3r", [67, 512], BF16)
    p.dma("sync", b3ls[0:3, :], b3l)
    p.dma("sync", b3ls[64:67, :], b3l)
    p.dma("sync", b3rs[0:3, :], b3r)
    p.dma("sync", b3rs[64:67, :], b3r)
    p.dma("sync", dmk[:].rearrange("p (m n) -> p m n", m=4),
          dmask5[1:5].rearrange("m p n -> p m n"))
    p.dma("sync", btb[:], btbl)
    p.dma("sync", lnw[:], lnwP)
    p.dma("sync", lnb[:], lnbP)
    p.dma("sync", lvs[:], lvec)
    p.dma("sync", onb[:], ones_bf)
    consts_ld = p.sD.n
    p.op("vector", lambda v: v.memset(hofA[:, :], 0.0))
    p.op("vector", lambda v: v.memset(hofA[0:64, 0:1], 1.0))
    p.op("vector", lambda v: v.memset(hofA[64:128, 1:2], 1.0), inc=p.sV)
    hof_ready = p.sV.n

    pl = st.psA
    st.tfree(pl)
    p.wait("tensor", p.sD, consts_ld)
    p.op("tensor", lambda t: t.matmul(pl.t[0:1, 0:1], lvs[:, 0:1], lvs[:, 1:2],
                                      start=True, stop=True))
    p.op("tensor", lambda t: t.matmul(pl.t[0:1, 1:2], lvs[:, 2:3], lvs[:, 3:4],
                                      start=True, stop=True), inc=p.sT)
    pl.ready = (p.sT, p.sT.n)
    p.wait("scalar", *pl.ready)
    p.op("scalar", lambda s: s.activation(out=lamv[0:1, :], in_=pl.t[0:1, 0:2],
                                          func=AF.Exp), inc=p.sS)
    pl.free = (p.sS, p.sS.n)
    p.wait("vector", p.sS)
    p.op("vector", lambda v: v.tensor_tensor(lamv[0:1, 0:1], lamv[0:1, 0:1],
                                             lamv[0:1, 1:2], ALU.add), inc=p.sV)
    p.wait("vector", p.sV)
    p.op("vector", lambda v: v.tensor_scalar_mul(lamv[0:1, 0:1], lamv[0:1, 0:1], -1.0),
         inc=p.sV)
    p.wait("vector", p.sV)
    p.op("vector", lambda v: v.tensor_scalar_add(lamv[0:1, 0:1], lamv[0:1, 0:1], 1.6),
         inc=p.sV)
    lam_ready = p.sV.n

    for dt in range(NDT):
        p.dma("gpsimd", wbuf[dt][:, 0:1024], wv[dt * 128:(dt + 1) * 128, :])
    wv_ready = p.sW.n
    for tt in range(16):
        p.op("vector", lambda v, o=vx[tt]: v.memset(
            o[:].rearrange("p (h j) -> p h j", h=H)[:, :, 64:65], 1.0),
            inc=(p.sV if tt == 15 else None))
    vones_ready = p.sV.n
    for tt in range(16):
        for ec in range(2):
            pp = st.psA if (tt * 2 + ec) % 2 == 0 else st.psB
            st.tfree(pp)
            p.wait("tensor", p.sW, wv_ready)
            if ec == 0:
                p.wait("tensor", p.sV, xn_tc[tt // 4])
            for dt in range(NDT):
                p.op("tensor", lambda t, o=pp.t[:, :],
                     l=st.XN[dt][:, tt * 128:(tt + 1) * 128],
                     r_=wbuf[dt][:, ec * 512:(ec + 1) * 512],
                     f=(dt == 0), la=(dt == NDT - 1):
                     t.matmul(o, l, r_, start=f, stop=la),
                     inc=(p.sT if dt == NDT - 1 else None))
            pp.ready = (p.sT, p.sT.n)
            p.wait("vector", *pp.ready)
            if tt == 0 and ec == 0:
                p.wait("vector", p.sV, vones_ready)
            p.op("vector", lambda v,
                 o=vx[tt][:].rearrange("p (h j) -> p h j", h=H)[:, ec * 8:(ec + 1) * 8, 0:64],
                 i=pp.t[:, :].rearrange("p (h j) -> p h j", j=64):
                 v.tensor_copy(o, i), inc=p.sV)
            pp.free = (p.sV, p.sV.n)
    v_done = p.sV.n

    slope = [2.0 ** (-8.0 * (h + 1) / H) for h in range(H)]
    for hp in range(8):
        for wi, (wsrc, dst, scl) in enumerate(((wq0, qs[0], 0.125), (wq1, qs[1], 0.125),
                                               (wk0, ks[0], 1.0), (wk1, ks[1], 1.0))):
            p.wait("gpsimd", p.sT)
            for dt in range(NDT):
                p.dma("gpsimd", wbuf[dt][:, 0:128],
                      wsrc[dt * 128:(dt + 1) * 128, hp * 128:(hp + 1) * 128])
            w_ready = p.sW.n
            for tc in range(NTC):
                sl = slice(tc * 512, (tc + 1) * 512)
                pp = st.psA if tc % 2 == 0 else st.psB
                st.tfree(pp)
                p.wait("tensor", p.sW, w_ready)
                for dt in range(NDT):
                    p.op("tensor", lambda t, o=pp.t[:, :], l=wbuf[dt][:, 0:128],
                         r_=st.XN[dt][:, sl], f=(dt == 0), la=(dt == NDT - 1):
                         t.matmul(o, l, r_, start=f, stop=la),
                         inc=(p.sT if dt == NDT - 1 else None))
                pp.ready = (p.sT, p.sT.n)
                p.wait("vector", *pp.ready)
                p.op("vector", lambda v, o=dst[:, sl], i=pp.t[:, :], s_=scl:
                     v.tensor_scalar_mul(o, i, s_), inc=p.sV)
                pp.free = (p.sV, p.sV.n)
        proj_done = p.sV.n
        for qc in range(NTC):
            qsl = slice(qc * 512, (qc + 1) * 512)
            nkt = 4 * qc + 4
            def _hdr(hh):
                hs = slice(hh * 64, (hh + 1) * 64)
                tpos = (64 * hh, 0)
                h = hp * 2 + hh
                return hs, tpos, h

            def S_blk(hh):
                hs, tpos, h = _hdr(hh)
                for s_i in range(2):
                    for kt in range(nkt):
                        if (qc * 512 - (kt * 128 + 127)) * slope[h] > 60.0:
                            continue
                        pp = [st.psA, st.psB, st.psC, st.psD][(s_i * nkt + kt) % 4]
                        st.tfree(pp)
                        if hp == 0 and qc == 0 and hh == 0 and s_i == 0 and kt == 0:
                            p.wait("tensor", p.sV, proj_done)
                            p.wait("tensor", p.sD, consts_ld)
                        mi = 0 if kt < 4 * qc else 1 + (kt - 4 * qc)
                        bcol = h * 20 + (kt - 4 * qc + 16)
                        if mi == 0:
                            p.op("tensor", lambda t, o=pp.t[:, :],
                                 l=ks[s_i][hs, kt * 128:(kt + 1) * 128],
                                 r_=qs[s_i][hs, qsl], tp=tpos:
                                 t.matmul(o, l, r_, start=True, stop=False,
                                          tile_position=tp))
                            p.op("tensor", lambda t, o=pp.t[:, :],
                                 l=b3ls[64 * hh:64 * hh + 3, h * 128:(h + 1) * 128],
                                 r_=b3rs[64 * hh:64 * hh + 3, :], tp=tpos:
                                 t.matmul(o, l, r_, start=False, stop=True,
                                          tile_position=tp), inc=p.sT)
                            pp.ready = (p.sT, p.sT.n)
                            p.wait("scalar", *pp.ready)
                            p.op("scalar", lambda s, o=et[s_i * 16 + kt][:],
                                 i=pp.t[:, :], b=btb[:, bcol:bcol + 1]:
                                 s.activation(out=o, in_=i, func=AF.Exp, bias=b),
                                 inc=p.sS)
                            pp.free = (p.sS, p.sS.n)
                            continue
                        p.op("tensor", lambda t, o=pp.t[:, :],
                             l=ks[s_i][hs, kt * 128:(kt + 1) * 128],
                             r_=qs[s_i][hs, qsl], tp=tpos:
                             t.matmul(o, l, r_, start=True, stop=True, tile_position=tp),
                             inc=p.sT)
                        pp.ready = (p.sT, p.sT.n)
                        pb_ = (s_i * nkt + kt) % 2
                        scr_ = scrS[pb_]
                        p.wait("vector", *pp.ready)
                        if exp_done[pb_]:
                            p.wait("vector", p.sS, exp_done[pb_])
                        p.op("vector", lambda v, o=scr_[:],
                             d_=dmk[:, (mi - 1) * 512:mi * 512],
                             i=pp.t[:, :], s_=slope[h]:
                             v.scalar_tensor_tensor(out=o, in0=d_, scalar=s_, in1=i,
                                                    op0=ALU.mult, op1=ALU.add), inc=p.sV)
                        pp.free = (p.sV, p.sV.n)
                        p.wait("scalar", p.sV)
                        p.op("scalar", lambda s, o=et[s_i * 16 + kt][:], i=scr_[:],
                             b=btb[:, bcol:bcol + 1]:
                             s.activation(out=o, in_=i, func=AF.Exp, bias=b), inc=p.sS)
                        exp_done[pb_] = p.sS.n

            def AV_blk(hh):
                hs, tpos, h = _hdr(hh)
                kts = [kt for kt in range(nkt)
                       if (qc * 512 - (kt * 128 + 127)) * slope[h] <= 60.0]
                for s_i, po in ((0, st.psE), (1, st.psF)):
                    st.tfree(po)
                    p.wait("tensor", p.sS)
                    if hp == 0 and qc == 0 and hh == 0 and s_i == 0:
                        p.wait("tensor", p.sV, v_done)
                    for ki, kt in enumerate(kts):
                        p.op("tensor", lambda t, o=po.t[0:65, :],
                             l=vx[kt][:, h * 65:(h + 1) * 65], r_=et[s_i * 16 + kt][:],
                             f=(ki == 0), la=(ki == len(kts) - 1):
                             t.matmul(o, l, r_, start=f, stop=la),
                             inc=(p.sT if ki == len(kts) - 1 else None))
                    po.ready = (p.sT, p.sT.n)

            def U_blk(hh):
                hs, tpos, h = _hdr(hh)
                p.wait("vector", *st.psE.ready)
                p.wait("vector", *st.psF.ready)
                if hp == 0 and qc == 0 and hh == 0:
                    p.wait("vector", p.sV, lam_ready)
                p.op("vector", lambda v, o=o1b[:], i=st.psF.t[0:64, :]:
                     v.tensor_copy(o, i))
                p.op("vector", lambda v, o=zrow[0:1, 0:512], i=st.psF.t[64:65, :]:
                     v.tensor_copy(o, i), inc=p.sV)
                p.wait("scalar", p.sV)
                p.op("scalar", lambda s, o=zrow[0:1, 0:512], i=zrow[0:1, 0:512]:
                     s.activation(out=o, in_=i, func=AF.Ln))
                p.op("scalar", lambda s, o=zrow[0:1, 0:512], i=zrow[0:1, 0:512]:
                     s.activation(out=o, in_=i, func=AF.Exp, scale=-1.0), inc=p.sS)
                p.wait("vector", p.sS)
                p.op("vector", lambda v, o=zrow[0:1, 0:512], i=st.psE.t[64:65, :]:
                     v.scalar_tensor_tensor(out=o, in0=i, scalar=lamv[0:1, 0:1],
                                            in1=zrow[0:1, 0:512], op0=ALU.mult,
                                            op1=ALU.mult))
                p.op("vector", lambda v, o=rowsb[0:1, 0:512], i=zrow[0:1, 0:512]:
                     v.tensor_copy(o, i), inc=p.sV)
                crow_done = p.sV.n
                st.psF.free = (p.sV, crow_done)
                pc = st.psC
                st.tfree(pc)
                p.wait("tensor", p.sV, crow_done)
                p.op("tensor", lambda t, o=pc.t[0:64, :], l=onb[:, 0:64],
                     r_=rowsb[0:1, 0:512]:
                     t.matmul(o, l, r_, start=True, stop=True), inc=p.sT)
                pc.ready = (p.sT, p.sT.n)
                p.wait("vector", *pc.ready)
                p.op("vector", lambda v, o=scrU[0:64, :], a=o1b[:], bc=pc.t[0:64, :]:
                     v.tensor_tensor(o, a, bc, ALU.mult), inc=p.sV)
                pc.free = (p.sV, p.sV.n)
                p.op("vector", lambda v, o=upair[hs, :], a=st.psE.t[0:64, :],
                     b=scrU[0:64, :]:
                     v.tensor_tensor(o, a, b, ALU.subtract), inc=p.sV)
                st.psE.free = (p.sV, p.sV.n)

            S_blk(0)
            AV_blk(0)
            S_blk(1)
            U_blk(0)
            AV_blk(1)
            U_blk(1)
            u_done = p.sV.n
            pc, pd = st.psC, st.psD
            st.tfree(pc)
            p.wait("tensor", p.sV, u_done)
            p.op("tensor", lambda t, o=pc.t[0:2, :], l=hofA[:, :], r_=upair[:]:
                 t.matmul(o, l, r_, start=True, stop=True), inc=p.sT)
            pc.ready = (p.sT, p.sT.n)
            p.wait("scalar", p.sV, u_done)
            p.wait("scalar", p.sT)
            p.op("scalar", lambda s, o=sqb[:], i=upair[:]:
                 s.activation(out=o, in_=i, func=AF.Square), inc=p.sS)
            st.tfree(pd)
            p.wait("tensor", p.sS)
            p.op("tensor", lambda t, o=pd.t[0:2, :], l=hofA[:, :], r_=sqb[:]:
                 t.matmul(o, l, r_, start=True, stop=True), inc=p.sT)
            pd.ready = (p.sT, p.sT.n)
            mu = prows[:, 0:512]
            v2 = prows[:, 512:1024]
            Bv = prows[:, 1024:1536]
            p.wait("vector", *pc.ready)
            p.wait("vector", *pd.ready)
            p.op("vector", lambda v, o=mu, i=pc.t[0:2, :]:
                 v.tensor_scalar_mul(o, i, 1.0 / 64.0))
            p.op("vector", lambda v, o=v2, i=pd.t[0:2, :]:
                 v.tensor_scalar_mul(o, i, 1.0 / 64.0), inc=p.sV)
            pc.free = pd.free = (p.sV, p.sV.n)
            p.op("vector", lambda v, o=Bv, a=mu: v.tensor_tensor(o, a, a, ALU.mult))
            p.op("vector", lambda v, o=v2, a=Bv, b=v2: v.scalar_tensor_tensor(
                 out=o, in0=a, scalar=-1.0, in1=b, op0=ALU.mult, op1=ALU.add))
            p.op("vector", lambda v, o=v2, a=v2:
                 v.tensor_scalar_add(o, a, 1e-5), inc=p.sV)
            p.wait("scalar", p.sV)
            p.op("scalar", lambda s, o=v2, i=v2: s.activation(out=o, in_=i, func=AF.Ln))
            p.op("scalar", lambda s, o=v2, i=v2:
                 s.activation(out=o, in_=i, func=AF.Exp, scale=-0.5), inc=p.sS)
            p.wait("vector", p.sS)
            p.op("vector", lambda v, o=Bv, a=mu, b=v2: v.scalar_tensor_tensor(
                 out=o, in0=a, scalar=-1.0, in1=b, op0=ALU.mult, op1=ALU.mult))
            p.op("vector", lambda v, o=prb[:, 0:512], i=v2: v.tensor_copy(o, i))
            p.op("vector", lambda v, o=prb[:, 512:1024], i=Bv: v.tensor_copy(o, i),
                 inc=p.sV)
            prows_ready = p.sV.n
            pe, pf = st.psE, st.psF
            for ps, rbase in ((pe, 0), (pf, 512)):
                st.tfree(ps)
                p.wait("tensor", p.sV, prows_ready)
                p.op("tensor", lambda t, o=ps.t[:, :], l=st.ind2[:, :],
                     r_=prb[:, rbase:rbase + 512]:
                     t.matmul(o, l, r_, start=True, stop=True), inc=p.sT)
                ps.ready = (p.sT, p.sT.n)
            p.wait("vector", *pe.ready)
            p.wait("vector", *pf.ready)
            p.op("vector", lambda v, o=scrU[:], a=upair[:], bc=pe.t[:, :]:
                 v.tensor_tensor(o, a, bc, ALU.mult), inc=p.sV)
            pe.free = (p.sV, p.sV.n)
            p.op("vector", lambda v, o=scrU[:], a=scrU[:], bc=pf.t[:, :]:
                 v.tensor_tensor(o, a, bc, ALU.add), inc=p.sV)
            pf.free = (p.sV, p.sV.n)
            p.wait("scalar", p.sV)
            p.op("scalar", lambda s, o=oTs[hp][:, qsl], i=scrU[:], hp_=hp:
                 s.activation(out=o, in_=i, func=AF.Identity,
                              scale=lnw[:, hp_:hp_ + 1], bias=lnb[:, hp_:hp_ + 1]),
                 inc=p.sS)
    att_done = p.sS.n

    p.wait("gpsimd", p.sT)
    for dt in range(NDT):
        p.dma("gpsimd", wbuf[dt][:, 0:1024], wo[dt * 128:(dt + 1) * 128, :])
    wo_ready = p.sW.n
    xin, yout = resid_stream
    sOut = p.sem("b0sOut")
    n_out = 0
    for dch in range(NDT):
        for tc in range(NTC):
            sl = slice(tc * 512, (tc + 1) * 512)
            pp = st.psE if tc % 2 == 0 else st.psF
            xr = xres[tc % 2]
            p.wait("sync", p.sV)
            p.wait("sync", sOut, 16 * max(0, n_out - 1))
            n_out += 1
            p.dma("sync", xr[:], xin[dch * 128:(dch + 1) * 128, sl])
            xr_ready = p.sD.n
            st.tfree(pp)
            p.wait("tensor", p.sW, wo_ready)
            if dch == 0 and tc == 0:
                p.wait("tensor", p.sS, att_done)
            for ei in range(8):
                p.op("tensor", lambda t, o=pp.t[:, :],
                     l=wbuf[ei][:, dch * 128:(dch + 1) * 128],
                     r_=oTs[ei][:, sl], f=(ei == 0), la=(ei == 7):
                     t.matmul(o, l, r_, start=f, stop=la),
                     inc=(p.sT if ei == 7 else None))
            pp.ready = (p.sT, p.sT.n)
            p.wait("vector", *pp.ready)
            p.wait("vector", p.sD, xr_ready)
            p.op("vector", lambda v, o=xr[:], i=pp.t[:, :]:
                 v.tensor_tensor(o, i, o, ALU.add), inc=p.sV)
            pp.free = (p.sV, p.sV.n)
            p.wait("scalar", p.sV)
            p.dma("scalar", yout[dch * 128:(dch + 1) * 128, sl], xr[:], inc=sOut)
    p.wait("sync", sOut)
    return (sOut, sOut.n)


def prep_attn_b0_weights(wq0, wq1, wk0, wk1, wv, wo, l0, l1, l2, l3, ln_w, ln_b):
    slope = 2.0 ** (-8.0 * np.arange(1, H + 1) / H)
    kr = np.arange(128)[:, None].astype(np.float32)
    qf = np.arange(512)[None, :].astype(np.float32)
    dmask5 = np.zeros((5, 128, 512), np.float32)
    dmask5[0] = kr - qf
    for i, off in enumerate((0, 128, 256, 384)):
        m = (kr + off) <= qf
        dmask5[1 + i] = np.where(m, kr - qf, -1e33)
    slope = 2.0 ** (-8.0 * np.arange(1, H + 1) / H)
    btbl = np.zeros((128, 320), np.float32)
    for h in range(H):
        for j in range(20):
            btbl[:, h * 20 + j] = slope[h] * (j - 16) * 128.0
    lnwP = np.zeros((128, 8), np.float32)
    lnbP = np.zeros((128, 8), np.float32)
    for hp in range(8):
        lnwP[:, hp] = np.concatenate([ln_w[2 * hp], ln_w[2 * hp + 1]])
        lnbP[:, hp] = np.concatenate([ln_b[2 * hp], ln_b[2 * hp + 1]])
    b3l = np.zeros((3, 16 * 128), np.float32)
    b3r = np.zeros((3, 512), np.float32)
    nn = np.arange(512, dtype=np.float32)
    b3r[0] = 1.0
    b3r[1] = -np.floor(nn / 16.0)
    b3r[2] = -(nn % 16.0)
    for h in range(H):
        b3l[0, h * 128:(h + 1) * 128] = slope[h] * np.arange(128)
        b3l[1, h * 128:(h + 1) * 128] = slope[h] * 16.0
        b3l[2, h * 128:(h + 1) * 128] = slope[h]
    return {
        "b3l": b3l.astype(BF), "b3r": b3r.astype(BF),
        "wq0": np.ascontiguousarray(wq0).astype(BF),
        "wq1": np.ascontiguousarray(wq1).astype(BF),
        "wk0": np.ascontiguousarray(wk0).astype(BF),
        "wk1": np.ascontiguousarray(wk1).astype(BF),
        "wv": np.ascontiguousarray(wv).astype(BF),
        "wo": np.ascontiguousarray(wo).astype(BF),
        "lvec": np.stack([l0, l1, l2, l3], axis=1).astype(np.float32),
        "dmask5": dmask5, "btbl": btbl, "lnwP": lnwP, "lnbP": lnbP,
        "ones_bf": np.ones((1, 128), BF),
    }


def build_attn_b0_launch():
    p = Prog()
    xT = p.inp("xT", [D, T], F32)
    ws = {n: p.inp(n, [D, D], BF16) for n in ("wq0", "wq1", "wk0", "wk1", "wv", "wo")}
    lvec = p.inp("lvec", [64, 4], F32)
    b3l = p.inp("b3l", [3, 16 * 128], BF16)
    b3r = p.inp("b3r", [3, 512], BF16)
    dmask5 = p.inp("dmask5", [5, 128, 512], F32)
    btbl = p.inp("btbl", [128, 320], F32)
    lnwP = p.inp("lnwP", [128, 8], F32)
    lnbP = p.inp("lnbP", [128, 8], F32)
    ones_bf = p.inp("ones_bf", [1, 128], BF16)
    consts = declare_const_inputs(p)
    yT = p.outp("yT", [D, T], F32)
    stt = St(p, with_x=False)
    stt.load_consts(*consts)
    xstr = [p.sb(f"xstr{i}", [128, 512], F32) for i in range(NDT)]
    xn_tcs = []
    emit_preln(stt, x_hbm=xT, xstr=xstr, xn_tcs=xn_tcs)
    emit_attn_b0(stt, ws["wq0"], ws["wq1"], ws["wk0"], ws["wk1"], ws["wv"], ws["wo"],
                 lvec, b3l, b3r, dmask5, btbl, lnwP, lnbP, ones_bf, xn_tcs,
                 resid_stream=(xT, yT))
    return p.finish()



def emit_attn_b1(st: St, wqP, wkP, wv, woT, mask4, ones_bf, xn_ready, resid_stream):
    xn_tc = xn_ready if isinstance(xn_ready, list) else [xn_ready] * 4
    """wqP/wkP: [1024, 512] f32 (cols = (h,32) zero-padded). wv: [1024,1024] f32 natural.
    woT: [1024, 1024] f32 = Wo^T (contract-e lhsT)."""
    p = st.p
    R2I = float(1.0 / np.sqrt(2.0))
    qT = [p.sb(f"b1q{i}", [128, T], BF16) for i in range(4)]
    kT = [p.sb(f"b1k{i}", [128, T], BF16) for i in range(4)]
    vx = [p.sb(f"b1v{i}", [128, H * 65], BF16) for i in range(16)]
    oTs = [p.sb(f"b1oT{i}", [128, T], BF16) for i in range(8)]
    wbuf = [p.sb(f"b1w{i}", [128, 2048], BF16) for i in range(NDT)]
    et = [wbuf[i // 4][:, (i % 4) * 512:(i % 4 + 1) * 512] for i in range(16)]
    msk = p.sb("b1msk", [128, 4 * 512], BF16)
    onb = p.sb("b1onb", [1, 128], BF16)
    recb = p.sb("b1recb", [1, 2 * 512], BF16)
    rec_prev = [0]
    r2col = p.sb("b1r2", [128, 1], F32)
    xres = [p.sb(f"b1xr{i}", [128, 512], F32) for i in range(2)]
    scrS1 = [st.scr, p.sb("b1scr2", [128, 512], F32)]
    et_dve = [0, 0]
    p.op("vector", lambda v: v.memset(r2col[:, :], R2I))
    p.dma("sync", onb[:], ones_bf)
    for m in range(4):
        p.dma("sync", msk[:, m * 512:(m + 1) * 512], mask4[m])
    consts_ld = p.sD.n

    for wsrc, dstl in ((wqP, qT), (wkP, kT)):
        p.wait("gpsimd", p.sT)
        for dt in range(NDT):
            p.dma("gpsimd", wbuf[dt][:, 0:512], wsrc[dt * 128:(dt + 1) * 128, :])
        w_ready = p.sW.n
        for ec in range(4):
            for tc in range(NTC):
                sl = slice(tc * 512, (tc + 1) * 512)
                pp = st.psA if (ec * NTC + tc) % 2 == 0 else st.psB
                st.tfree(pp)
                p.wait("tensor", p.sW, w_ready)
                if ec == 0:
                    p.wait("tensor", p.sV, xn_tc[tc])
                for dt in range(NDT):
                    p.op("tensor", lambda t, o=pp.t[:, :], l=wbuf[dt][:, ec * 128:(ec + 1) * 128],
                         r_=st.XN[dt][:, sl], f=(dt == 0), la=(dt == NDT - 1):
                         t.matmul(o, l, r_, start=f, stop=la),
                         inc=(p.sT if dt == NDT - 1 else None))
                pp.ready = (p.sT, p.sT.n)
                p.wait("vector", *pp.ready)
                p.op("vector", lambda v, o=dstl[ec][:, sl], i=pp.t[:, :]:
                     v.tensor_copy(o, i), inc=p.sV)
                pp.free = (p.sV, p.sV.n)
    qk_done = p.sV.n

    p.wait("gpsimd", p.sT)
    for dt in range(NDT):
        p.dma("gpsimd", wbuf[dt][:, 0:1024], wv[dt * 128:(dt + 1) * 128, :])
    wv_ready = p.sW.n
    for tt in range(16):
        p.op("vector", lambda v, o=vx[tt]: v.memset(
            o[:].rearrange("p (h j) -> p h j", h=H)[:, :, 64:65], 1.0),
            inc=(p.sV if tt == 15 else None))
    vones_ready = p.sV.n
    for tt in range(16):
        for ec in range(2):
            pp = st.psA if (tt * 2 + ec) % 2 == 0 else st.psB
            st.tfree(pp)
            p.wait("tensor", p.sW, wv_ready)
            for dt in range(NDT):
                p.op("tensor", lambda t, o=pp.t[:, :],
                     l=st.XN[dt][:, tt * 128:(tt + 1) * 128],
                     r_=wbuf[dt][:, ec * 512:(ec + 1) * 512],
                     f=(dt == 0), la=(dt == NDT - 1):
                     t.matmul(o, l, r_, start=f, stop=la),
                     inc=(p.sT if dt == NDT - 1 else None))
            pp.ready = (p.sT, p.sT.n)
            p.wait("vector", *pp.ready)
            if tt == 0 and ec == 0:
                p.wait("vector", p.sV, vones_ready)
            p.op("vector", lambda v,
                 o=vx[tt][:].rearrange("p (h j) -> p h j", h=H)[:, ec * 8:(ec + 1) * 8, 0:64],
                 i=pp.t[:, :].rearrange("p (h j) -> p h j", j=64):
                 v.tensor_copy(o, i), inc=p.sV)
            pp.free = (p.sV, p.sV.n)
    v_done = p.sV.n

    for qc in range(NTC):
        qsl = slice(qc * 512, (qc + 1) * 512)
        nkt = 4 * qc + 4
        for hp in range(8):
            for hh, po in ((0, st.psE), (1, st.psF)):
                h = hp * 2 + hh
                til = h // 4
                hs = slice((h % 4) * 32, (h % 4) * 32 + 32)
                tpos = ((h % 4) * 32, 0)
                for kt in range(nkt):
                    pp = [st.psA, st.psB, st.psC, st.psD][kt % 4]
                    st.tfree(pp)
                    if qc == 0 and hp == 0 and hh == 0 and kt == 0:
                        p.wait("tensor", p.sV, qk_done)
                        p.wait("tensor", p.sD, consts_ld)
                    p.op("tensor", lambda t, o=pp.t[:, :], l=kT[til][hs, kt * 128:(kt + 1) * 128],
                         r_=qT[til][hs, qsl], tp=tpos:
                         t.matmul(o, l, r_, start=True, stop=True, tile_position=tp),
                         inc=p.sT)
                    pp.ready = (p.sT, p.sT.n)
                    scr_ = scrS1[kt % 2]
                    p.wait("scalar", *pp.ready)
                    if et_dve[kt % 2]:
                        p.wait("scalar", p.sV, et_dve[kt % 2])
                    p.op("scalar", lambda s, o=scr_[:], i=pp.t[:, :]:
                         s.activation(out=o, in_=i, func=AF.Square, scale=R2I / 4.0,
                                      bias=r2col[:, 0:1]), inc=p.sS)
                    pp.free = (p.sS, p.sS.n)
                    mi = kt - 4 * qc
                    p.wait("vector", p.sS)
                    if mi >= 0:
                        p.op("vector", lambda v, o=et[kt][:], i=scr_[:],
                             m=msk[:, mi * 512:(mi + 1) * 512]:
                             v.scalar_tensor_tensor(out=o, in0=i, scalar=0.5, in1=m,
                                                    op0=ALU.add, op1=ALU.mult), inc=p.sV)
                    else:
                        p.op("vector", lambda v, o=et[kt][:], i=scr_[:]:
                             v.tensor_scalar_add(o, i, 0.5), inc=p.sV)
                    et_dve[kt % 2] = p.sV.n
                av_done_v = p.sV.n
                st.tfree(po)
                p.wait("tensor", p.sV, av_done_v)
                if qc == 0 and hp == 0 and hh == 0:
                    p.wait("tensor", p.sV, v_done)
                for kt in range(nkt):
                    p.op("tensor", lambda t, o=po.t[0:65, :],
                         l=vx[kt][:, h * 65:(h + 1) * 65], r_=et[kt][:],
                         f=(kt == 0), la=(kt == nkt - 1):
                         t.matmul(o, l, r_, start=f, stop=la),
                         inc=(p.sT if kt == nkt - 1 else None))
                po.ready = (p.sT, p.sT.n)
            p.wait("vector", *st.psE.ready)
            p.wait("vector", *st.psF.ready)
            if rec_prev[0]:
                p.wait("vector", p.sS, rec_prev[0])
            for hh, po in ((0, st.psE), (1, st.psF)):
                p.op("vector", lambda v, o=st.rows[0:1, hh * 512:(hh + 1) * 512],
                     i=po.t[64:65, :]: v.tensor_scalar_add(o, i, 1e-12))
                p.op("vector", lambda v, o=oTs[hp][hh * 64:(hh + 1) * 64, qsl],
                     i=po.t[0:64, :]: v.tensor_copy(o, i), inc=p.sV)
            recs_done = p.sV.n
            p.wait("scalar", p.sV, recs_done)
            p.op("scalar", lambda s, o=st.rows[0:1, 0:1024], i=st.rows[0:1, 0:1024]:
                 s.activation(out=o, in_=i, func=AF.Ln))
            p.op("scalar", lambda s, o=recb[0:1, 0:1024], i=st.rows[0:1, 0:1024]:
                 s.activation(out=o, in_=i, func=AF.Exp, scale=-1.0), inc=p.sS)
            rec_prev[0] = p.sS.n
            st.psE.free = st.psF.free = (p.sV, recs_done)
            pr = st.psO
            st.tfree(pr)
            p.wait("tensor", p.sS, rec_prev[0])
            p.op("tensor", lambda t, o=pr.t[0:64, 0:512], l=onb[:, 0:64],
                 r_=recb[0:1, 0:512]: t.matmul(o, l, r_, start=True, stop=True))
            p.op("tensor", lambda t, o=pr.t[64:128, 0:512], l=onb[:, 0:64],
                 r_=recb[0:1, 512:1024]: t.matmul(o, l, r_, start=True, stop=True),
                 inc=p.sT)
            pr.ready = (p.sT, p.sT.n)
            p.wait("vector", *pr.ready)
            p.op("vector", lambda v, o=oTs[hp][:, qsl], bc=pr.t[:, 0:512]:
                 v.tensor_tensor(o, o, bc, ALU.mult), inc=p.sV)
            pr.free = (p.sV, p.sV.n)
    att_done = p.sV.n

    p.wait("gpsimd", p.sT)
    for dt in range(NDT):
        p.dma("gpsimd", wbuf[dt][:, 0:1024], woT[dt * 128:(dt + 1) * 128, :])
    wo_ready = p.sW.n
    xin, yout = resid_stream
    sOut = p.sem("b1sOut")
    n_out = 0
    for dch in range(NDT):
        for tc in range(NTC):
            sl = slice(tc * 512, (tc + 1) * 512)
            pp = st.psE if tc % 2 == 0 else st.psF
            xr = xres[tc % 2]
            p.wait("sync", p.sV)
            p.wait("sync", sOut, 16 * max(0, n_out - 1))
            n_out += 1
            p.dma("sync", xr[:], xin[dch * 128:(dch + 1) * 128, sl])
            xr_ready = p.sD.n
            st.tfree(pp)
            p.wait("tensor", p.sW, wo_ready)
            if dch == 0 and tc == 0:
                p.wait("tensor", p.sV, att_done)
            for ei in range(8):
                p.op("tensor", lambda t, o=pp.t[:, :],
                     l=wbuf[ei][:, dch * 128:(dch + 1) * 128],
                     r_=oTs[ei][:, sl], f=(ei == 0), la=(ei == 7):
                     t.matmul(o, l, r_, start=f, stop=la),
                     inc=(p.sT if ei == 7 else None))
            pp.ready = (p.sT, p.sT.n)
            p.wait("vector", *pp.ready)
            p.wait("vector", p.sD, xr_ready)
            p.op("vector", lambda v, o=xr[:], i=pp.t[:, :]:
                 v.tensor_tensor(o, i, o, ALU.add), inc=p.sV)
            pp.free = (p.sV, p.sV.n)
            p.wait("scalar", p.sV)
            p.dma("scalar", yout[dch * 128:(dch + 1) * 128, sl], xr[:], inc=sOut)
    p.wait("sync", sOut)
    return (sOut, sOut.n)


def prep_attn_b1_weights(Wq, Wk, Wv, Wo):
    wqP = np.zeros((1024, 512), np.float32)
    wkP = np.zeros((1024, 512), np.float32)
    for h in range(H):
        wqP[:, h * 32:h * 32 + 16] = Wq[h * 16:(h + 1) * 16, :].T
        wkP[:, h * 32:h * 32 + 16] = Wk[h * 16:(h + 1) * 16, :].T
    kr = np.arange(128)[:, None]
    qf = np.arange(512)[None, :]
    mask4 = np.stack([((kr + off) <= qf).astype(BF) for off in (0, 128, 256, 384)])
    return {"wqP": wqP.astype(BF), "wkP": wkP.astype(BF),
            "wv": np.ascontiguousarray(Wv.T).astype(BF),
            "woT": np.ascontiguousarray(Wo.T).astype(BF), "mask4": mask4,
            "ones_bf": np.ones((1, 128), BF)}


def build_attn_b1_launch():
    p = Prog()
    xT = p.inp("xT", [D, T], F32)
    wqP = p.inp("wqP", [D, 512], BF16)
    wkP = p.inp("wkP", [D, 512], BF16)
    wv = p.inp("wv", [D, D], BF16)
    woT = p.inp("woT", [D, D], BF16)
    mask4 = p.inp("mask4", [4, 128, 512], BF16)
    ones_bf = p.inp("ones_bf", [1, 128], BF16)
    consts = declare_const_inputs(p)
    yT = p.outp("yT", [D, T], F32)
    stt = St(p, with_x=False)
    stt.load_consts(*consts)
    xstr = [p.sb(f"xstr{i}", [128, 512], F32) for i in range(NDT)]
    xn_tcs = []
    emit_preln(stt, x_hbm=xT, xstr=xstr, xn_tcs=xn_tcs)
    emit_attn_b1(stt, wqP, wkP, wv, woT, mask4, ones_bf, xn_tcs,
                 resid_stream=(xT, yT))
    return p.finish()



def emit_ffn1(st: St, rwT, rngt, rnbt, w1, lngt, lnbt, w2t, ongt, onbt,
              mut, vart, x_src, idf, ones_bf, xn_ready, resid_stream,
              ubf=None):
    p = st.p
    wbuf = [p.sb(f"f1w{i}", [128, 2048], BF16) for i in range(NDT)]
    w2bb = [p.sb("f1wB", [128, 16 * 128], BF16), p.sb("f1wB2", [128, 16 * 128], BF16)]
    if ubf is None:
        ubf = [p.sb(f"f1u{i}", [128, 1024], BF16) for i in range(16)]
    outac = [p.sb(f"f1oa{i}", [128, 1024], F32) for i in range(NDT)]
    sqb = p.sb("f1sq", [128, 1024], BF16)
    obf = p.sb("f1obf", [128, 1024], BF16)
    sqb2 = [sqb, p.sb("f1sqB", [128, 1024], BF16)]
    obf2 = [obf, p.sb("f1obB", [128, 1024], BF16)]
    hrl = p.sb("f1hrl", [128, 512], BF16)
    hrl2 = [hrl, st.scr_bf]
    scr2 = [st.scr, p.sb("f1scr2", [128, 512], F32)]
    sq_done = [0, 0]
    hr_done = [0, 0]
    mubc = p.sb("f1mu", [128, 1024], F32)
    w1m = p.sb("f1w1m", [128, NDT], BF16)
    rws = [p.sb(f"f1rw{i}", [128, 4], F32) for i in range(NDT)]
    rwtmp = p.sb("f1rwtmp", [128, 4], F32)
    rnbF = p.sb("f1rnbF", [128, NDT], F32)
    onesc = p.sb("f1onec", [128, 1], F32)
    srow = p.sb("f1srow", [1, 4], F32)
    sbc = p.sb("f1sbc", [128, 4], F32)
    ltok = p.sb("f1lt", [128, 64], F32)
    wtok = p.sb("f1wt", [128, 64], F32)
    mrow = p.sb("f1mr", [128, 32], F32)
    b4bc = p.sb("f1b4", [128, 4], F32)
    wrow = p.sb("f1wr", [1, 4 * T], BF16)
    rowf = p.sb("f1rowf", [1, 6 * 512], F32)
    rowb2 = p.sb("f1rowb", [1, 2 * 512], BF16)
    onb = p.sb("f1onb", [1, 128], BF16)
    lng = p.sb("f1lng", [128, 64], F32)
    lnb_ = p.sb("f1lnb", [128, 64], F32)
    ong = p.sb("f1ong", [128, NDT], F32)
    onbv = p.sb("f1onbv", [128, NDT], F32)
    rng_ = p.sb("f1rng", [128, NDT], F32)
    xres = [p.sb(f"f1xr{i}", [128, 512], F32) for i in range(2)]
    for dst_, src_ in ((onb, ones_bf), (lng, lngt), (lnb_, lnbt),
                       (ong, ongt), (onbv, onbt), (rng_, rngt)):
        p.dma("sync", dst_[:], src_)
    for dt in range(NDT):
        p.dma("sync", rws[dt][:], rwT[dt * 128:(dt + 1) * 128, :])
    p.dma("sync", rnbF[:], rnbt)
    consts_ld = p.sD.n
    w_ld = p.sW.n

    def tiny(eng, fn):
        sem = {"vector": p.sV, "scalar": p.sS}[eng]
        p.wait(eng, sem)
        p.op(eng, fn, inc=sem)

    p.op("vector", lambda v: v.memset(onesc[:, :], 1.0), inc=p.sV)
    p.wait("vector", p.sD, consts_ld)
    for dt in range(NDT):
        p.op("vector", lambda v, o=rws[dt][:], i=rws[dt][:], g=rng_[:, dt:dt + 1]:
             v.tensor_scalar_mul(o, i, g), inc=(p.sV if dt == NDT - 1 else None))
    rws_ready = p.sV.n
    pb = st.psA
    st.tfree(pb)
    for dt in range(NDT):
        p.wait("sync", p.sT)
        p.dma("sync", rwtmp[:], rwT[dt * 128:(dt + 1) * 128, :])
        p.wait("tensor", p.sD)
        p.op("tensor", lambda t, o=pb.t[0:1, 0:4], l=rnbF[:, dt:dt + 1],
             r_=rwtmp[:], f=(dt == 0), la=(dt == NDT - 1):
             t.matmul(o, l, r_, start=f, stop=la), inc=p.sT)
    pb.ready = (p.sT, p.sT.n)
    pb2 = st.psB
    st.tfree(pb2)
    p.wait("tensor", p.sV, rws_ready)
    for dt in range(NDT):
        p.op("tensor", lambda t, o=pb2.t[0:1, 0:4], l=onesc[:, :], r_=rws[dt][:, :],
             f=(dt == 0), la=(dt == NDT - 1):
             t.matmul(o, l, r_, start=f, stop=la), inc=p.sT)
    pb2.ready = (p.sT, p.sT.n)
    p.wait("vector", *pb.ready)
    p.op("vector", lambda v: v.tensor_copy(rowf[0:1, 0:4], pb.t[0:1, 0:4]), inc=p.sV)
    pb.free = (p.sV, p.sV.n)
    p.wait("vector", *pb2.ready)
    p.op("vector", lambda v: v.tensor_copy(srow[0:1, :], pb2.t[0:1, 0:4]), inc=p.sV)
    pb2.free = (p.sV, p.sV.n)
    rows4_done = p.sV.n
    for ps, rsrc, dst_ in ((st.psA, rowf[0:1, 0:4], b4bc), (st.psB, srow[0:1, :], sbc)):
        st.tfree(ps)
        p.wait("tensor", p.sV, rows4_done)
        p.op("tensor", lambda t, o=ps.t[:, 0:4], l=st.ones_row[:], r_=rsrc:
             t.matmul(o, l, r_, start=True, stop=True), inc=p.sT)
        ps.ready = (p.sT, p.sT.n)
        p.wait("vector", *ps.ready)
        p.op("vector", lambda v, o=dst_[:], i=ps.t[:, 0:4]: v.tensor_copy(o, i),
             inc=p.sV)
        ps.free = (p.sV, p.sV.n)
    for tc in range(NTC):
        ldfn, xaps = x_src(tc)
        if ldfn is not None:
            ldfn()
        pl = st.psC
        st.tfree(pl)
        p.wait("tensor", p.sV, rws_ready)
        p.wait("tensor", p.sD)
        for dt in range(NDT):
            p.op("tensor", lambda t, o=pl.t[0:4, :], l=rws[dt][:, :], r_=xaps[dt],
                 f=(dt == 0), la=(dt == NDT - 1):
                 t.matmul(o, l, r_, start=f, stop=la),
                 inc=(p.sT if dt == NDT - 1 else None))
        pl.ready = (p.sT, p.sT.n)
        p.wait("vector", *pl.ready)
        p.op("vector", lambda v, i=pl.t[0:4, :]: v.tensor_copy(st.scr[0:4, :], i),
             inc=p.sV)
        pl.free = (p.sV, p.sV.n)
        lrows_done = p.sV.n
        pd = st.psD
        st.tfree(pd)
        p.wait("tensor", p.sV, lrows_done)
        for c in range(4):
            p.op("tensor", lambda t, o=pd.t[:, c * 4:(c + 1) * 4],
                 i=st.scr[0:4, c * 128:(c + 1) * 128]:
                 t.transpose(o, i, idf[0:4, 0:4]), inc=(p.sT if c == 3 else None))
        pd.ready = (p.sT, p.sT.n)
        p.wait("vector", *pd.ready)
        p.op("vector", lambda v, o=ltok[:, tc * 16:(tc + 1) * 16], i=pd.t[:, 0:16]:
             v.tensor_copy(o, i), inc=p.sV)
        pd.free = (p.sV, p.sV.n)
    tiny("vector", lambda v: v.tensor_scalar_add(mrow[:, 0:16], vart[:], EPS_LN))
    tiny("vector", lambda v: v.reciprocal(mrow[:, 16:32], mrow[:, 0:16]))
    tiny("vector", lambda v: v.tensor_tensor(mrow[:, 16:32], vart[:], mrow[:, 16:32],
                                             ALU.mult))
    tiny("vector", lambda v: v.tensor_scalar_add(mrow[:, 16:32], mrow[:, 16:32],
                                                 EPS_LN))
    tiny("vector", lambda v: v.tensor_tensor(vart[:], mrow[:, 16:32], mrow[:, 0:16],
                                             ALU.mult))
    p.wait("scalar", p.sV)
    p.op("scalar", lambda s: s.activation(out=vart[:], in_=vart[:], func=AF.Ln),
         inc=p.sS)
    p.wait("scalar", p.sS)
    p.op("scalar", lambda s: s.activation(out=vart[:], in_=vart[:], func=AF.Exp,
                                          scale=-0.5), inc=p.sS)
    p.wait("vector", p.sS)
    lt3 = ltok[:].rearrange("p (c e) -> p c e", e=4)
    wt3 = wtok[:].rearrange("p (c e) -> p c e", e=4)
    bc16 = lambda t_: t_.unsqueeze(2).broadcast_to([128, 16, 4])
    tiny("vector", lambda v: v.tensor_tensor(
        wt3, bc16(mut[:]), sbc[:].unsqueeze(1).broadcast_to([128, 16, 4]), ALU.mult))
    tiny("vector", lambda v: v.tensor_tensor(ltok[:], ltok[:], wtok[:], ALU.subtract))
    tiny("vector", lambda v: v.tensor_tensor(lt3, lt3, bc16(vart[:]), ALU.mult))
    tiny("vector", lambda v: v.tensor_tensor(
        lt3, lt3, b4bc[:].unsqueeze(1).broadcast_to([128, 16, 4]), ALU.add))
    tiny("vector", lambda v: v.reduce_max(mrow[:, 0:16], lt3, axis=AX))
    tiny("vector", lambda v: v.tensor_tensor(lt3, lt3, bc16(mrow[:, 0:16]),
                                             ALU.subtract))
    p.wait("scalar", p.sV)
    p.op("scalar", lambda s: s.activation(out=ltok[:], in_=ltok[:], func=AF.Exp),
         inc=p.sS)
    p.wait("vector", p.sS)
    tiny("vector", lambda v: v.reduce_sum(mrow[:, 0:16], lt3, axis=AX))
    tiny("vector", lambda v: v.reciprocal(mrow[:, 0:16], mrow[:, 0:16]))
    tiny("vector", lambda v: v.tensor_tensor(lt3, lt3, bc16(mrow[:, 0:16]), ALU.mult))
    tiny("vector", lambda v: v.reduce_max(mrow[:, 0:16], lt3, axis=AX))
    tiny("vector", lambda v: v.tensor_tensor(wt3, lt3, bc16(mrow[:, 0:16]),
                                             ALU.is_equal))
    tiny("vector", lambda v: v.scalar_tensor_tensor(
        out=wt3, in0=wt3, scalar=-1e30, in1=lt3, op0=ALU.mult, op1=ALU.add))
    tiny("vector", lambda v: v.reduce_max(mrow[:, 16:32], wt3, axis=AX))
    tiny("vector", lambda v: v.tensor_tensor(wt3, lt3, bc16(mrow[:, 16:32]),
                                             ALU.is_ge))
    tiny("vector", lambda v: v.tensor_tensor(wt3, wt3, lt3, ALU.mult))
    tiny("vector", lambda v: v.tensor_tensor(mrow[:, 0:16], mrow[:, 0:16],
                                             mrow[:, 16:32], ALU.add))
    tiny("vector", lambda v: v.tensor_scalar_add(mrow[:, 0:16], mrow[:, 0:16], 1e-8))
    tiny("vector", lambda v: v.reciprocal(mrow[:, 0:16], mrow[:, 0:16]))
    tiny("vector", lambda v: v.tensor_tensor(wt3, wt3, bc16(mrow[:, 0:16]), ALU.mult))
    wt_done = p.sV.n
    import os
    if os.environ.get("F1DBG"):
        dbg = p.nc.dram_tensor("dbg", [128, 64 + 16 + 64], F32, kind="ExternalOutput").ap()
        sdbg = p.sem("f1dbg")
        p.wait("sync", p.sV, wt_done)
        p.dma("sync", dbg[:, 0:64], wtok[:], inc=sdbg)
        p.dma("sync", dbg[:, 64:80], vart[:], inc=sdbg)
        p.dma("sync", dbg[:, 80:144], ltok[:], inc=sdbg)
        p.wait("sync", sdbg)
    pw = st.psF
    for e in range(4):
        for cg in range(4):
            st.tfree(pw)
            p.wait("tensor", p.sV, wt_done)
            for j in range(4):
                c = cg * 4 + j
                p.op("tensor", lambda t, o=pw.t[0:1, j * 128:(j + 1) * 128],
                     i=wtok[:, c * 4 + e:c * 4 + e + 1]:
                     t.transpose(o, i, idf[:]), inc=(p.sT if j == 3 else None))
            pw.ready = (p.sT, p.sT.n)
            p.wait("vector", *pw.ready)
            p.op("vector", lambda v,
                 o=wrow[0:1, e * T + cg * 512:e * T + (cg + 1) * 512],
                 i=pw.t[0:1, 0:512]: v.tensor_copy(o, i), inc=p.sV)
            pw.free = (p.sV, p.sV.n)
    wrow_done = p.sV.n
    import os as _os2
    if _os2.environ.get("F1DBG"):
        dbg3 = p.nc.dram_tensor("dbg3", [1, 4 * T], BF16, kind="ExternalOutput").ap()
        sdbg3 = p.sem("f1dbg3")
        p.wait("sync", p.sV, wrow_done)
        p.dma("sync", dbg3[0:1, :], wrow[0:1, :], inc=sdbg3)
        p.wait("sync", sdbg3)

    import os as _os
    dbg2 = None
    if _os.environ.get("F1DBG2"):
        dbg2 = p.nc.dram_tensor("dbg2", [128, 4 * 1024], F32, kind="ExternalOutput").ap()
        sdbg2 = p.sem("f1dbg2")
        dbg2_parts = []
    sOut = p.sem("f1sOut")
    f1_nout = 0
    xin, yout = resid_stream
    groups = [(th, e) for th in range(2) for e in range(4)]
    w1_rdy = {}

    def fetch_w1(gi, free_val):
        th_, e_ = groups[gi]
        p.wait("sync", p.sT, free_val)
        for dt in range(NDT):
            p.dma("sync", wbuf[dt][:], w1[e_, dt * 128:(dt + 1) * 128, :])
        w1_rdy[gi] = p.sD.n

    import os as _os8
    _PF = int(_os8.environ.get("F1PF", "1"))
    _ROT = int(_os8.environ.get("F1ROT", "4"))
    fetch_w1(0, 0)
    w2bb_free = [0, 0]
    for gi, (th, e) in enumerate(groups):
        t0 = th * 1024
        if True:
            if gi > 0 and not _PF:
                fetch_w1(gi, p.sT.n)
            w1_ready = w1_rdy[gi]
            p.wait("vector", p.sD, w1_ready)
            for dt in range(NDT):
                p.op("vector", lambda v, o=st.scr[:, 0:1], i=wbuf[dt][:]:
                     v.reduce_sum(o, i, axis=AX), inc=p.sV)
                p.wait("vector", p.sV)
                p.op("vector", lambda v, o=w1m[:, dt:dt + 1], i=st.scr[:, 0:1]:
                     v.tensor_scalar_mul(o, i, 1.0 / 2048.0), inc=p.sV)
                p.wait("vector", p.sV)
            w1m_ready = p.sV.n
            for tc2 in range(2):
                sl = slice(t0 + tc2 * 512, t0 + (tc2 + 1) * 512)
                pm_ = st.psC if tc2 == 0 else st.psD
                st.tfree(pm_)
                p.wait("tensor", p.sV, w1m_ready)
                for dt in range(NDT):
                    p.op("tensor", lambda t, o=pm_.t[0:1, :], l=w1m[:, dt:dt + 1],
                         r_=st.XN[dt][:, sl], f=(dt == 0), la=(dt == NDT - 1):
                         t.matmul(o, l, r_, start=f, stop=la),
                         inc=(p.sT if dt == NDT - 1 else None))
                pm_.ready = (p.sT, p.sT.n)
                p.wait("vector", *pm_.ready)
                p.op("vector", lambda v, o=rowb2[0:1, tc2 * 512:(tc2 + 1) * 512],
                     i=pm_.t[0:1, :]: v.tensor_copy(o, i), inc=p.sV)
                murow_done = p.sV.n
                pm_.free = (p.sV, murow_done)
                st.tfree(pm_)
                p.wait("tensor", p.sV, murow_done)
                p.op("tensor", lambda t, o=pm_.t[:, :], l=onb[:, :],
                     r_=rowb2[0:1, tc2 * 512:(tc2 + 1) * 512]:
                     t.matmul(o, l, r_, start=True, stop=True), inc=p.sT)
                pm_.ready = (p.sT, p.sT.n)
                p.wait("vector", *pm_.ready)
                p.op("vector", lambda v, o=mubc[:, tc2 * 512:(tc2 + 1) * 512],
                     i=pm_.t[:, :]: v.tensor_copy(o, i), inc=p.sV)
                pm_.free = (p.sV, p.sV.n)
            mubc_done = p.sV.n
            if dbg2 is not None and th == 0 and e == 0:
                p.wait("sync", p.sV, mubc_done)
                p.dma("sync", dbg2[:, 0:1024], mubc[:], inc=sdbg2)
            pstat = [st.psC, st.psD]
            st.tfree(pstat[0])
            st.tfree(pstat[1])
            stat_q = []
            stat_first = [True, True]
            stat_tfin = [0, 0]
            sq_sfin = [0, 0]
            prot = [st.psA, st.psB, st.psE, st.psF][:_ROT]
            for ht in range(16):
                for tc2 in range(2):
                    sl = slice(t0 + tc2 * 512, t0 + (tc2 + 1) * 512)
                    usl = slice(tc2 * 512, (tc2 + 1) * 512)
                    pp = prot[(ht * 2 + tc2) % _ROT]
                    st.tfree(pp)
                    if ht == 0 and tc2 == 0:
                        p.wait("tensor", p.sV, mubc_done)
                    for dt in range(NDT):
                        p.op("tensor", lambda t, o=pp.t[:, :],
                             l=wbuf[dt][:, ht * 128:(ht + 1) * 128],
                             r_=st.XN[dt][:, sl], f=(dt == 0), la=(dt == NDT - 1):
                             t.matmul(o, l, r_, start=f, stop=la),
                             inc=(p.sT if dt == NDT - 1 else None))
                    pp.ready = (p.sT, p.sT.n)
                    if ht == 15 and tc2 == 1:
                        hid_mm_done = p.sT.n
                    p.wait("vector", *pp.ready)
                    p.op("vector", lambda v, o=ubf[ht][:, usl], i=pp.t[:, :],
                         m=mubc[:, usl]:
                         v.tensor_tensor(o, i, m, ALU.subtract), inc=p.sV)
                    u_done = p.sV.n
                    pp.free = (p.sV, u_done)
                    i1_ = ht * 2 + tc2
                    p.wait("scalar", p.sV, u_done)
                    if i1_ >= 2:
                        p.wait("scalar", p.sT, stat_tfin[i1_ % 2])
                    p.op("scalar", lambda s, o=sqb[:, usl], i=ubf[ht][:, usl]:
                         s.activation(out=o, in_=i, func=AF.Square), inc=p.sS)
                    sq_sfin[i1_ % 2] = p.sS.n
                    stat_q.append((tc2, usl, p.sS.n, ht == 15))
                    if len(stat_q) >= 2:
                        tcp, uslp, sqv, lst = stat_q.pop(0)
                        p.wait("tensor", p.sS, sqv)
                        p.op("tensor", lambda t, o=pstat[tcp].t[0:1, :],
                             l=st.ones_col[:, 0:1], r_=sqb[:, uslp],
                             f0=stat_first[tcp], la=lst:
                             t.matmul(o, l, r_, start=f0, stop=la), inc=p.sT)
                        stat_first[tcp] = False
                        stat_tfin[tcp] = p.sT.n
            while stat_q:
                tcp, uslp, sqv, lst = stat_q.pop(0)
                p.wait("tensor", p.sS, sqv)
                p.op("tensor", lambda t, o=pstat[tcp].t[0:1, :],
                     l=st.ones_col[:, 0:1], r_=sqb[:, uslp], f0=stat_first[tcp],
                     la=lst:
                     t.matmul(o, l, r_, start=f0, stop=la), inc=p.sT)
                stat_first[tcp] = False
                stat_tfin[tcp] = p.sT.n
            pstat[0].ready = pstat[1].ready = (p.sT, p.sT.n)
            if _PF and gi + 1 < len(groups):
                fetch_w1(gi + 1, hid_mm_done)
            if dbg2 is not None and th == 0 and e == 0:
                p.wait("sync", p.sV)
                p.op("sync", lambda s_: s_.dma_start(
                    out=dbg2[:, 1024:2048],
                    in_=ubf[0][:].bitcast(F32)) if False else None)
            p.wait("vector", *pstat[0].ready)
            for tc2 in range(2):
                rsl = slice(tc2 * 512, (tc2 + 1) * 512)
                p.op("vector", lambda v, o=rowf[0:1, rsl], i=pstat[tc2].t[0:1, :]:
                     v.tensor_scalar_mul(o, i, 1.0 / 2048.0),
                     inc=(p.sV if tc2 == 1 else None))
            pstat[0].free = pstat[1].free = (p.sV, p.sV.n)
            tiny("vector", lambda v: v.tensor_scalar_add(rowf[0:1, 0:1024],
                                                         rowf[0:1, 0:1024], EPS_LN))
            p.wait("scalar", p.sV)
            p.op("scalar", lambda s: s.activation(out=rowf[0:1, 0:1024],
                                                  in_=rowf[0:1, 0:1024], func=AF.Ln),
                 inc=p.sS)
            p.wait("scalar", p.sS)
            p.op("scalar", lambda s: s.activation(out=rowf[0:1, 0:1024],
                                                  in_=rowf[0:1, 0:1024], func=AF.Exp,
                                                  scale=-0.5), inc=p.sS)
            p.wait("vector", p.sS)
            p.op("vector", lambda v: v.tensor_copy(rowb2[0:1, 0:1024],
                                                   rowf[0:1, 0:1024]), inc=p.sV)
            rrows_done = p.sV.n
            bcs = [st.psC, st.psD, st.psE, st.psF]
            for i2, (ps, rsrc) in enumerate((
                    (st.psC, rowb2[0:1, 0:512]), (st.psD, rowb2[0:1, 512:1024]),
                    (st.psE, wrow[0:1, e * T + t0:e * T + t0 + 512]),
                    (st.psF, wrow[0:1, e * T + t0 + 512:e * T + t0 + 1024]))):
                st.tfree(ps)
                p.wait("tensor", p.sV, rrows_done)
                if i2 >= 2:
                    p.wait("tensor", p.sV, wrow_done)
                p.op("tensor", lambda t, o=ps.t[:, :], l=onb[:, :], r_=rsrc:
                     t.matmul(o, l, r_, start=True, stop=True), inc=p.sT)
                ps.ready = (p.sT, p.sT.n)
            p.wait("vector", *bcs[3].ready)
            ph2 = [(ht, tc2) for tc2 in range(2) for ht in range(16)]
            hw_half = [0, 0]
            m1v = {}
            m2v = {}
            actS = {}
            pend = []

            def emit_mult2(j):
                htj, tcj = ph2[j]
                uslj = slice(tcj * 512, (tcj + 1) * 512)
                p.wait("vector", p.sS, actS[j])
                p.op("vector", lambda v, o=ubf[htj][:, uslj], a=hrl2[j % 2][:],
                     bc=bcs[2 + tcj].t[:, :]:
                     v.tensor_tensor(o, a, bc, ALU.mult), inc=p.sV)
                m2v[j] = p.sV.n
                if htj == 15:
                    hw_half[tcj] = p.sV.n

            for i2_, (ht, tc2) in enumerate(ph2):
                usl = slice(tc2 * 512, (tc2 + 1) * 512)
                pb_ = i2_ % 2
                scr_ = scr2[pb_]
                hrl_ = hrl2[pb_]
                if i2_ >= 2:
                    p.wait("vector", p.sS, actS[i2_ - 2])
                p.op("vector", lambda v, o=scr_[:], a=ubf[ht][:, usl],
                     bc=bcs[tc2].t[:, :]:
                     v.tensor_tensor(o, a, bc, ALU.mult), inc=p.sV)
                m1v[i2_] = p.sV.n
                p.wait("scalar", p.sV, m1v[i2_])
                if i2_ >= 2:
                    p.wait("scalar", p.sV, m2v[i2_ - 2])
                p.op("scalar", lambda s, o=hrl_[:], i=scr_[:], col=e * 16 + ht:
                     s.activation(out=o, in_=i, func=AF.Relu,
                                  scale=lng[:, col:col + 1],
                                  bias=lnb_[:, col:col + 1]), inc=p.sS)
                actS[i2_] = p.sS.n
                pend.append(i2_)
                if len(pend) >= 2:
                    emit_mult2(pend.pop(0))
            while pend:
                emit_mult2(pend.pop(0))
            hw_done = p.sV.n
            if dbg2 is not None and th == 0 and e == 0:
                p.wait("sync", p.sV, hw_done)
                p.dma("sync", dbg2[:, 2048:2560], ubf[0][:].bitcast(F32), inc=sdbg2)
                p.wait("vector", sdbg2)
                p.wait("tensor", sdbg2)
            for ps in bcs:
                ps.free = (p.sV, hw_done)
            w2_rdy = [0, 0]

            def fetch_w2(dch_, e_=e):
                p.wait("gpsimd", p.sT, w2bb_free[dch_ % 2])
                p.dma("gpsimd", w2bb[dch_ % 2][:].rearrange("p (h c) -> p h c", h=16),
                      w2t[e_, dch_].rearrange("h p c -> p h c"))
                w2_rdy[dch_ % 2] = p.sW.n

            fetch_w2(0)
            fetch_w2(1)
            for dch in range(NDT):
                w2c = w2bb[dch % 2]
                w2_ready = w2_rdy[dch % 2]
                for tc2 in range(2):
                    usl = slice(tc2 * 512, (tc2 + 1) * 512)
                    pp = st.psA if tc2 % 2 == 0 else st.psB
                    st.tfree(pp)
                    p.wait("tensor", p.sW, w2_ready)
                    if dch == 0:
                        p.wait("tensor", p.sV, hw_half[tc2])
                    for ht in range(16):
                        p.op("tensor", lambda t, o=pp.t[:, :],
                             l=w2c[:, ht * 128:(ht + 1) * 128], r_=ubf[ht][:, usl],
                             f=(ht == 0), la=(ht == 15):
                             t.matmul(o, l, r_, start=f, stop=la),
                             inc=(p.sT if ht == 15 else None))
                    pp.ready = (p.sT, p.sT.n)
                    p.wait("vector", *pp.ready)
                    if e == 0:
                        p.op("vector", lambda v, o=outac[dch][:, usl], i=pp.t[:, :]:
                             v.tensor_copy(o, i), inc=p.sV)
                    else:
                        p.op("vector", lambda v, o=outac[dch][:, usl], i=pp.t[:, :]:
                             v.tensor_tensor(o, i, o, ALU.add), inc=p.sV)
                    pp.free = (p.sV, p.sV.n)
                w2bb_free[dch % 2] = p.sT.n
                if dch + 2 < NDT:
                    fetch_w2(dch + 2)
            if dbg2 is not None and th == 0 and e == 0:
                p.wait("sync", p.sV)
                p.dma("sync", dbg2[:, 1024:2048], outac[0][:], inc=sdbg2)
                p.wait("vector", sdbg2)
                p.wait("tensor", sdbg2)
        if e != 3:
            continue
        if dbg2 is not None and th == 0:
            p.wait("sync", p.sV)
            p.dma("sync", dbg2[:, 3072:4096], outac[0][:], inc=sdbg2)
            p.wait("vector", sdbg2)
            p.wait("tensor", sdbg2)
        pS1, pS2, pQ1, pQ2 = st.psC, st.psD, st.psE, st.psF
        for ps in (pS1, pS2, pQ1, pQ2):
            st.tfree(ps)
        oln_cp = {}
        oln_sq = {}
        oln_tfin = [0, 0]
        oln_pend = []

        def oln_stats(dc_):
            par_ = dc_ % 2
            ob_ = obf2[par_]
            sq_ = sqb2[par_]
            p.wait("tensor", p.sV, oln_cp[dc_])
            p.wait("tensor", p.sS, oln_sq[dc_])
            for tc2 in range(2):
                usl = slice(tc2 * 512, (tc2 + 1) * 512)
                p.op("tensor", lambda t, o=(pS1 if tc2 == 0 else pS2).t[0:1, :],
                     l=st.ones_col[:, 0:1], r_=ob_[:, usl],
                     f=(dc_ == 0), la=(dc_ == NDT - 1):
                     t.matmul(o, l, r_, start=f, stop=la))
                p.op("tensor", lambda t, o=(pQ1 if tc2 == 0 else pQ2).t[0:1, :],
                     l=st.ones_col[:, 0:1], r_=sq_[:, usl],
                     f=(dc_ == 0), la=(dc_ == NDT - 1):
                     t.matmul(o, l, r_, start=f, stop=la),
                     inc=(p.sT if tc2 == 1 else None))
            oln_tfin[par_] = p.sT.n

        for dch in range(NDT):
            par = dch % 2
            if dch >= 2:
                p.wait("vector", p.sT, oln_tfin[par])
            else:
                p.wait("vector", p.sT)
            p.op("vector", lambda v, o=obf2[par][:], i=outac[dch][:]:
                 v.tensor_copy(o, i), inc=p.sV)
            oln_cp[dch] = p.sV.n
            p.wait("scalar", p.sV, oln_cp[dch])
            if dch >= 2:
                p.wait("scalar", p.sT, oln_tfin[par])
            else:
                p.wait("scalar", p.sT)
            p.op("scalar", lambda s, o=sqb2[par][:], i=outac[dch][:]:
                 s.activation(out=o, in_=i, func=AF.Square), inc=p.sS)
            oln_sq[dch] = p.sS.n
            oln_pend.append(dch)
            if len(oln_pend) >= 2:
                oln_stats(oln_pend.pop(0))
        while oln_pend:
            oln_stats(oln_pend.pop(0))
        stats_done = p.sT.n
        for ps in (pS1, pS2, pQ1, pQ2):
            ps.ready = (p.sT, stats_done)
        p.wait("vector", p.sT, stats_done)
        for tc2, (psS, psQ) in enumerate(((pS1, pQ1), (pS2, pQ2))):
            mu_ = rowf[0:1, tc2 * 512:(tc2 + 1) * 512]
            v2_ = rowf[0:1, 1024 + tc2 * 512:1024 + (tc2 + 1) * 512]
            ms_ = rowf[0:1, 2048 + tc2 * 512:2048 + (tc2 + 1) * 512]
            p.op("vector", lambda v, o=mu_, i=psS.t[0:1, :]:
                 v.tensor_scalar_mul(o, i, 1.0 / D))
            p.op("vector", lambda v, o=v2_, i=psQ.t[0:1, :]:
                 v.tensor_scalar_mul(o, i, 1.0 / D), inc=p.sV)
            p.op("vector", lambda v, o=ms_, a=mu_: v.tensor_tensor(o, a, a, ALU.mult))
            p.op("vector", lambda v, o=v2_, a=ms_, b=v2_: v.scalar_tensor_tensor(
                 out=o, in0=a, scalar=-1.0, in1=b, op0=ALU.mult, op1=ALU.add))
            p.op("vector", lambda v, o=v2_, a=v2_:
                 v.tensor_scalar_add(o, a, EPS_LN), inc=p.sV)
            p.wait("scalar", p.sV)
            p.op("scalar", lambda s, o=v2_, i=v2_:
                 s.activation(out=o, in_=i, func=AF.Ln), inc=p.sS)
            p.wait("scalar", p.sS)
            p.op("scalar", lambda s, o=v2_, i=v2_:
                 s.activation(out=o, in_=i, func=AF.Exp, scale=-0.5), inc=p.sS)
            p.wait("vector", p.sS)
            p.op("vector", lambda v, o=ms_, a=mu_, b=v2_: v.scalar_tensor_tensor(
                 out=o, in0=a, scalar=-1.0, in1=b, op0=ALU.mult, op1=ALU.mult))
            p.op("vector", lambda v, o=rowb2[0:1, 0:512], i=v2_: v.tensor_copy(o, i))
            p.op("vector", lambda v, o=rowb2[0:1, 512:1024], i=ms_:
                 v.tensor_copy(o, i), inc=p.sV)
            orow_done = p.sV.n
            import os as _os3
            if _os3.environ.get("F1DBG4") and th == 0 and tc2 == 0:
                dbg4 = p.nc.dram_tensor("dbg4", [1, 3 * 512], F32, kind="ExternalOutput").ap()
                sdbg4 = p.sem("f1dbg4")
                p.wait("sync", p.sV, orow_done)
                p.dma("sync", dbg4[0:1, 0:512], mu_, inc=sdbg4)
                p.dma("sync", dbg4[0:1, 512:1024], v2_, inc=sdbg4)
                p.dma("sync", dbg4[0:1, 1024:1536], ms_, inc=sdbg4)
                p.wait("sync", sdbg4)
            psS.free = psQ.free = (p.sV, orow_done)
            pA, pB = st.psA, st.psB
            for ps, rsrc in ((pA, rowb2[0:1, 0:512]), (pB, rowb2[0:1, 512:1024])):
                st.tfree(ps)
                p.wait("tensor", p.sV, orow_done)
                p.op("tensor", lambda t, o=ps.t[:, :], l=onb[:, :], r_=rsrc:
                     t.matmul(o, l, r_, start=True, stop=True), inc=p.sT)
                ps.ready = (p.sT, p.sT.n)
            p.wait("vector", *pA.ready)
            p.wait("vector", *pB.ready)
            for dch in range(NDT):
                usl = slice(tc2 * 512, (tc2 + 1) * 512)
                gsl = slice(t0 + tc2 * 512, t0 + (tc2 + 1) * 512)
                xr = xres[dch % 2]
                p.wait("sync", p.sV)
                p.wait("sync", sOut, 16 * max(0, f1_nout - 1))
                f1_nout += 1
                p.dma("sync", xr[:], xin[dch * 128:(dch + 1) * 128, gsl])
                xr_ready = p.sD.n
                p.op("vector", lambda v, o=st.scr[:], a=outac[dch][:, usl],
                     bc=pA.t[:, :]: v.tensor_tensor(o, a, bc, ALU.mult))
                p.op("vector", lambda v, o=st.scr[:], a=st.scr[:], bc=pB.t[:, :]:
                     v.tensor_tensor(o, a, bc, ALU.add))
                p.op("vector", lambda v, o=st.scr[:], i=st.scr[:], dc=dch:
                     v.tensor_scalar(out=o, in0=i, scalar1=ong[:, dc:dc + 1],
                                     scalar2=onbv[:, dc:dc + 1], op0=ALU.mult,
                                     op1=ALU.add))
                p.wait("vector", p.sD, xr_ready)
                p.op("vector", lambda v, o=xr[:], a=st.scr[:], b=xr[:]:
                     v.tensor_tensor(o, a, b, ALU.add), inc=p.sV)
                p.wait("scalar", p.sV)
                p.dma("scalar", yout[dch * 128:(dch + 1) * 128, gsl], xr[:], inc=sOut)
            pA.free = pB.free = (p.sV, p.sV.n)
    p.wait("sync", sOut)
    return (sOut, sOut.n)


def prep_ffn1_weights(router_w, rn_g, rn_b, w1, ln_g, ln_b, w2, on_g, on_b):
    lngt = np.zeros((128, 64), np.float32)
    lnbt = np.zeros((128, 64), np.float32)
    for e in range(4):
        for ht in range(16):
            lngt[:, e * 16 + ht] = ln_g[e, ht * 128:(ht + 1) * 128]
            lnbt[:, e * 16 + ht] = ln_b[e, ht * 128:(ht + 1) * 128]
    tile8 = lambda vec: np.ascontiguousarray(vec.reshape(8, 128).T)
    return {
        "rwT": np.ascontiguousarray(router_w.T),
        "rngt": tile8(rn_g), "rnbt": tile8(rn_b),
        "w1": np.ascontiguousarray(w1).astype(BF),
        "lngt": lngt, "lnbt": lnbt,
        "w2t": np.ascontiguousarray(
            w2.reshape(4, 16, 128, 8, 128).transpose(0, 3, 1, 2, 4)).astype(BF),
        "ongt": tile8(on_g), "onbt": tile8(on_b),
        "idf32": np.eye(128, dtype=np.float32),
        "ones_bf": np.ones((1, 128), BF),
    }


def build_ffn1_launch():
    p = Prog()
    xT = p.inp("xT", [D, T], F32)
    rwT = p.inp("rwT", [D, 4], F32)
    rngt = p.inp("rngt", [128, NDT], F32)
    rnbt = p.inp("rnbt", [128, NDT], F32)
    w1 = p.inp("w1", [4, D, 2048], BF16)
    lngt = p.inp("lngt", [128, 64], F32)
    lnbt = p.inp("lnbt", [128, 64], F32)
    w2t = p.inp("w2t", [4, 8, 16, 128, 128], BF16)
    ongt = p.inp("ongt", [128, NDT], F32)
    onbt = p.inp("onbt", [128, NDT], F32)
    idf32 = p.inp("idf32", [128, 128], F32)
    ones_bf = p.inp("ones_bf", [1, 128], BF16)
    consts = declare_const_inputs(p)
    yT = p.outp("yT", [D, T], F32)
    stt = St(p, with_x=False)
    stt.load_consts(*consts)
    ubf = [p.sb(f"f1u{i}", [128, 1024], BF16) for i in range(16)]
    xstr = [ubf[i][:].bitcast(F32) for i in range(NDT)]
    mut = p.sb("f1mut", [128, 16], F32)
    vart = p.sb("f1vt", [128, 16], F32)
    idfs = p.sb("f1id", [128, 128], F32)
    p.dma("sync", idfs[:], idf32)
    idf_ld = p.sD.n
    pF = stt.psF

    def row_stash(tc, rows_val):
        p.wait("tensor", p.sD, idf_ld)
        p.wait("tensor", p.sV, rows_val)
        stt.tfree(pF)
        for c in range(4):
            p.op("tensor", lambda t, o=pF.t[:, c:c + 1],
                 i=stt.rows[0:1, c * 128:(c + 1) * 128]:
                 t.transpose(o, i, idfs[0:1, 0:1]))
        for c in range(4):
            p.op("tensor", lambda t, o=pF.t[:, 8 + c:9 + c],
                 i=stt.rows[0:1, 3 * 512 + c * 128:3 * 512 + (c + 1) * 128]:
                 t.transpose(o, i, idfs[0:1, 0:1]),
                 inc=(p.sT if c == 3 else None))
        tdone = p.sT.n
        pF.ready = (p.sT, tdone)
        p.wait("vector", *pF.ready)
        p.op("vector", lambda v, o=mut[:, tc * 4:(tc + 1) * 4], i=pF.t[:, 0:4]:
             v.tensor_copy(o, i))
        p.op("vector", lambda v, o=vart[:, tc * 4:(tc + 1) * 4], i=pF.t[:, 8:12]:
             v.tensor_copy(o, i), inc=p.sV)
        pF.free = (p.sV, p.sV.n)
        return tdone
    xn_ready = emit_preln(stt, x_hbm=xT, xstr=xstr, row_stash=row_stash)

    def x_src(tc):
        sl = slice(tc * 512, (tc + 1) * 512)

        def ld():
            p.wait("sync", p.sT)
            for dt in range(NDT):
                p.dma("sync", xstr[dt][:, 0:512], xT[dt * 128:(dt + 1) * 128, sl])
        return ld, [xstr[dt][:, 0:512] for dt in range(NDT)]

    emit_ffn1(stt, rwT, rngt, rnbt, w1, lngt, lnbt, w2t, ongt, onbt,
              mut, vart, x_src, idfs, ones_bf, xn_ready,
              resid_stream=(xT, yT), ubf=ubf)
    return p.finish()



def _a0_dummy_out(p, st, resid_stream):
    sOutD = p.sem("a0dum")
    p.wait("sync", p.sV)
    p.wait("sync", p.sS)
    p.wait("sync", p.sT)
    p.dma("sync", resid_stream[1][0:128, 0:512], st.scr[:], inc=sOutD)
    p.wait("sync", sOutD)


def emit_attn_a0(st: St, wrT, wkT, wvT, wwT, woT, tmt, ut, lnw64, lnb64, idbf,
                 mask_scan, mask32, xn_ready, resid_stream, vtok=None):
    import os
    PH = int(os.environ.get("A0PHASE", "9"))
    p = st.p
    rT = [p.sb(f"a0r{i}", [128, T], BF16) for i in range(NDT)]
    kT = [p.sb(f"a0k{i}", [128, T], BF16) for i in range(NDT)]
    if vtok is None:
        vtok = [p.sb(f"a0v{i}", [128, 1024], BF16) for i in range(16)]
    osb = vtok
    MIX = [p.sb(f"a0m{i}", [128, T], BF16) for i in range(NDT)]
    ktok = [MIX[i // 2][:, (i % 2) * 1024:(i % 2 + 1) * 1024] for i in range(16)]
    rtl = st.XN
    oT = st.XN
    wbuf = [p.sb(f"a0w{i}", [128, 1024], BF16) for i in range(NDT)]
    Ssb = p.sb("a0S", [128, 512], F32)
    Sbf = p.sb("a0Sb", [128, 512], BF16)
    Dsb = p.sb("a0D", [128, 512], BF16)
    csb = p.sb("a0c", [128, 256], F32)
    negw = p.sb("a0nw", [128, 512], F32)
    poscum = p.sb("a0pc", [128, 512], F32)
    grows = p.sb("a0gr", [128, 3 * 16], F32)
    pbf = p.sb("a0pb", [128, 512], BF16)
    tmv = p.sb("a0tm", [128, 32], F32)
    uv = p.sb("a0u", [128, NDT], F32)
    lnwv = p.sb("a0lw", [128, 64], F32)
    lnbv = p.sb("a0lb", [128, 64], F32)
    idb = p.sb("a0id", [128, 128], BF16)
    msc = p.sb("a0ms", [128, 512], BF16)
    m32 = p.sb("a0m32", [32, 32], BF16)
    hofF = p.sb("a0hF", [128, 2], F32)
    onec1 = p.sb("a0one1", [128, 1], F32)
    xres = [p.sb(f"a0xr{i}", [128, 512], F32) for i in range(2)]
    for dst_, src_ in ((tmv, tmt), (uv, ut), (lnwv, lnw64), (lnbv, lnb64),
                      (idb, idbf), (m32, mask32)):
        p.dma("sync", dst_[:], src_)
    consts_ld = p.sD.n
    p.op("vector", lambda v: v.memset(msc[:, :], 1.0))
    p.op("vector", lambda v: v.memset(
        msc[:].rearrange("p (c q) -> p c q", q=32)[:, :, 0:1], 0.0), inc=p.sV)
    p.op("vector", lambda v: v.memset(hofF[:, :], 0.0))
    p.op("vector", lambda v: v.memset(hofF[0:64, 0:1], 1.0))
    p.op("vector", lambda v: v.memset(hofF[64:128, 1:2], 1.0))
    p.op("vector", lambda v: v.memset(onec1[:, :], 1.0))
    p.op("vector", lambda v: v.memset(Ssb[:, :], 0.0))
    p.op("vector", lambda v: v.memset(Sbf[:, :], 0.0), inc=p.sV)
    p.nc.const_aps.register(F32, 1.0, onec1[:, 0:1])

    def build_mix(col):
        p.wait("vector", p.sT)
        p.wait("vector", p.sD, consts_ld)
        mix_tcs = []
        for tc in range(NTC):
            for dt in range(NDT):
                if tc == 0:
                    p.op("vector", lambda v, o=MIX[dt][:, 0:1], i=st.XN[dt][:, 0:1],
                         tm=tmv[:, 4 * dt + col:4 * dt + col + 1]:
                         v.tensor_scalar_mul(o, i, tm))
                lo1 = max(tc * 512, 1)
                n = 512 if tc > 0 else 511
                p.op("vector", lambda v, o=st.scr[:, 0:n], n=n,
                     a=st.XN[dt][:, lo1:lo1 + n], b=st.XN[dt][:, lo1 - 1:lo1 + n - 1]:
                     v.tensor_tensor(o, a, b, ALU.subtract))
                p.op("vector", lambda v, o=MIX[dt][:, lo1:lo1 + n], n=n,
                     a=st.scr[:, 0:n], b=st.XN[dt][:, lo1 - 1:lo1 + n - 1],
                     tm=tmv[:, 4 * dt + col:4 * dt + col + 1]:
                     v.scalar_tensor_tensor(out=o, in0=a, scalar=tm, in1=b,
                                            op0=ALU.mult, op1=ALU.add),
                     inc=(p.sV if dt == NDT - 1 else None))
            mix_tcs.append(p.sV.n)
        return mix_tcs

    def proj(wsrc, outmode, dsts, sig=False, mix_tcs=None):
        p.wait("gpsimd", p.sT)
        for dt in range(NDT):
            p.dma("gpsimd", wbuf[dt][:], wsrc[dt * 128:(dt + 1) * 128, :])
        w_ready = p.sW.n
        if outmode == "feat":
            for tc in range(NTC):
                for ec in range(NDT):
                    sl = slice(tc * 512, (tc + 1) * 512)
                    pp = st.psA if (tc * NDT + ec) % 2 == 0 else st.psB
                    st.tfree(pp)
                    p.wait("tensor", p.sW, w_ready)
                    if ec == 0:
                        p.wait("tensor", p.sV,
                               mix_tcs[tc] if mix_tcs else None)
                    for dt in range(NDT):
                        p.op("tensor", lambda t, o=pp.t[:, :],
                             l=wbuf[dt][:, ec * 128:(ec + 1) * 128],
                             r_=MIX[dt][:, sl], f=(dt == 0), la=(dt == NDT - 1):
                             t.matmul(o, l, r_, start=f, stop=la),
                             inc=(p.sT if dt == NDT - 1 else None))
                    pp.ready = (p.sT, p.sT.n)
                    if sig:
                        p.wait("scalar", *pp.ready)
                        p.op("scalar", lambda s, o=dsts[ec][:, sl], i=pp.t[:, :]:
                             s.activation(out=o, in_=i, func=AF.Sigmoid), inc=p.sS)
                        pp.free = (p.sS, p.sS.n)
                    else:
                        p.wait("vector", *pp.ready)
                        p.op("vector", lambda v, o=dsts[ec][:, sl], i=pp.t[:, :]:
                             v.tensor_copy(o, i), inc=p.sV)
                        pp.free = (p.sV, p.sV.n)
        else:
            for tt in range(16):
                for ec in range(2):
                    pp = st.psA if (tt * 2 + ec) % 2 == 0 else st.psB
                    st.tfree(pp)
                    p.wait("tensor", p.sW, w_ready)
                    if ec == 0:
                        p.wait("tensor", p.sV,
                               mix_tcs[tt // 4] if mix_tcs else None)
                    for dt in range(NDT):
                        p.op("tensor", lambda t, o=pp.t[:, :],
                             l=MIX[dt][:, tt * 128:(tt + 1) * 128],
                             r_=wbuf[dt][:, ec * 512:(ec + 1) * 512],
                             f=(dt == 0), la=(dt == NDT - 1):
                             t.matmul(o, l, r_, start=f, stop=la),
                             inc=(p.sT if dt == NDT - 1 else None))
                    pp.ready = (p.sT, p.sT.n)
                    p.wait("vector", *pp.ready)
                    p.op("vector", lambda v, o=dsts[tt][:, ec * 512:(ec + 1) * 512],
                         i=pp.t[:, :]: v.tensor_copy(o, i), inc=p.sV)
                    pp.free = (p.sV, p.sV.n)

    m0 = build_mix(0)
    proj(wrT, "feat", rT, sig=True, mix_tcs=m0)
    m1 = build_mix(1)
    proj(wkT, "feat", kT, mix_tcs=m1)
    m2 = build_mix(2)
    proj(wvT, "tok", vtok, mix_tcs=m2)
    build_mix(3)

    if PH < 2:
        _a0_dummy_out(p, st, resid_stream)
        return
    p.wait("gpsimd", p.sT)
    for dt in range(NDT):
        p.dma("gpsimd", wbuf[dt][:], wwT[dt * 128:(dt + 1) * 128, :])
    ww_ready = p.sW.n
    pcb = st.psC
    st.tfree(pcb)
    for ec in range(NDT):
        for tc in range(NTC):
            gsl = slice(tc * 512, (tc + 1) * 512)
            pp = st.psA if tc % 2 == 0 else st.psB
            st.tfree(pp)
            p.wait("tensor", p.sW, ww_ready)
            if ec == 0 and tc == 0:
                p.wait("tensor", p.sV)
            for dt in range(NDT):
                p.op("tensor", lambda t, o=pp.t[:, :],
                     l=wbuf[dt][:, ec * 128:(ec + 1) * 128],
                     r_=MIX[dt][:, gsl], f=(dt == 0), la=(dt == NDT - 1):
                     t.matmul(o, l, r_, start=f, stop=la),
                     inc=(p.sT if dt == NDT - 1 else None))
            pp.ready = (p.sT, p.sT.n)
            p.wait("scalar", *pp.ready)
            p.wait("scalar", p.sV)
            p.op("scalar", lambda s, o=poscum[:, 0:512], i=pp.t[:, :]:
                 s.activation(out=o, in_=i, func=AF.Abs), inc=p.sS)
            abs_done = p.sS.n
            p.op("scalar", lambda s, o=poscum[:, 0:512], i=poscum[:, 0:512]:
                 s.activation(out=o, in_=i, func=AF.Exp, scale=-1.0))
            p.op("scalar", lambda s, o=poscum[:, 0:512], i=poscum[:, 0:512]:
                 s.activation(out=o, in_=i, func=AF.Ln, bias=onec1[:, 0:1]),
                 inc=p.sS)
            p.wait("vector", p.sS, abs_done)
            p.op("vector", lambda v, o=negw[:, 0:512], i=pp.t[:, :]:
                 v.tensor_scalar_mul(o, i, -1.0), inc=p.sV)
            pp.free = (p.sV, p.sV.n)
            p.op("vector", lambda v, o=negw[:, 0:512], i=negw[:, 0:512]:
                 v.tensor_relu(o, i))
            p.wait("vector", p.sS)
            p.op("vector", lambda v, o=negw[:, 0:512], a=negw[:, 0:512],
                 b=poscum[:, 0:512]:
                 v.scalar_tensor_tensor(out=o, in0=a, scalar=1e-4, in1=b,
                                        op0=ALU.add, op1=ALU.add), inc=p.sV)
            p.wait("vector", p.sV)
            p.op("vector", lambda v: v.tensor_tensor_scan(
                out=poscum[:, 0:512], data0=msc[:, 0:512], data1=negw[:, 0:512],
                initial=0.0, op0=ALU.mult, op1=ALU.add), inc=p.sV)
            pc_done = p.sV.n
            p.wait("scalar", p.sV, pc_done)
            p.op("scalar", lambda s,
                 o=Dsb[:, ec * 64 + tc * 16:ec * 64 + tc * 16 + 16],
                 i=poscum[:, 0:512].rearrange("p (c q) -> p c q", q=32)[:, :, 31:32]:
                 s.activation(out=o, in_=i, func=AF.Exp, scale=-1.0))
            p.op("scalar", lambda s,
                 o=negw[:, 0:512].rearrange("p (c q) -> p c q", q=32)[:, :, 1:32],
                 i=poscum[:, 0:512].rearrange("p (c q) -> p c q", q=32)[:, :, 0:31]:
                 s.activation(out=o, in_=i, func=AF.Exp, scale=-1.0), inc=p.sS)
            p.wait("vector", p.sS)
            p.op("vector", lambda v,
                 o=negw[:, 0:512].rearrange("p (c q) -> p c q", q=32)[:, :, 0:1]:
                 v.memset(o, 1.0), inc=p.sV)
            p.wait("vector", p.sV)
            p.op("vector", lambda v, o=rtl[ec][:, gsl], a=rT[ec][:, gsl],
                 b=negw[:, 0:512]:
                 v.tensor_tensor(o, a, b, ALU.mult), inc=p.sV)
            rt_done = p.sV.n
            p.wait("scalar", p.sV, rt_done)
            p.op("scalar", lambda s, o=negw[:, 0:512], i=kT[ec][:, gsl],
                 b=uv[:, ec:ec + 1]:
                 s.activation(out=o, in_=i, func=AF.Exp, bias=b), inc=p.sS)
            p.wait("vector", p.sS)
            p.op("vector", lambda v, o=negw[:, 0:512], a=rT[ec][:, gsl],
                 b=negw[:, 0:512]:
                 v.tensor_tensor(o, b, a, ALU.mult), inc=p.sV)
            re_done = p.sV.n
            p.wait("tensor", p.sV, re_done)
            for tl in range(4):
                tt = tc * 4 + tl
                p.op("tensor", lambda t,
                     o=pcb.t[:, tt * 16 + 2 * ec:tt * 16 + 2 * ec + 2],
                     l=negw[:, tl * 128:(tl + 1) * 128], r_=hofF[:, :]:
                     t.matmul(o, l, r_, start=True, stop=True),
                     inc=(p.sT if tl == 3 else None))
            cmm_done = p.sT.n
            p.wait("scalar", p.sT, cmm_done)
            p.op("scalar", lambda s, o=negw[:, 0:512], i=poscum[:, 0:512]:
                 s.activation(out=o, in_=i, func=AF.Exp), inc=p.sS)
            p.wait("vector", p.sS)
            p.op("vector", lambda v, o=kT[ec][:, gsl], a=kT[ec][:, gsl],
                 b=negw[:, 0:512]:
                 v.tensor_tensor(o, a, b, ALU.mult), inc=p.sV)
    prep_done = p.sV.n
    wo_can_load = p.sT.n
    pcb.ready = (p.sT, p.sT.n)
    p.wait("vector", *pcb.ready)
    p.op("vector", lambda v: v.tensor_copy(csb[:], pcb.t[:, 0:256]), inc=p.sV)
    pcb.free = (p.sV, p.sV.n)

    if PH < 3:
        _a0_dummy_out(p, st, resid_stream)
        return
    pt = st.psD
    ptb = pt.t[:].bitcast(BF16)
    for dt in range(NDT):
        for tt in range(16):
            st.tfree(pt)
            if dt == 0 and tt == 0:
                p.wait("tensor", p.sV, prep_done)
            p.op("tensor", lambda t, o=ptb[:, 0:128],
                 i=kT[dt][:, tt * 128:(tt + 1) * 128]:
                 t.transpose(o, i, idb[:]), inc=p.sT)
            pt.ready = (p.sT, p.sT.n)
            p.wait("vector", *pt.ready)
            p.op("vector", lambda v, o=ktok[tt][:, dt * 128:(dt + 1) * 128],
                 i=ptb[:, 0:128]: v.tensor_copy(o, i), inc=p.sV)
            pt.free = (p.sV, p.sV.n)
    ktok_done = p.sV.n

    if PH < 4:
        _a0_dummy_out(p, st, resid_stream)
        return
    SUB = int(os.environ.get("A0SUB", "15"))
    NTT = int(os.environ.get("A0TT", "16"))
    NCC = int(os.environ.get("A0CC", "4"))
    pG = st.psC
    pP = st.psD
    poI = st.psO
    for tt in range(NTT):
        st.tfree(poI)
        st.tfree(st.psE)
        st.tfree(st.psF)
        for cc in range(NCC):
            c = tt * 4 + cc
            rb = cc * 32
            st.tfree(pG)
            if c == 0:
                p.wait("tensor", p.sV, ktok_done)
            if SUB & 1:
                for h in range(16):
                    dt, ldt = h // 2, h % 2
                    p.op("tensor", lambda t, o=pG.t[ldt * 64:(ldt + 1) * 64,
                                                    dt * 64:(dt + 1) * 64],
                         l=ktok[tt][rb:rb + 32, dt * 128 + ldt * 64:dt * 128 + ldt * 64 + 64],
                         r_=vtok[tt][rb:rb + 32, h * 64:(h + 1) * 64],
                         tp=(rb, ldt * 64):
                         t.matmul(o, l, r_, start=True, stop=True, tile_position=tp),
                         inc=(p.sT if h == 15 else None))
            else:
                p.op("tensor", lambda t: t.matmul(
                    pG.t[0:64, 0:64], ktok[tt][0:32, 0:64], vtok[tt][0:32, 0:64],
                    start=True, stop=True), inc=p.sT)
            pG.ready = (p.sT, p.sT.n)
            if SUB & 2:
                p.wait("tensor", p.sT)
                for ldt in range(2):
                    if ldt == 1:
                        p.wait("tensor", p.sT)
                    for dt in range(8):
                        h = dt * 2 + ldt
                        p.op("tensor", lambda t,
                             o=poI.t[rb:rb + 32, h * 64:(h + 1) * 64],
                             l=rtl[dt][ldt * 64:(ldt + 1) * 64, c * 32:(c + 1) * 32],
                             r_=Sbf[ldt * 64:(ldt + 1) * 64, dt * 64:(dt + 1) * 64],
                             tp=(ldt * 64, rb):
                             t.matmul(o, l, r_, start=True, stop=True,
                                      tile_position=tp),
                             inc=(p.sT if dt == 7 else None))
            else:
                p.op("tensor", lambda t: t.matmul(
                    poI.t[0:64, 0:64], ktok[tt][0:32, 0:64], vtok[tt][0:32, 0:64],
                    start=True, stop=True), inc=p.sT)
            ointer_done = p.sT.n
            st.tfree(pP)
            if SUB & 4:
                p.wait("tensor", p.sT)
                for ldt in range(2):
                    if ldt == 1:
                        p.wait("tensor", p.sT)
                    for dt in range(8):
                        h = dt * 2 + ldt
                        p.op("tensor", lambda t, o=pP.t[0:32, h * 32:(h + 1) * 32],
                             l=kT[dt][ldt * 64:(ldt + 1) * 64, c * 32:(c + 1) * 32],
                             r_=rtl[dt][ldt * 64:(ldt + 1) * 64, c * 32:(c + 1) * 32],
                             tp=(ldt * 64, 0):
                             t.matmul(o, l, r_, start=True, stop=True,
                                      tile_position=tp),
                             inc=(p.sT if dt == 7 else None))
            else:
                p.op("tensor", lambda t: t.matmul(
                    pP.t[0:32, 0:32], ktok[tt][0:32, 0:32], vtok[tt][0:32, 0:32],
                    start=True, stop=True), inc=p.sT)
            pP.ready = (p.sT, p.sT.n)
            p.wait("vector", p.sT, ointer_done)
            p.op("vector", lambda v, o=Ssb[:], a=Ssb[:], b=pG.t[:, :]:
                 v.tensor_tensor(o, a, b, ALU.add), inc=p.sV)
            p.wait("vector", p.sV)
            p.op("vector", lambda v,
                 o=Ssb[:].rearrange("p (d j) -> p d j", j=64),
                 a=Ssb[:].rearrange("p (d j) -> p d j", j=64), c_=c:
                 v.tensor_tensor(o, a, Dsb[:].rearrange("p (d c2) -> p d c2", c2=64)
                                 [:, :, c_:c_ + 1].broadcast_to([128, 8, 64]),
                                 ALU.mult), inc=p.sV)
            p.wait("vector", p.sV)
            p.op("vector", lambda v: v.tensor_copy(Sbf[:], Ssb[:]), inc=p.sV)
            pG.free = (p.sV, p.sV.n)
            p.wait("vector", *pP.ready)
            p.op("vector", lambda v,
                 o=pbf[rb:rb + 32, :].rearrange("p (h q) -> p h q", q=32),
                 i=pP.t[0:32, :].rearrange("p (h q) -> p h q", q=32),
                 m=m32[:, :].unsqueeze(1).broadcast_to([32, 16, 32]):
                 v.tensor_tensor(o, i, m, ALU.mult), inc=p.sV)
            pP.free = (p.sV, p.sV.n)
            pmask_done = p.sV.n
            p.wait("tensor", p.sV, pmask_done)
            p.wait("tensor", p.sT)
            if SUB & 8:
                for h in range(16):
                    po_ = st.psE if h < 8 else st.psF
                    hc = h if h < 8 else h - 8
                    p.op("tensor", lambda t, o=po_.t[rb:rb + 32, hc * 64:(hc + 1) * 64],
                         l=pbf[rb:rb + 32, h * 32:(h + 1) * 32],
                         r_=vtok[tt][rb:rb + 32, h * 64:(h + 1) * 64],
                         tp=(rb, rb):
                         t.matmul(o, l, r_, start=True, stop=True, tile_position=tp),
                         inc=(p.sT if h == 15 else None))
            else:
                p.op("tensor", lambda t: t.matmul(
                    st.psE.t[0:64, 0:64], ktok[tt][0:32, 0:64], vtok[tt][0:32, 0:64],
                    start=True, stop=True), inc=p.sT)
        tile_done = p.sT.n
        poI.ready = st.psE.ready = st.psF.ready = (p.sT, tile_done)
        p.wait("vector", p.sT, tile_done)
        for half in range(2):
            hs = slice(half * 512, (half + 1) * 512)
            pix = st.psE if half == 0 else st.psF
            p.op("vector", lambda v,
                 o=st.scr[:].rearrange("p (h j) -> p h j", j=64),
                 a=vtok[tt][:, hs].rearrange("p (h j) -> p h j", j=64),
                 b=csb[:, tt * 16 + half * 8:tt * 16 + (half + 1) * 8]
                 .unsqueeze(2).broadcast_to([128, 8, 64]):
                 v.tensor_tensor(o, a, b, ALU.mult), inc=p.sV)
            p.wait("vector", p.sV)
            p.op("vector", lambda v, o=st.scr[:], a=poI.t[:, hs], b=st.scr[:]:
                 v.tensor_tensor(o, a, b, ALU.add), inc=p.sV)
            p.wait("vector", p.sV)
            p.op("vector", lambda v, o=osb[tt][:, hs], a=pix.t[:, 0:512], b=st.scr[:]:
                 v.tensor_tensor(o, a, b, ALU.add), inc=p.sV)
        poI.free = st.psE.free = st.psF.free = (p.sV, p.sV.n)
    state_done = p.sV.n

    if PH < 5:
        _a0_dummy_out(p, st, resid_stream)
        return
    pr = st.psA
    prb = pr.t[:].bitcast(BF16)
    for tt in range(16):
        o3 = osb[tt][:].rearrange("p (h j) -> p h j", j=64)
        p.wait("vector", p.sV)
        p.op("vector", lambda v, o=grows[:, 0:16], i=o3:
             v.reduce_sum(o, i, axis=AX), inc=p.sV)
        for half in range(2):
            hs = slice(half * 512, (half + 1) * 512)
            p.wait("scalar", p.sV)
            p.op("scalar", lambda s, o=st.scr_bf[:], i=osb[tt][:, hs]:
                 s.activation(out=o, in_=i, func=AF.Square), inc=p.sS)
            p.wait("vector", p.sS)
            p.op("vector", lambda v, o=grows[:, 16 + half * 8:24 + half * 8],
                 i=st.scr_bf[:].rearrange("p (h j) -> p h j", j=64):
                 v.reduce_sum(o, i, axis=AX), inc=p.sV)
        p.wait("vector", p.sV)
        p.op("vector", lambda v: v.tensor_scalar_mul(grows[:, 0:16], grows[:, 0:16],
                                                     1.0 / 64.0), inc=p.sV)
        p.wait("vector", p.sV)
        p.op("vector", lambda v: v.tensor_scalar_mul(grows[:, 16:32], grows[:, 16:32],
                                                     1.0 / 64.0), inc=p.sV)
        p.wait("vector", p.sV)
        p.op("vector", lambda v: v.tensor_tensor(grows[:, 32:48], grows[:, 0:16],
                                                 grows[:, 0:16], ALU.mult), inc=p.sV)
        p.wait("vector", p.sV)
        p.op("vector", lambda v: v.scalar_tensor_tensor(
            out=grows[:, 16:32], in0=grows[:, 32:48], scalar=-1.0,
            in1=grows[:, 16:32], op0=ALU.mult, op1=ALU.add), inc=p.sV)
        p.wait("vector", p.sV)
        p.op("vector", lambda v: v.tensor_scalar_add(grows[:, 16:32], grows[:, 16:32],
                                                     EPS_LN), inc=p.sV)
        p.wait("scalar", p.sV)
        p.op("scalar", lambda s: s.activation(out=grows[:, 16:32], in_=grows[:, 16:32],
                                              func=AF.Ln), inc=p.sS)
        p.wait("scalar", p.sS)
        p.op("scalar", lambda s: s.activation(out=grows[:, 16:32], in_=grows[:, 16:32],
                                              func=AF.Exp, scale=-0.5), inc=p.sS)
        p.wait("vector", p.sS)
        p.op("vector", lambda v: v.scalar_tensor_tensor(
            out=grows[:, 32:48], in0=grows[:, 0:16], scalar=-1.0,
            in1=grows[:, 16:32], op0=ALU.mult, op1=ALU.mult), inc=p.sV)
        p.wait("vector", p.sV)
        p.op("vector", lambda v, o=o3, a=o3,
             b=grows[:, 16:32].unsqueeze(2).broadcast_to([128, 16, 64]):
             v.tensor_tensor(o, a, b, ALU.mult), inc=p.sV)
        p.wait("vector", p.sV)
        p.op("vector", lambda v, o=o3, a=o3,
             b=grows[:, 32:48].unsqueeze(2).broadcast_to([128, 16, 64]):
             v.tensor_tensor(o, a, b, ALU.add), inc=p.sV)
        p.wait("vector", p.sV)
        p.op("vector", lambda v, o=o3, a=o3,
             b=lnwv[:, 0:64].unsqueeze(1).broadcast_to([128, 16, 64]):
             v.tensor_tensor(o, a, b, ALU.mult), inc=p.sV)
        p.wait("vector", p.sV)
        p.op("vector", lambda v, o=o3, a=o3,
             b=lnbv[:, 0:64].unsqueeze(1).broadcast_to([128, 16, 64]):
             v.tensor_tensor(o, a, b, ALU.add), inc=p.sV)
        ln_done = p.sV.n
        for dt in range(NDT):
            st.tfree(pr)
            p.wait("tensor", p.sV, ln_done)
            p.op("tensor", lambda t, o=prb[:, 0:128],
                 i=rT[dt][:, tt * 128:(tt + 1) * 128]:
                 t.transpose(o, i, idb[:]), inc=p.sT)
            pr.ready = (p.sT, p.sT.n)
            p.wait("vector", *pr.ready)
            p.op("vector", lambda v, o=osb[tt][:, dt * 128:(dt + 1) * 128],
                 a=osb[tt][:, dt * 128:(dt + 1) * 128], b=prb[:, 0:128]:
                 v.tensor_tensor(o, a, b, ALU.mult), inc=p.sV)
            pr.free = (p.sV, p.sV.n)
    on_done = p.sV.n
    pt2 = st.psB
    ptb2 = pt2.t[:].bitcast(BF16)
    for dt in range(NDT):
        for tt in range(16):
            st.tfree(pt2)
            if dt == 0 and tt == 0:
                p.wait("tensor", p.sV, on_done)
            p.op("tensor", lambda t, o=ptb2[:, 0:128],
                 i=osb[tt][:, dt * 128:(dt + 1) * 128]:
                 t.transpose(o, i, idb[:]), inc=p.sT)
            pt2.ready = (p.sT, p.sT.n)
            p.wait("vector", *pt2.ready)
            p.op("vector", lambda v, o=oT[dt][:, tt * 128:(tt + 1) * 128],
                 i=ptb2[:, 0:128]: v.tensor_copy(o, i), inc=p.sV)
            pt2.free = (p.sV, p.sV.n)
    oT_done = p.sV.n

    p.wait("gpsimd", p.sT, wo_can_load)
    for dt in range(NDT):
        p.dma("gpsimd", wbuf[dt][:], woT[dt * 128:(dt + 1) * 128, :])
    wo_ready = p.sW.n
    xin, yout = resid_stream
    sOut = p.sem("a0sOut")
    n_out = 0
    for dch in range(NDT):
        for tc in range(NTC):
            sl = slice(tc * 512, (tc + 1) * 512)
            pp = st.psE if tc % 2 == 0 else st.psF
            xr = xres[tc % 2]
            p.wait("sync", p.sV)
            p.wait("sync", sOut, 16 * max(0, n_out - 1))
            n_out += 1
            p.dma("sync", xr[:], xin[dch * 128:(dch + 1) * 128, sl])
            xr_ready = p.sD.n
            st.tfree(pp)
            p.wait("tensor", p.sW, wo_ready)
            if dch == 0 and tc == 0:
                p.wait("tensor", p.sV, oT_done)
            for ei in range(NDT):
                p.op("tensor", lambda t, o=pp.t[:, :],
                     l=wbuf[ei][:, dch * 128:(dch + 1) * 128],
                     r_=oT[ei][:, sl], f=(ei == 0), la=(ei == 7):
                     t.matmul(o, l, r_, start=f, stop=la),
                     inc=(p.sT if ei == 7 else None))
            pp.ready = (p.sT, p.sT.n)
            p.wait("vector", *pp.ready)
            p.wait("vector", p.sD, xr_ready)
            p.op("vector", lambda v, o=xr[:], i=pp.t[:, :], b=xr[:]:
                 v.tensor_tensor(o, i, b, ALU.add), inc=p.sV)
            pp.free = (p.sV, p.sV.n)
            p.wait("scalar", p.sV)
            p.dma("scalar", yout[dch * 128:(dch + 1) * 128, sl], xr[:], inc=sOut)
    p.wait("sync", sOut)
    return (sOut, sOut.n)


def prep_attn_a0_weights(W_r, W_k, W_v, W_w, W_o, u, tm_r, tm_k, tm_v, tm_w,
                         ln_w, ln_b):
    tmt = np.zeros((128, 32), np.float32)
    for dt in range(8):
        for ci, vec in enumerate((tm_r, tm_k, tm_v, tm_w)):
            tmt[:, 4 * dt + ci] = vec[dt * 128:(dt + 1) * 128]
    ut = np.zeros((128, 8), np.float32)
    for dt in range(8):
        ut[:, dt] = np.concatenate([u[2 * dt], u[2 * dt + 1]])
    msk = np.ones((128, 1024), BF)
    msk[:, 0::32] = 0.0
    m32 = (np.arange(32)[:, None] < np.arange(32)[None, :]).astype(BF)
    return {
        "wrT": np.ascontiguousarray(W_r.T).astype(BF),
        "wkT": np.ascontiguousarray(W_k.T).astype(BF),
        "wvT": np.ascontiguousarray(W_v.T).astype(BF),
        "wwT": np.ascontiguousarray(W_w.T).astype(BF),
        "woT": np.ascontiguousarray(W_o.T).astype(BF),
        "tmt": tmt, "ut": ut,
        "lnw64": np.tile(ln_w[None, :], (128, 1)).astype(np.float32),
        "lnb64": np.tile(ln_b[None, :], (128, 1)).astype(np.float32),
        "idbf": np.eye(128, dtype=BF),
        "mask_scan": msk, "mask32": m32,
    }


def build_attn_a0_launch():
    p = Prog()
    xT = p.inp("xT", [D, T], F32)
    ws = {n: p.inp(n, [D, D], BF16) for n in ("wrT", "wkT", "wvT", "wwT", "woT")}
    tmt = p.inp("tmt", [128, 32], F32)
    ut = p.inp("ut", [128, NDT], F32)
    lnw64 = p.inp("lnw64", [128, 64], F32)
    lnb64 = p.inp("lnb64", [128, 64], F32)
    idbf = p.inp("idbf", [128, 128], BF16)
    mask_scan = p.inp("mask_scan", [128, 1024], BF16)
    mask32 = p.inp("mask32", [32, 32], BF16)
    consts = declare_const_inputs(p)
    yT = p.outp("yT", [D, T], F32)
    stt = St(p, with_x=False)
    stt.load_consts(*consts)
    vtok_pre = [p.sb(f"a0v{i}", [128, 1024], BF16) for i in range(16)]
    xstr = [vtok_pre[i][:].bitcast(F32) for i in range(NDT)]
    xn_ready = emit_preln(stt, x_hbm=xT, xstr=xstr)
    emit_attn_a0(stt, ws["wrT"], ws["wkT"], ws["wvT"], ws["wwT"], ws["woT"],
                 tmt, ut, lnw64, lnb64, idbf, mask_scan, mask32, xn_ready,
                 resid_stream=(xT, yT), vtok=vtok_pre)
    return p.finish()


def prep_attn_a1_weights(W_qk, W_v, W_out, q_scale, q_bias, k_scale, k_bias):
    perm = []
    for hp in range(8):
        h0, h1 = 2 * hp, 2 * hp + 1
        perm += list(range(h0 * 128, h0 * 128 + 64))
        perm += list(range(h1 * 128, h1 * 128 + 64))
    qcols = []
    cols = []
    for hp in range(8):
        h0, h1 = 2 * hp, 2 * hp + 1
        cols += list(range(h0 * 128, h0 * 128 + 64))
        cols += list(range(h1 * 128, h1 * 128 + 64))
        cols += list(range(h0 * 128 + 64, h0 * 128 + 128))
        cols += list(range(h1 * 128 + 64, h1 * 128 + 128))
    wqkP = np.ascontiguousarray(W_qk[:, cols])
    kr = np.arange(128)[:, None]
    qf = np.arange(512)[None, :]
    mask4 = np.stack([((kr + off) <= qf).astype(BF) for off in (0, 128, 256, 384)])
    return {
        "wqkP": wqkP.astype(BF),
        "wv": np.ascontiguousarray(W_v).astype(BF),
        "wout": np.ascontiguousarray(W_out).astype(BF),
        "qsb": np.stack([np.concatenate([q_scale, q_scale]),
                         np.concatenate([q_bias, q_bias])], axis=1).astype(np.float32),
        "ksb": np.stack([np.concatenate([k_scale, k_scale]),
                         np.concatenate([k_bias, k_bias])], axis=1).astype(np.float32),
        "mask4": mask4,
        "ones_bf": np.ones((1, 128), BF),
    }


def build_attn_a1_launch():
    p = Prog()
    xT = p.inp("xT", [D, T], F32)
    wqkP = p.inp("wqkP", [D, 2048], BF16)
    wv = p.inp("wv", [D, D], BF16)
    wout = p.inp("wout", [D, D], BF16)
    qsb = p.inp("qsb", [128, 2], F32)
    ksb = p.inp("ksb", [128, 2], F32)
    mask4 = p.inp("mask4", [4, 128, 512], BF16)
    ones_bf = p.inp("ones_bf", [1, 128], BF16)
    consts = declare_const_inputs(p)
    yT = p.outp("yT", [D, T], F32)
    stt = St(p, with_x=False)
    stt.load_consts(*consts)
    xstr = [p.sb(f"xstr{i}", [128, 512], F32) for i in range(NDT)]
    xn_tcs = []
    emit_preln(stt, x_hbm=xT, xstr=xstr, xn_tcs=xn_tcs)
    emit_attn_a1(stt, wqkP, wv, wout, qsb, ksb, mask4, ones_bf, xn_tcs,
                 resid_stream=(xT, yT))
    return p.finish()



def _barrier(p: Prog, done):
    for e in ENGS:
        p.wait(e, done[0], done[1])


def build_fused_launch():
    p = Prog()
    xT = p.inp("xT", [D, T], F32)
    wi = {}
    for pfx, specs in (
        ("a0_", [("wrT", [D, D], BF16), ("wkT", [D, D], BF16),
                 ("wvT", [D, D], BF16),
                 ("wwT", [D, D], BF16), ("woT", [D, D], BF16), ("tmt", [128, 32], F32),
                 ("ut", [128, NDT], F32), ("lnw64", [128, 64], F32),
                 ("lnb64", [128, 64], F32), ("idbf", [128, 128], BF16),
                 ("mask_scan", [128, 1024], BF16), ("mask32", [32, 32], BF16)]),
        ("f0_", [("wgT", [D, 3072], BF16), ("wuT", [D, 3072], BF16),
                 ("wdTt", [24, 8, 128, 128], BF16)]),
        ("b0_", [("wq0", [D, D], BF16), ("wq1", [D, D], BF16),
                 ("wk0", [D, D], BF16),
                 ("wk1", [D, D], BF16), ("wv", [D, D], BF16), ("wo", [D, D], BF16),
                 ("lvec", [64, 4], F32), ("b3l", [3, 16 * 128], BF16),
                 ("b3r", [3, 512], BF16), ("dmask5", [5, 128, 512], F32),
                 ("btbl", [128, 320], F32), ("lnwP", [128, 8], F32),
                 ("lnbP", [128, 8], F32), ("ones_bf", [1, 128], BF16)]),
        ("a1_", [("wqkP", [D, 2048], BF16), ("wv", [D, D], BF16),
                 ("wout", [D, D], BF16),
                 ("qsb", [128, 2], F32), ("ksb", [128, 2], F32),
                 ("mask4", [4, 128, 512], BF16), ("ones_bf", [1, 128], BF16)]),
        ("f1_", [("rwT", [D, 4], F32), ("rngt", [128, NDT], F32),
                 ("rnbt", [128, NDT], F32), ("w1", [4, D, 2048], BF16),
                 ("lngt", [128, 64], F32), ("lnbt", [128, 64], F32),
                 ("w2t", [4, 8, 16, 128, 128], BF16), ("ongt", [128, NDT], F32),
                 ("onbt", [128, NDT], F32), ("idf32", [128, 128], F32),
                 ("ones_bf", [1, 128], BF16)]),
        ("b1_", [("wqP", [D, 512], BF16), ("wkP", [D, 512], BF16),
                 ("wv", [D, D], BF16),
                 ("woT", [D, D], BF16), ("mask4", [4, 128, 512], BF16),
                 ("ones_bf", [1, 128], BF16)]),
    ):
        for nm, shp, dt in specs:
            wi[pfx + nm] = p.inp(pfx + nm, shp, dt)
    consts = declare_const_inputs(p)
    yT = p.outp("yT", [D, T], F32)
    h = [p.nc.dram_tensor(f"hbuf{i}", [D, T], F32, kind="Internal").ap()
         for i in range(2)]
    stt = St(p, with_x=False)
    stt.load_consts(*consts)

    def stage_a0(cur, nxt):
        p.push_scope()
        vtok_pre = [p.sb(f"a0v{i}", [128, 1024], BF16) for i in range(16)]
        xstr = [vtok_pre[i][:].bitcast(F32) for i in range(NDT)]
        xn_ready = emit_preln(stt, x_hbm=cur, xstr=xstr)
        done = emit_attn_a0(stt, wi["a0_wrT"], wi["a0_wkT"], wi["a0_wvT"],
                            wi["a0_wwT"], wi["a0_woT"], wi["a0_tmt"], wi["a0_ut"],
                            wi["a0_lnw64"], wi["a0_lnb64"], wi["a0_idbf"],
                            wi["a0_mask_scan"], wi["a0_mask32"], xn_ready,
                            resid_stream=(cur, nxt), vtok=vtok_pre)
        p.pop_scope()
        return done

    def stage_f0(cur, nxt):
        p.push_scope()
        stt.X = [p.sb(f"X{i}", [128, T], F32) for i in range(NDT)]
        launch_prologue(p, stt, cur)
        xn_tcs = []
        emit_preln(stt, xn_tcs=xn_tcs)
        emit_ffn0(stt, wi["f0_wgT"], wi["f0_wuT"], wi["f0_wdTt"], xn_tcs)
        done = launch_epilogue(p, stt, nxt, (p.sV, p.sV.n))
        stt.X = None
        p.pop_scope()
        return done

    def stage_b0(cur, nxt):
        p.push_scope()
        xstr = [p.sb(f"xstr{i}", [128, 512], F32) for i in range(NDT)]
        xn_tcs = []
        emit_preln(stt, x_hbm=cur, xstr=xstr, xn_tcs=xn_tcs)
        done = emit_attn_b0(stt, wi["b0_wq0"], wi["b0_wq1"], wi["b0_wk0"],
                            wi["b0_wk1"], wi["b0_wv"], wi["b0_wo"], wi["b0_lvec"],
                            wi["b0_b3l"], wi["b0_b3r"],
                            wi["b0_dmask5"], wi["b0_btbl"], wi["b0_lnwP"],
                            wi["b0_lnbP"], wi["b0_ones_bf"], xn_tcs,
                            resid_stream=(cur, nxt))
        p.pop_scope()
        return done

    def stage_a1(cur, nxt):
        p.push_scope()
        xstr = [p.sb(f"xstr{i}", [128, 512], F32) for i in range(NDT)]
        xn_tcs = []
        emit_preln(stt, x_hbm=cur, xstr=xstr, xn_tcs=xn_tcs)
        done = emit_attn_a1(stt, wi["a1_wqkP"], wi["a1_wv"], wi["a1_wout"],
                            wi["a1_qsb"], wi["a1_ksb"], wi["a1_mask4"],
                            wi["a1_ones_bf"], xn_tcs, resid_stream=(cur, nxt))
        p.pop_scope()
        return done

    def stage_f1(cur, nxt):
        p.push_scope()
        ubf = [p.sb(f"f1u{i}", [128, 1024], BF16) for i in range(16)]
        xstr = [ubf[i][:].bitcast(F32) for i in range(NDT)]
        mut = p.sb("f1mut", [128, 16], F32)
        vart = p.sb("f1vt", [128, 16], F32)
        idfs = p.sb("f1id", [128, 128], F32)
        p.dma("sync", idfs[:], wi["f1_idf32"])
        idf_ld = p.sD.n
        pF = stt.psF

        def row_stash(tc, rows_val):
            p.wait("tensor", p.sD, idf_ld)
            p.wait("tensor", p.sV, rows_val)
            stt.tfree(pF)
            for c in range(4):
                p.op("tensor", lambda t, o=pF.t[:, c:c + 1],
                     i=stt.rows[0:1, c * 128:(c + 1) * 128]:
                     t.transpose(o, i, idfs[0:1, 0:1]))
            for c in range(4):
                p.op("tensor", lambda t, o=pF.t[:, 8 + c:9 + c],
                     i=stt.rows[0:1, 3 * 512 + c * 128:3 * 512 + (c + 1) * 128]:
                     t.transpose(o, i, idfs[0:1, 0:1]),
                     inc=(p.sT if c == 3 else None))
            tdone = p.sT.n
            pF.ready = (p.sT, tdone)
            p.wait("vector", *pF.ready)
            p.op("vector", lambda v, o=mut[:, tc * 4:(tc + 1) * 4], i=pF.t[:, 0:4]:
                 v.tensor_copy(o, i))
            p.op("vector", lambda v, o=vart[:, tc * 4:(tc + 1) * 4], i=pF.t[:, 8:12]:
                 v.tensor_copy(o, i), inc=p.sV)
            pF.free = (p.sV, p.sV.n)
            return tdone
        xn_ready = emit_preln(stt, x_hbm=cur, xstr=xstr, row_stash=row_stash)

        def x_src(tc):
            sl = slice(tc * 512, (tc + 1) * 512)

            def ld():
                p.wait("sync", p.sT)
                for dt in range(NDT):
                    p.dma("sync", xstr[dt][:, 0:512],
                          cur[dt * 128:(dt + 1) * 128, sl])
            return ld, [xstr[dt][:, 0:512] for dt in range(NDT)]

        done = emit_ffn1(stt, wi["f1_rwT"], wi["f1_rngt"], wi["f1_rnbt"],
                         wi["f1_w1"], wi["f1_lngt"], wi["f1_lnbt"], wi["f1_w2t"],
                         wi["f1_ongt"], wi["f1_onbt"], mut, vart, x_src,
                         idfs, wi["f1_ones_bf"], xn_ready,
                         resid_stream=(cur, nxt), ubf=ubf)
        p.pop_scope()
        return done

    def stage_b1(cur, nxt):
        p.push_scope()
        xstr = [p.sb(f"xstr{i}", [128, 512], F32) for i in range(NDT)]
        xn_tcs = []
        emit_preln(stt, x_hbm=cur, xstr=xstr, xn_tcs=xn_tcs)
        done = emit_attn_b1(stt, wi["b1_wqP"], wi["b1_wkP"], wi["b1_wv"],
                            wi["b1_woT"], wi["b1_mask4"], wi["b1_ones_bf"],
                            xn_tcs, resid_stream=(cur, nxt))
        p.pop_scope()
        return done

    stages = [stage_a0, stage_f0, stage_b0, stage_f0, stage_a1, stage_f1,
              stage_b1, stage_f1]
    cur = xT
    for i, fn in enumerate(stages):
        nxt = yT if i == len(stages) - 1 else h[i % 2]
        done = fn(cur, nxt)
        if i < len(stages) - 1:
            _barrier(p, done)
        cur = nxt
    return p.finish()


def kernel(**inputs):
    from concourse.bass_utils import run_bass_kernel_spmd
    g = {k: np.asarray(v) for k, v in inputs.items()}
    x = np.asarray(g["L0_attn_a_x"], np.float32)
    cst = const_inputs()
    wa0 = prep_attn_a0_weights(
        g["L0_attn_a_W_r"], g["L0_attn_a_W_k"], g["L0_attn_a_W_v"],
        g["L0_attn_a_W_w"], g["L0_attn_a_W_o"], g["L0_attn_a_u"],
        g["L0_attn_a_tm_r"], g["L0_attn_a_tm_k"], g["L0_attn_a_tm_v"],
        g["L0_attn_a_tm_w"], g["L0_attn_a_ln_w"], g["L0_attn_a_ln_b"])
    wf0 = prep_ffn0_weights(g["L0_ffn_w_gate"], g["L0_ffn_w_up"],
                            g["L0_ffn_w_down"])
    wb0 = prep_attn_b0_weights(
        g["L0_attn_b_wq0"], g["L0_attn_b_wq1"], g["L0_attn_b_wk0"],
        g["L0_attn_b_wk1"], g["L0_attn_b_wv"], g["L0_attn_b_wo"],
        g["L0_attn_b_l0"], g["L0_attn_b_l1"], g["L0_attn_b_l2"],
        g["L0_attn_b_l3"], g["L0_attn_b_ln_w"], g["L0_attn_b_ln_b"])
    wa1 = prep_attn_a1_weights(
        g["L1_attn_a_W_qk"], g["L1_attn_a_W_v"], g["L1_attn_a_W_out"],
        g["L1_attn_a_q_scale"], g["L1_attn_a_q_bias"],
        g["L1_attn_a_k_scale"], g["L1_attn_a_k_bias"])
    wf1 = prep_ffn1_weights(
        g["L1_ffn_router_w"], g["L1_ffn_rn_g"], g["L1_ffn_rn_b"],
        g["L1_ffn_w1"], g["L1_ffn_ln_g"], g["L1_ffn_ln_b"], g["L1_ffn_w2"],
        g["L1_ffn_on_g"], g["L1_ffn_on_b"])
    wb1 = prep_attn_b1_weights(g["L1_attn_b_Wq"], g["L1_attn_b_Wk"],
                               g["L1_attn_b_Wv"], g["L1_attn_b_Wo"])
    wmap = {}
    for pfx, wd in (("a0_", wa0), ("f0_", wf0), ("b0_", wb0), ("a1_", wa1),
                    ("f1_", wf1), ("b1_", wb1)):
        for k, v in wd.items():
            wmap[pfx + k] = v
    nc = build_fused_launch()
    in_maps = [{**wmap, **cst, "xT": np.ascontiguousarray(x[b].T)}
               for b in range(8)]
    res = run_bass_kernel_spmd(nc, in_maps, core_ids=list(range(8)))
    return np.stack([res.results[b]["yT"].T for b in range(8)]).astype(np.float32)


# version 21 (final)
# speedup vs baseline: 1.0103x; 1.0103x over previous
import numpy as np
import ml_dtypes
from contextlib import ExitStack
import concourse.bass as bass
import concourse.mybir as mybir
from concourse.bass_utils import run_bass_kernel_spmd

F32 = mybir.dt.float32
BF16 = mybir.dt.bfloat16
I32 = mybir.dt.int32
AF = mybir.ActivationFunctionType
ALU = mybir.AluOpType
AX = mybir.AxisListType.X
BF = ml_dtypes.bfloat16

T = 2048
D = 1024
NDT = 8
NTC = 4
H = 16
DH = 64
EPS_LN = 1e-5

ENGS = ("sync", "gpsimd", "tensor", "vector", "scalar")


class Sem:
    __slots__ = ("h", "n")

    def __init__(self, h):
        self.h = h
        self.n = 0


class Prog:

    def __init__(self):
        self.nc = bass.Bass("TRN2", target_bir_lowering=False)
        self.stack = ExitStack()
        self.scopes = []
        self.names = set()
        self.q = {e: [] for e in ENGS}
        self.sD = self.sem("sD")
        self.sW = self.sem("sW")
        self.sT = self.sem("sT")
        self.sV = self.sem("sV")
        self.sS = self.sem("sS")
        self.sG = self.sem("sG")

    def _uniq(self, name):
        n, k = name, 0
        while n in self.names:
            k += 1
            n = f"{name}_{k}"
        self.names.add(n)
        return n

    def _enter(self, cm):
        st_ = self.scopes[-1] if self.scopes else self.stack
        return st_.enter_context(cm)

    def push_scope(self):
        self.scopes.append(ExitStack())

    def pop_scope(self):
        self.scopes.pop().close()

    def sem(self, name):
        return Sem(self.stack.enter_context(self.nc.semaphore(self._uniq(name))))

    def sb(self, name, shape, dt=F32):
        return self._enter(self.nc.sbuf_tensor(self._uniq(name), shape, dt))

    def ps(self, name, shape, dt=F32):
        return self.stack.enter_context(self.nc.psum_tensor(name, shape, dt))

    def inp(self, name, shape, dt=F32):
        return self.nc.dram_tensor(name, shape, dt, kind="ExternalInput").ap()

    def outp(self, name, shape, dt=F32):
        return self.nc.dram_tensor(name, shape, dt, kind="ExternalOutput").ap()

    def op(self, eng, fn, inc=None, k=1):
        if inc is not None:
            inc.n += k
            self.q[eng].append(lambda e, fn=fn, s=inc.h, k=k: fn(e).then_inc(s, k))
        else:
            self.q[eng].append(fn)

    def wait(self, eng, sem, val=None):
        v = sem.n if val is None else val
        if v > 0:
            self.q[eng].append(lambda e, s=sem.h, v=v: e.wait_ge(s, v))

    def dma(self, queue, out, in_, inc=None):
        s = inc if inc is not None else (self.sD if queue == "sync" else self.sW)
        self.op(queue, lambda e, o=out, i=in_: e.dma_start(out=o, in_=i), inc=s, k=16)

    def finish(self):
        block = self.stack.enter_context(self.nc.Block())
        for e in ENGS:
            fns = list(self.q[e])
            if fns:
                getattr(block, e)(lambda eng, fns=fns: [f(eng) for f in fns])
        self.stack.close()
        return self.nc


class PS:
    __slots__ = ("t", "ready", "free")

    def __init__(self, t):
        self.t = t
        self.ready = None
        self.free = None


class St:

    def __init__(self, p: Prog, with_x=True):
        self.p = p
        self.X = [p.sb(f"X{i}", [128, T], F32) for i in range(NDT)] if with_x else None
        self.XN = [p.sb(f"XN{i}", [128, T], BF16) for i in range(NDT)]
        self.scr = p.sb("scr", [128, 512], F32)
        self.scr_bf = p.sb("scr_bf", [128, 512], BF16)
        self.rows = p.sb("rows", [1, 4 * 512], F32)
        self.ones_row = p.sb("ones_row", [1, 128], F32)
        self.ones_col = p.sb("ones_col", [128, 2], BF16)
        self.ind2 = p.sb("ind2", [2, 128], BF16)
        self.eps = p.sb("eps", [1, 8], F32)
        self.psA = PS(p.ps("psA", [128, 512], F32))
        self.psB = PS(p.ps("psB", [128, 512], F32))
        self.psC = PS(p.ps("psC", [128, 512], F32))
        self.psD = PS(p.ps("psD", [128, 512], F32))
        self.psE = PS(p.ps("psE", [128, 512], F32))
        self.psF = PS(p.ps("psF", [128, 512], F32))
        self.psO = PS(p.ps("psO", [128, 1024], F32))

    def load_consts(self, c_onesf, c_onesbf, c_eps, c_ind2):
        p = self.p
        p.dma("sync", self.ones_row[:], c_onesf)
        p.dma("sync", self.ones_col[:], c_onesbf)
        p.dma("sync", self.eps[:], c_eps)
        p.dma("sync", self.ind2[:], c_ind2)
        p.nc.const_aps.register(F32, EPS_LN, self.eps[:, 0:1])

    def tfree(self, ps):
        if ps.free is not None:
            self.p.wait("tensor", ps.free[0], ps.free[1])
            ps.free = None


def const_inputs():
    ind2 = np.zeros((2, 128), BF)
    ind2[0, 0:64] = 1.0
    ind2[1, 64:128] = 1.0
    return {
        "c_onesf": np.ones((1, 128), np.float32),
        "c_onesbf": np.ones((128, 2), BF),
        "c_eps": np.array([[EPS_LN, 0, float(1.0 / np.sqrt(2.0)), 0, 0, 0, 0, 0]],
                          np.float32),
        "c_ind2": ind2,
    }


def declare_const_inputs(p: Prog):
    return (p.inp("c_onesf", [1, 128], F32),
            p.inp("c_onesbf", [128, 2], BF16),
            p.inp("c_eps", [1, 8], F32),
            p.inp("c_ind2", [2, 128], BF16))



def emit_preln(st: St, xn_free=None, x_hbm=None, xstr=None, row_stash=None,
               xn_tcs=None):
    p = st.p
    XN = st.XN
    if xn_free is not None:
        p.wait("vector", xn_free[0], xn_free[1])
    if x_hbm is None:
        for dt in range(NDT):
            p.op("vector", lambda v, a=XN[dt], b=st.X[dt]: v.tensor_copy(a[:], b[:]),
                 inc=p.sV)
    xcopy_done = p.sV.n
    for tc in range(NTC):
        sl = slice(tc * 512, (tc + 1) * 512)
        if x_hbm is not None:
            p.wait("sync", p.sV)
            for dt in range(NDT):
                p.dma("sync", xstr[dt][:], x_hbm[dt * 128:(dt + 1) * 128, sl])
            xs_ready = p.sD.n
            p.wait("vector", p.sD, xs_ready)
            for dt in range(NDT):
                p.op("vector", lambda v, a=XN[dt][:, sl], b=xstr[dt][:]:
                     v.tensor_copy(a, b), inc=p.sV)
            xcopy_done = p.sV.n
            p.wait("scalar", p.sD, xs_ready)
        Xsl = (lambda dt, sl=sl: st.X[dt][:, sl]) if x_hbm is None else \
              (lambda dt: xstr[dt][:])
        psA, psB, psC, psDp = st.psA, st.psB, st.psC, st.psD
        r = st.rows
        st.tfree(psA)
        p.wait("tensor", p.sV, xcopy_done)
        for dt in range(NDT):
            p.op("tensor", lambda t, o=psA.t[0:1, :], l=st.ones_col[:, 0:1],
                 r_=XN[dt][:, sl], f=(dt == 0), la=(dt == NDT - 1):
                 t.matmul(o, l, r_, start=f, stop=la),
                 inc=(p.sT if dt == NDT - 1 else None))
        psA.ready = (p.sT, p.sT.n)
        st.tfree(psB)
        for dt in range(NDT):
            if dt > 0:
                p.wait("scalar", p.sT)
            p.op("scalar", lambda s, a=st.scr_bf[:], b=Xsl(dt):
                 s.activation(out=a, in_=b, func=AF.Square), inc=p.sS)
            p.wait("tensor", p.sS)
            p.op("tensor", lambda t, o=psB.t[0:1, :], l=st.ones_col[:, 0:1],
                 r_=st.scr_bf[:], f=(dt == 0), la=(dt == NDT - 1):
                 t.matmul(o, l, r_, start=f, stop=la), inc=p.sT)
        psB.ready = (p.sT, p.sT.n)
        def rr(k):
            return r[0:1, k * 512:(k + 1) * 512]
        p.wait("vector", *psA.ready)
        p.wait("vector", *psB.ready)
        p.op("vector", lambda v, o=rr(0), i=psA.t[0:1, :]:
             v.tensor_scalar_mul(o, i, 1.0 / D))
        p.op("vector", lambda v, o=rr(3), i=psB.t[0:1, :]:
             v.tensor_scalar_mul(o, i, 1.0 / D), inc=p.sV)
        psA.free = (p.sV, p.sV.n)
        psB.free = (p.sV, p.sV.n)
        p.op("vector", lambda v, o=rr(1), a=rr(0):
             v.tensor_tensor(o, a, a, ALU.mult))
        p.op("vector", lambda v, o=rr(3), a=rr(3), b=rr(1):
             v.tensor_tensor(o, a, b, ALU.subtract), inc=p.sV)
        if row_stash is not None:
            stash_done = row_stash(tc, p.sV.n)
            p.wait("scalar", p.sT, stash_done)
        p.wait("scalar", p.sV)
        p.op("scalar", lambda s, o=rr(3), i=rr(3):
             s.activation(out=o, in_=i, func=AF.Ln, bias=st.eps[0:1, 0:1]))
        p.op("scalar", lambda s, o=rr(1), i=rr(3):
             s.activation(out=o, in_=i, func=AF.Exp, scale=-0.5), inc=p.sS)
        p.wait("vector", p.sS)
        p.op("vector", lambda v, o=rr(2), a=rr(0), b=rr(1):
             v.scalar_tensor_tensor(out=o, in0=a, scalar=-1.0, in1=b,
                                    op0=ALU.mult, op1=ALU.mult), inc=p.sV)
        rows_ready = p.sV.n
        for ps, row in ((psC, rr(1)), (psDp, rr(2))):
            st.tfree(ps)
            p.wait("tensor", p.sV, rows_ready)
            p.op("tensor", lambda t, o=ps.t[:, :], l=st.ones_row[:], r_=row:
                 t.matmul(o, l, r_, start=True, stop=True), inc=p.sT)
            ps.ready = (p.sT, p.sT.n)
        p.wait("vector", *psC.ready)
        p.wait("vector", *psDp.ready)
        for dt in range(NDT):
            p.op("vector", lambda v, x=Xsl(dt), bc=psC.t[:, :]:
                 v.tensor_tensor(st.scr[:], x, bc, ALU.mult))
            p.op("vector", lambda v, o=XN[dt][:, sl], bc=psDp.t[:, :]:
                 v.tensor_tensor(o, st.scr[:], bc, ALU.add), inc=p.sV)
        psC.free = (p.sV, p.sV.n)
        psDp.free = (p.sV, p.sV.n)
        if xn_tcs is not None:
            xn_tcs.append(p.sV.n)
    return p.sV.n



def emit_ffn0(st: St, wgT, wuT, wdTt, xn_ready):
    xn_tc = xn_ready if isinstance(xn_ready, list) else [xn_ready] * 4
    """wgT, wuT: HBM [1024, 3072] int32 (host-transposed). wdTt: [24, 8, 128, 128] int32."""
    p = st.p
    NH = 24
    wg_sb = [p.sb(f"wg{i}", [128, 1024], BF16) for i in range(NDT)]
    wu_sb = [p.sb(f"wu{i}", [128, 1024], BF16) for i in range(NDT)]
    hid = [p.sb(f"hid{i}", [128, 1024], BF16) for i in range(NH)]
    wd_bb = [p.sb("wd", [128, NH * 128], BF16), p.sb("wd2", [128, NH * 128], BF16)]
    silu2 = [p.sb("silu_scr", [128, 512], F32), p.sb("silu_scr2", [128, 512], F32)]
    silu_stt = [0, 0]
    s = 1.0 / 127.0

    units = [(th_, g_, h_) for th_ in range(2) for g_ in range(3) for h_ in range(2)]
    w_rdy = {}
    half_free = [0, 0]

    def stream_unit(ui):
        th_, g_, h_ = units[ui]
        p.wait("gpsimd", p.sT, half_free[h_])
        csl = slice(h_ * 512, (h_ + 1) * 512)
        gsl_ = slice(g_ * 1024 + h_ * 512, g_ * 1024 + (h_ + 1) * 512)
        for dt in range(NDT):
            p.dma("gpsimd", wg_sb[dt][:, csl], wgT[dt * 128:(dt + 1) * 128, gsl_])
            p.dma("gpsimd", wu_sb[dt][:, csl], wuT[dt * 128:(dt + 1) * 128, gsl_])
        w_rdy[ui] = p.sW.n

    stream_unit(0)
    stream_unit(1)
    for th in range(2):
        for grp in range(3):
            ubase = (th * 3 + grp) * 2
            for hi in range(8):
                ht = grp * 8 + hi
                for tc2 in range(2):
                    sl = slice(th * 1024 + tc2 * 512, th * 1024 + (tc2 + 1) * 512)
                    sl2 = slice(tc2 * 512, (tc2 + 1) * 512)
                    pg = st.psA if (hi * 2 + tc2) % 2 == 0 else st.psC
                    pu = st.psB if (hi * 2 + tc2) % 2 == 0 else st.psD
                    st.tfree(pg)
                    if hi == 0 and tc2 == 0:
                        p.wait("tensor", p.sW, w_rdy[ubase])
                    if hi == 4 and tc2 == 0:
                        p.wait("tensor", p.sW, w_rdy[ubase + 1])
                    if grp == 0 and hi == 0:
                        p.wait("tensor", p.sV, xn_tc[th * 2 + tc2])
                    for dt in range(NDT):
                        p.op("tensor", lambda t, o=pg.t[:, :],
                             l=wg_sb[dt][:, hi * 128:(hi + 1) * 128],
                             r_=st.XN[dt][:, sl], f=(dt == 0), la=(dt == NDT - 1):
                             t.matmul(o, l, r_, start=f, stop=la),
                             inc=(p.sT if dt == NDT - 1 else None))
                    pg.ready = (p.sT, p.sT.n)
                    st.tfree(pu)
                    for dt in range(NDT):
                        p.op("tensor", lambda t, o=pu.t[:, :],
                             l=wu_sb[dt][:, hi * 128:(hi + 1) * 128],
                             r_=st.XN[dt][:, sl], f=(dt == 0), la=(dt == NDT - 1):
                             t.matmul(o, l, r_, start=f, stop=la),
                             inc=(p.sT if dt == NDT - 1 else None))
                    pu.ready = (p.sT, p.sT.n)
                    if tc2 == 1 and hi in (3, 7):
                        hh_ = 0 if hi == 3 else 1
                        half_free[hh_] = p.sT.n
                        if ubase + 2 + hh_ < len(units):
                            stream_unit(ubase + 2 + hh_)
                    sil_ = silu2[(hi * 2 + tc2) % 2]
                    p.wait("scalar", *pg.ready)
                    if silu_stt[(hi * 2 + tc2) % 2]:
                        p.wait("scalar", p.sV, silu_stt[(hi * 2 + tc2) % 2])
                    p.op("scalar", lambda sc, o=sil_[:], i=pg.t[:, :]:
                         sc.activation(out=o, in_=i, func=AF.Silu, scale=s), inc=p.sS)
                    pg.free = (p.sS, p.sS.n)
                    p.wait("vector", p.sS)
                    p.wait("vector", *pu.ready)
                    p.op("vector", lambda v, o=hid[ht][:, sl2], i=pu.t[:, :], sl_=sil_:
                         v.scalar_tensor_tensor(out=o, in0=i, scalar=s, in1=sl_[:],
                                                op0=ALU.mult, op1=ALU.mult), inc=p.sV)
                    pu.free = (p.sV, p.sV.n)
                    silu_stt[(hi * 2 + tc2) % 2] = p.sV.n
        hid_ready = p.sV.n
        wd_rdy = [0, 0]
        wd_free = [p.sT.n, p.sT.n]

        def wfetch(dt_):
            p.wait("gpsimd", p.sT, wd_free[dt_ % 2])
            p.dma("gpsimd", wd_bb[dt_ % 2][:].rearrange("p (h c) -> p h c", h=NH),
                  wdTt[:, dt_].rearrange("h p c -> p h c"))
            wd_rdy[dt_ % 2] = p.sW.n

        wfetch(0)
        wfetch(1)
        for dt in range(NDT):
            wd_sb = wd_bb[dt % 2]
            wd_ready = wd_rdy[dt % 2]
            for tc2 in range(2):
                sl = slice(th * 1024 + tc2 * 512, th * 1024 + (tc2 + 1) * 512)
                sl2 = slice(tc2 * 512, (tc2 + 1) * 512)
                pe = st.psE if tc2 % 2 == 0 else st.psF
                st.tfree(pe)
                p.wait("tensor", p.sW, wd_ready)
                if dt == 0 and tc2 == 0:
                    p.wait("tensor", p.sV, hid_ready)
                for ht in range(NH):
                    p.op("tensor", lambda t, o=pe.t[:, :],
                         l=wd_sb[:, ht * 128:(ht + 1) * 128], r_=hid[ht][:, sl2],
                         f=(ht == 0), la=(ht == NH - 1):
                         t.matmul(o, l, r_, start=f, stop=la),
                         inc=(p.sT if ht == NH - 1 else None))
                pe.ready = (p.sT, p.sT.n)
                p.wait("vector", *pe.ready)
                p.op("vector", lambda v, x=st.X[dt][:, sl], i=pe.t[:, :]:
                     v.scalar_tensor_tensor(out=x, in0=i, scalar=s, in1=x,
                                            op0=ALU.mult, op1=ALU.add), inc=p.sV)
                pe.free = (p.sV, p.sV.n)
            wd_free[dt % 2] = p.sT.n
            if dt + 2 < NDT:
                wfetch(dt + 2)
    return p.sV.n



def launch_prologue(p: Prog, stt: St, xT):
    qs_ = ("sync", "scalar", "gpsimd")
    for dt in range(NDT):
        p.dma(qs_[dt % 3], stt.X[dt][:], xT[dt * 128:(dt + 1) * 128, :],
              inc=p.sD)
    ld = p.sD.n
    for e in ("vector", "scalar", "tensor", "gpsimd"):
        p.wait(e, p.sD, ld)


def launch_epilogue(p: Prog, stt: St, yT, done):
    sOut = p.sem("sOut")
    qs_ = ("sync", "scalar", "gpsimd")
    for q in qs_:
        p.wait(q, done[0], done[1])
    for dt in range(NDT):
        p.dma(qs_[dt % 3], yT[dt * 128:(dt + 1) * 128, :], stt.X[dt][:], inc=sOut)
    p.wait("sync", sOut)
    return (sOut, sOut.n)


def build_ffn0_launch():
    p = Prog()
    xT = p.inp("xT", [D, T], F32)
    wgT = p.inp("wgT", [D, 3072], BF16)
    wuT = p.inp("wuT", [D, 3072], BF16)
    wdTt = p.inp("wdTt", [24, 8, 128, 128], BF16)
    consts = declare_const_inputs(p)
    yT = p.outp("yT", [D, T], F32)
    stt = St(p)
    stt.load_consts(*consts)
    launch_prologue(p, stt, xT)
    xn_tcs = []
    emit_preln(stt, xn_tcs=xn_tcs)
    emit_ffn0(stt, wgT, wuT, wdTt, xn_tcs)
    launch_epilogue(p, stt, yT, (p.sV, p.sV.n))
    return p.finish()


def prep_ffn0_weights(w_gate, w_up, w_down):
    return {
        "wgT": np.ascontiguousarray(w_gate.T).astype(BF),
        "wuT": np.ascontiguousarray(w_up.T).astype(BF),
        "wdTt": np.ascontiguousarray(
            w_down.T.reshape(24, 128, 8, 128).transpose(0, 2, 1, 3)).astype(BF),
    }



def emit_attn_a1(st: St, wqkP, wv, wout, qsb, ksb, mask4, ones_bf, xn_ready,
                 resid_stream):
    xn_tc = xn_ready if isinstance(xn_ready, list) else [xn_ready] * 4
    import os
    PH = int(os.environ.get("A1PHASE", "5"))
    NHP = int(os.environ.get("A1HP", "8"))
    NEC = int(os.environ.get("A1EC", "2"))
    NTCL = int(os.environ.get("A1TC", "4"))
    p = st.p
    qn = p.sb("a1qn", [128, T], BF16)
    kn = p.sb("a1kn", [128, T], BF16)
    vx = [p.sb(f"a1v{i}", [128, H * 65], BF16) for i in range(16)]
    et = [p.sb(f"a1e{i}", [128, 512], BF16) for i in range(16)]
    oT = st.XN
    wbuf = [p.sb(f"a1w{i}", [128, 1024], BF16) for i in range(NDT)]
    qsv = p.sb("a1qs", [128, 2], F32)
    ksv = p.sb("a1ks", [128, 2], F32)
    msk = p.sb("a1msk", [128, 4 * 512], BF16)
    sqb = p.sb("a1sq", [128, 512], BF16)
    prows = p.sb("a1pr", [2, 3 * 512], F32)
    prb = p.sb("a1prb", [2, 4 * 512], BF16)
    prb_free = [0, 0]
    idact = [0]
    ln_pend = []
    hofA = p.sb("a1hA", [128, 2], BF16)
    onb = p.sb("a1onb", [1, 128], BF16)
    recb = p.sb("a1recb", [1, 2 * 512], BF16)
    rec_prev = [0]
    oTs = [p.sb(f"a1oT{i}", [128, T], BF16) for i in range(8)]
    xres = [p.sb(f"a1xr{i}", [128, 512], F32) for i in range(2)]
    p.dma("sync", qsv[:], qsb)
    p.dma("sync", ksv[:], ksb)
    p.dma("sync", onb[:], ones_bf)
    for m in range(4):
        p.dma("sync", msk[:, m * 512:(m + 1) * 512], mask4[m])
    consts_ld = p.sD.n
    p.op("vector", lambda v: v.memset(hofA[:, :], 0.0))
    p.op("vector", lambda v: v.memset(hofA[0:64, 0:1], 1.0))
    p.op("vector", lambda v: v.memset(hofA[64:128, 1:2], 1.0), inc=p.sV)
    hof_ready = p.sV.n

    for dt in range(NDT):
        p.dma("gpsimd", wbuf[dt][:], wv[dt * 128:(dt + 1) * 128, :])
    wv_ready = p.sW.n
    for tt in range(16):
        p.op("vector", lambda v, o=vx[tt]: v.memset(
            o[:].rearrange("p (h j) -> p h j", h=H)[:, :, 64:65], 1.0),
            inc=(p.sV if tt == 15 else None))
    vones_ready = p.sV.n
    for tt in range(16):
        for ec in range(2):
            pp = st.psA if (tt * 2 + ec) % 2 == 0 else st.psB
            st.tfree(pp)
            p.wait("tensor", p.sW, wv_ready)
            if ec == 0:
                p.wait("tensor", p.sV, xn_tc[tt // 4])
            for dt in range(NDT):
                p.op("tensor", lambda t, o=pp.t[:, :],
                     l=st.XN[dt][:, tt * 128:(tt + 1) * 128],
                     r_=wbuf[dt][:, ec * 512:(ec + 1) * 512],
                     f=(dt == 0), la=(dt == NDT - 1):
                     t.matmul(o, l, r_, start=f, stop=la),
                     inc=(p.sT if dt == NDT - 1 else None))
            pp.ready = (p.sT, p.sT.n)
            p.wait("vector", *pp.ready)
            if tt == 0 and ec == 0:
                p.wait("vector", p.sV, vones_ready)
            p.op("vector", lambda v,
                 o=vx[tt][:].rearrange("p (h j) -> p h j", h=H)[:, ec * 8:(ec + 1) * 8, 0:64],
                 i=pp.t[:, :].rearrange("p (h j) -> p h j", j=64):
                 v.tensor_copy(o, i), inc=p.sV)
            pp.free = (p.sV, p.sV.n)
    v_done = p.sV.n

    for hp in range(NHP if PH >= 2 else 0):
        for ecl, dst, svec in ((0, qn, qsv), (1, kn, ksv))[:NEC]:
            ec = hp * 2 + ecl
            p.wait("gpsimd", p.sT)
            for dt in range(NDT):
                p.dma("gpsimd", wbuf[dt][:, 0:128],
                      wqkP[dt * 128:(dt + 1) * 128, ec * 128:(ec + 1) * 128])
            w_ready = p.sW.n
            for tc in range(NTCL):
                sl = slice(tc * 512, (tc + 1) * 512)
                pp = st.psA if tc % 2 == 0 else st.psB
                st.tfree(pp)
                p.wait("tensor", p.sW, w_ready)
                if hp == 0 and ecl == 0 and tc == 0:
                    p.wait("tensor", p.sV, hof_ready)
                for dt in range(NDT):
                    p.op("tensor", lambda t, o=pp.t[:, :], l=wbuf[dt][:, 0:128],
                         r_=st.XN[dt][:, sl], f=(dt == 0), la=(dt == NDT - 1):
                         t.matmul(o, l, r_, start=f, stop=la),
                         inc=(p.sT if dt == NDT - 1 else None))
                pp.ready = (p.sT, p.sT.n)
                p.wait("vector", *pp.ready)
                p.op("vector", lambda v, o=dst[:, sl], i=pp.t[:, :]:
                     v.tensor_copy(o, i), inc=p.sV)
                raw_done = p.sV.n
                p.wait("scalar", p.sV, raw_done)
                p.wait("scalar", p.sT)
                p.op("scalar", lambda s, o=sqb[:], i=pp.t[:, :]:
                     s.activation(out=o, in_=i, func=AF.Square), inc=p.sS)
                pp.free = (p.sS, p.sS.n)
                pc, pd = st.psC, st.psD
                st.tfree(pc)
                p.wait("tensor", p.sV, raw_done)
                p.op("tensor", lambda t, o=pc.t[0:2, :], l=hofA[:, :], r_=dst[:, sl]:
                     t.matmul(o, l, r_, start=True, stop=True), inc=p.sT)
                pc.ready = (p.sT, p.sT.n)
                st.tfree(pd)
                p.wait("tensor", p.sS)
                p.op("tensor", lambda t, o=pd.t[0:2, :], l=hofA[:, :], r_=sqb[:]:
                     t.matmul(o, l, r_, start=True, stop=True), inc=p.sT)
                pd.ready = (p.sT, p.sT.n)
                mu = prows[:, 0:512]
                v2 = prows[:, 512:1024]
                Bv = prows[:, 1024:1536]
                p.wait("vector", *pc.ready)
                p.wait("vector", *pd.ready)
                p.op("vector", lambda v, o=mu, i=pc.t[0:2, :]:
                     v.tensor_scalar_mul(o, i, 1.0 / 64.0))
                p.op("vector", lambda v, o=v2, i=pd.t[0:2, :]:
                     v.tensor_scalar_mul(o, i, 1.0 / 64.0), inc=p.sV)
                pc.free = pd.free = (p.sV, p.sV.n)
                p.op("vector", lambda v, o=Bv, a=mu: v.tensor_tensor(o, a, a, ALU.mult))
                p.op("vector", lambda v, o=v2, a=Bv, b=v2: v.scalar_tensor_tensor(
                     out=o, in0=a, scalar=-1.0, in1=b, op0=ALU.mult, op1=ALU.add))
                p.op("vector", lambda v, o=v2, a=v2:
                     v.tensor_scalar_add(o, a, 1e-6), inc=p.sV)
                p.wait("scalar", p.sV)
                p.op("scalar", lambda s, o=v2, i=v2: s.activation(out=o, in_=i, func=AF.Ln))
                p.op("scalar", lambda s, o=v2, i=v2:
                     s.activation(out=o, in_=i, func=AF.Exp, scale=-0.5), inc=p.sS)
                par = (hp * 2 * NTCL + ecl * NTCL + tc) % 2
                pb0_ = par * 1024
                p.wait("vector", p.sT, prb_free[par])
                p.wait("vector", p.sS)
                p.op("vector", lambda v, o=Bv, a=mu, b=v2: v.scalar_tensor_tensor(
                     out=o, in0=a, scalar=-1.0, in1=b, op0=ALU.mult, op1=ALU.mult))
                p.op("vector", lambda v, o=prb[:, pb0_:pb0_ + 512], i=v2:
                     v.tensor_copy(o, i))
                p.op("vector", lambda v, o=prb[:, pb0_ + 512:pb0_ + 1024], i=Bv:
                     v.tensor_copy(o, i), inc=p.sV)
                prows_ready = p.sV.n

                def ln_tail(dst_=dst, sl_=sl, svec_=svec, par_=par,
                            prv_=prows_ready):
                    pb_ = par_ * 1024
                    pe, pf = st.psE, st.psF
                    for ps, rbase in ((pe, pb_), (pf, pb_ + 512)):
                        st.tfree(ps)
                        p.wait("tensor", p.sV, prv_)
                        p.op("tensor", lambda t, o=ps.t[:, :], l=st.ind2[:, :],
                             r_=prb[:, rbase:rbase + 512]:
                             t.matmul(o, l, r_, start=True, stop=True), inc=p.sT)
                        ps.ready = (p.sT, p.sT.n)
                    prb_free[par_] = p.sT.n
                    p.wait("vector", *pe.ready)
                    p.wait("vector", *pf.ready)
                    if idact[0]:
                        p.wait("vector", p.sS, idact[0])
                    p.op("vector", lambda v, o=st.scr[:], a=dst_[:, sl_],
                         bc=pe.t[:, :]:
                         v.tensor_tensor(o, a, bc, ALU.mult), inc=p.sV)
                    pe.free = (p.sV, p.sV.n)
                    p.op("vector", lambda v, o=st.scr[:], a=st.scr[:], bc=pf.t[:, :]:
                         v.tensor_tensor(o, a, bc, ALU.add), inc=p.sV)
                    pf.free = (p.sV, p.sV.n)
                    p.wait("scalar", p.sV)
                    p.op("scalar", lambda s, o=dst_[:, sl_], i=st.scr[:], sv=svec_:
                         s.activation(out=o, in_=i, func=AF.Identity,
                                      scale=sv[:, 0:1], bias=sv[:, 1:2]), inc=p.sS)
                    idact[0] = p.sS.n

                ln_pend.append(ln_tail)
                if len(ln_pend) >= 2:
                    ln_pend.pop(0)()
        while ln_pend:
            ln_pend.pop(0)()
        qk_done = p.sS.n
        if hp == NHP - 1:
            wout_can_load = p.sT.n
        if hp == 0:
            p.wait("scalar", p.sD, consts_ld)
            p.wait("vector", p.sD, consts_ld)
        for qc in range(NTC if PH >= 3 else 0):
            qsl = slice(qc * 512, (qc + 1) * 512)
            nkt = 4 * qc + 4
            for hh, po in ((0, st.psE), (1, st.psF)):
                hs = slice(hh * 64, (hh + 1) * 64)
                tpos = (64 * hh, 0)
                for kt in range(nkt):
                    pp = [st.psA, st.psB, st.psC, st.psD][kt % 4]
                    st.tfree(pp)
                    if kt == 0 and hh == 0 and qc == 0:
                        p.wait("tensor", p.sS, qk_done)
                    p.op("tensor", lambda t, o=pp.t[:, :], l=kn[hs, kt * 128:(kt + 1) * 128],
                         r_=qn[hs, qsl], tp=tpos:
                         t.matmul(o, l, r_, start=True, stop=True, tile_position=tp),
                         inc=p.sT)
                    pp.ready = (p.sT, p.sT.n)
                    p.wait("scalar", *pp.ready)
                    p.op("scalar", lambda s, o=et[kt][:], i=pp.t[:, :]:
                         s.activation(out=o, in_=i, func=AF.Exp, scale=1.0 / 64.0),
                         inc=p.sS)
                    pp.free = (p.sS, p.sS.n)
                p.wait("vector", p.sS)
                for kt in range(4 * qc, nkt):
                    mi = (kt * 128 - qc * 512) // 128
                    p.op("vector", lambda v, o=et[kt][:], m=msk[:, mi * 512:(mi + 1) * 512]:
                         v.tensor_tensor(o, o, m, ALU.mult),
                         inc=(p.sV if kt == nkt - 1 else None))
                mask_done = p.sV.n
                if PH < 4:
                    continue
                h = hp * 2 + hh
                st.tfree(po)
                p.wait("tensor", p.sV, mask_done)
                if hp == 0 and qc == 0 and hh == 0:
                    p.wait("tensor", p.sV, v_done)
                for kt in range(nkt):
                    p.op("tensor", lambda t, o=po.t[0:65, :],
                         l=vx[kt][:, h * 65:(h + 1) * 65], r_=et[kt][:],
                         f=(kt == 0), la=(kt == nkt - 1):
                         t.matmul(o, l, r_, start=f, stop=la),
                         inc=(p.sT if kt == nkt - 1 else None))
                po.ready = (p.sT, p.sT.n)
            if PH < 4:
                continue
            p.wait("vector", *st.psE.ready)
            p.wait("vector", *st.psF.ready)
            if rec_prev[0]:
                p.wait("vector", p.sS, rec_prev[0])
            for hh, po in ((0, st.psE), (1, st.psF)):
                p.op("vector", lambda v, o=st.rows[0:1, hh * 512:(hh + 1) * 512],
                     i=po.t[64:65, :]: v.tensor_copy(o, i))
                p.op("vector", lambda v, o=oTs[hp][hh * 64:(hh + 1) * 64, qsl],
                     i=po.t[0:64, :]: v.tensor_copy(o, i), inc=p.sV)
            recs_done = p.sV.n
            p.wait("scalar", p.sV, recs_done)
            p.op("scalar", lambda s, o=st.rows[0:1, 0:1024], i=st.rows[0:1, 0:1024]:
                 s.activation(out=o, in_=i, func=AF.Ln))
            p.op("scalar", lambda s, o=recb[0:1, 0:1024], i=st.rows[0:1, 0:1024]:
                 s.activation(out=o, in_=i, func=AF.Exp, scale=-1.0), inc=p.sS)
            rec_prev[0] = p.sS.n
            st.psE.free = st.psF.free = (p.sV, recs_done)
            pr = st.psO
            st.tfree(pr)
            p.wait("tensor", p.sS, rec_prev[0])
            p.op("tensor", lambda t, o=pr.t[0:64, 0:512], l=onb[:, 0:64],
                 r_=recb[0:1, 0:512]: t.matmul(o, l, r_, start=True, stop=True))
            p.op("tensor", lambda t, o=pr.t[64:128, 0:512], l=onb[:, 0:64],
                 r_=recb[0:1, 512:1024]: t.matmul(o, l, r_, start=True, stop=True),
                 inc=p.sT)
            pr.ready = (p.sT, p.sT.n)
            p.wait("vector", *pr.ready)
            p.op("vector", lambda v, o=oTs[hp][:, qsl], bc=pr.t[:, 0:512]:
                 v.tensor_tensor(o, o, bc, ALU.mult), inc=p.sV)
            pr.free = (p.sV, p.sV.n)
    att_done = p.sV.n

    if PH < 5:
        sOut0 = p.sem("a1sOut0")
        p.wait("sync", p.sV)
        p.wait("sync", p.sS)
        for dt in range(NDT):
            p.dma("sync", resid_stream[1][dt * 128:(dt + 1) * 128, 0:512],
                  st.scr[:], inc=sOut0)
        p.wait("sync", sOut0)
        return
    p.wait("gpsimd", p.sT, wout_can_load)
    for dt in range(NDT):
        p.dma("gpsimd", wbuf[dt][:], wout[dt * 128:(dt + 1) * 128, :])
    wo_ready = p.sW.n
    xin, yout = resid_stream
    sOut = p.sem("a1sOut")
    n_out = 0
    for dch in range(NDT):
        for tc in range(NTC):
            sl = slice(tc * 512, (tc + 1) * 512)
            pp = st.psE if tc % 2 == 0 else st.psF
            xr = xres[tc % 2]
            p.wait("sync", p.sV)
            p.wait("sync", sOut, 16 * max(0, n_out - 1))
            n_out += 1
            p.dma("sync", xr[:], xin[dch * 128:(dch + 1) * 128, sl])
            xr_ready = p.sD.n
            st.tfree(pp)
            p.wait("tensor", p.sW, wo_ready)
            if dch == 0 and tc == 0:
                p.wait("tensor", p.sV, att_done)
            for ei in range(8):
                p.op("tensor", lambda t, o=pp.t[:, :],
                     l=wbuf[ei][:, dch * 128:(dch + 1) * 128],
                     r_=oTs[ei][:, sl], f=(ei == 0), la=(ei == 7):
                     t.matmul(o, l, r_, start=f, stop=la),
                     inc=(p.sT if ei == 7 else None))
            pp.ready = (p.sT, p.sT.n)
            p.wait("vector", *pp.ready)
            p.wait("vector", p.sD, xr_ready)
            p.op("vector", lambda v, o=xr[:], i=pp.t[:, :]:
                 v.scalar_tensor_tensor(out=o, in0=i, scalar=1.0 / 1024.0, in1=o,
                                        op0=ALU.mult, op1=ALU.add), inc=p.sV)
            pp.free = (p.sV, p.sV.n)
            p.wait("scalar", p.sV)
            p.dma("scalar", yout[dch * 128:(dch + 1) * 128, sl], xr[:], inc=sOut)
    p.wait("sync", sOut)
    return (sOut, sOut.n)



def emit_attn_b0(st: St, wq0, wq1, wk0, wk1, wv, wo, lvec, b3l, b3r, dmask5,
                 btbl, lnwP, lnbP, ones_bf, xn_ready, resid_stream):
    xn_tc = xn_ready if isinstance(xn_ready, list) else [xn_ready] * 4
    p = st.p
    qs = [p.sb(f"b0q{i}", [128, T], BF16) for i in range(2)]
    ks = [p.sb(f"b0k{i}", [128, T], BF16) for i in range(2)]
    vx = [p.sb(f"b0v{i}", [128, H * 65], BF16) for i in range(16)]
    oTs = [p.sb(f"b0oT{i}", [128, T], BF16) for i in range(8)]
    wbuf = [p.sb(f"b0w{i}", [128, 2048], BF16) for i in range(NDT)]
    et = [wbuf[i // 4][:, (i % 4) * 512:(i % 4 + 1) * 512] for i in range(32)]
    dmk = p.sb("b0dmk", [128, 4 * 512], F32)
    btb = p.sb("b0btb", [128, 320], F32)
    lnw = p.sb("b0lnw", [128, 8], F32)
    lnb = p.sb("b0lnb", [128, 8], F32)
    lvs = p.sb("b0lv", [64, 4], F32)
    onb = p.sb("b0onb", [1, 128], BF16)
    upair = p.sb("b0up", [128, 512], BF16)
    o1b = p.sb("b0o1b", [64, 512], BF16)
    rowsb = p.sb("b0rb", [1, 512], BF16)
    prows = p.sb("b0pr", [2, 3 * 512], F32)
    prb = p.sb("b0prb", [2, 2 * 512], BF16)
    sqb = p.sb("b0sq", [128, 512], BF16)
    hofA = p.sb("b0hA", [128, 2], BF16)
    lamv = p.sb("b0lam", [1, 2], F32)
    xres = [p.sb(f"b0xr{i}", [128, 512], F32) for i in range(2)]
    scrS = [st.scr, p.sb("b0scr2", [128, 512], F32)]
    scrU = p.sb("b0scr3", [128, 512], F32)
    zrow = st.rows
    exp_done = [0, 0]
    b3ls = p.sb("b0b3l", [67, 16 * 128], BF16)
    b3rs = p.sb("b0b3r", [67, 512], BF16)
    p.dma("sync", b3ls[0:3, :], b3l)
    p.dma("sync", b3ls[64:67, :], b3l)
    p.dma("sync", b3rs[0:3, :], b3r)
    p.dma("sync", b3rs[64:67, :], b3r)
    p.dma("sync", dmk[:].rearrange("p (m n) -> p m n", m=4),
          dmask5[1:5].rearrange("m p n -> p m n"))
    p.dma("sync", btb[:], btbl)
    p.dma("sync", lnw[:], lnwP)
    p.dma("sync", lnb[:], lnbP)
    p.dma("sync", lvs[:], lvec)
    p.dma("sync", onb[:], ones_bf)
    consts_ld = p.sD.n
    p.op("vector", lambda v: v.memset(hofA[:, :], 0.0))
    p.op("vector", lambda v: v.memset(hofA[0:64, 0:1], 1.0))
    p.op("vector", lambda v: v.memset(hofA[64:128, 1:2], 1.0), inc=p.sV)
    hof_ready = p.sV.n

    pl = st.psA
    st.tfree(pl)
    p.wait("tensor", p.sD, consts_ld)
    p.op("tensor", lambda t: t.matmul(pl.t[0:1, 0:1], lvs[:, 0:1], lvs[:, 1:2],
                                      start=True, stop=True))
    p.op("tensor", lambda t: t.matmul(pl.t[0:1, 1:2], lvs[:, 2:3], lvs[:, 3:4],
                                      start=True, stop=True), inc=p.sT)
    pl.ready = (p.sT, p.sT.n)
    p.wait("scalar", *pl.ready)
    p.op("scalar", lambda s: s.activation(out=lamv[0:1, :], in_=pl.t[0:1, 0:2],
                                          func=AF.Exp), inc=p.sS)
    pl.free = (p.sS, p.sS.n)
    p.wait("vector", p.sS)
    p.op("vector", lambda v: v.tensor_tensor(lamv[0:1, 0:1], lamv[0:1, 0:1],
                                             lamv[0:1, 1:2], ALU.add), inc=p.sV)
    p.wait("vector", p.sV)
    p.op("vector", lambda v: v.tensor_scalar_mul(lamv[0:1, 0:1], lamv[0:1, 0:1], -1.0),
         inc=p.sV)
    p.wait("vector", p.sV)
    p.op("vector", lambda v: v.tensor_scalar_add(lamv[0:1, 0:1], lamv[0:1, 0:1], 1.6),
         inc=p.sV)
    lam_ready = p.sV.n

    for dt in range(NDT):
        p.dma("gpsimd", wbuf[dt][:, 0:1024], wv[dt * 128:(dt + 1) * 128, :])
    wv_ready = p.sW.n
    for tt in range(16):
        p.op("vector", lambda v, o=vx[tt]: v.memset(
            o[:].rearrange("p (h j) -> p h j", h=H)[:, :, 64:65], 1.0),
            inc=(p.sV if tt == 15 else None))
    vones_ready = p.sV.n
    for tt in range(16):
        for ec in range(2):
            pp = st.psA if (tt * 2 + ec) % 2 == 0 else st.psB
            st.tfree(pp)
            p.wait("tensor", p.sW, wv_ready)
            if ec == 0:
                p.wait("tensor", p.sV, xn_tc[tt // 4])
            for dt in range(NDT):
                p.op("tensor", lambda t, o=pp.t[:, :],
                     l=st.XN[dt][:, tt * 128:(tt + 1) * 128],
                     r_=wbuf[dt][:, ec * 512:(ec + 1) * 512],
                     f=(dt == 0), la=(dt == NDT - 1):
                     t.matmul(o, l, r_, start=f, stop=la),
                     inc=(p.sT if dt == NDT - 1 else None))
            pp.ready = (p.sT, p.sT.n)
            p.wait("vector", *pp.ready)
            if tt == 0 and ec == 0:
                p.wait("vector", p.sV, vones_ready)
            p.op("vector", lambda v,
                 o=vx[tt][:].rearrange("p (h j) -> p h j", h=H)[:, ec * 8:(ec + 1) * 8, 0:64],
                 i=pp.t[:, :].rearrange("p (h j) -> p h j", j=64):
                 v.tensor_copy(o, i), inc=p.sV)
            pp.free = (p.sV, p.sV.n)
    v_done = p.sV.n

    slope = [2.0 ** (-8.0 * (h + 1) / H) for h in range(H)]
    for hp in range(8):
        for wi, (wsrc, dst, scl) in enumerate(((wq0, qs[0], 0.125), (wq1, qs[1], 0.125),
                                               (wk0, ks[0], 1.0), (wk1, ks[1], 1.0))):
            p.wait("gpsimd", p.sT)
            for dt in range(NDT):
                p.dma("gpsimd", wbuf[dt][:, 0:128],
                      wsrc[dt * 128:(dt + 1) * 128, hp * 128:(hp + 1) * 128])
            w_ready = p.sW.n
            for tc in range(NTC):
                sl = slice(tc * 512, (tc + 1) * 512)
                pp = st.psA if tc % 2 == 0 else st.psB
                st.tfree(pp)
                p.wait("tensor", p.sW, w_ready)
                for dt in range(NDT):
                    p.op("tensor", lambda t, o=pp.t[:, :], l=wbuf[dt][:, 0:128],
                         r_=st.XN[dt][:, sl], f=(dt == 0), la=(dt == NDT - 1):
                         t.matmul(o, l, r_, start=f, stop=la),
                         inc=(p.sT if dt == NDT - 1 else None))
                pp.ready = (p.sT, p.sT.n)
                p.wait("vector", *pp.ready)
                p.op("vector", lambda v, o=dst[:, sl], i=pp.t[:, :], s_=scl:
                     v.tensor_scalar_mul(o, i, s_), inc=p.sV)
                pp.free = (p.sV, p.sV.n)
        proj_done = p.sV.n
        for qc in range(NTC):
            qsl = slice(qc * 512, (qc + 1) * 512)
            nkt = 4 * qc + 4
            def _hdr(hh):
                hs = slice(hh * 64, (hh + 1) * 64)
                tpos = (64 * hh, 0)
                h = hp * 2 + hh
                return hs, tpos, h

            def S_blk(hh):
                hs, tpos, h = _hdr(hh)
                for s_i in range(2):
                    for kt in range(nkt):
                        if (qc * 512 - (kt * 128 + 127)) * slope[h] > 60.0:
                            continue
                        pp = [st.psA, st.psB, st.psC, st.psD][(s_i * nkt + kt) % 4]
                        st.tfree(pp)
                        if hp == 0 and qc == 0 and hh == 0 and s_i == 0 and kt == 0:
                            p.wait("tensor", p.sV, proj_done)
                            p.wait("tensor", p.sD, consts_ld)
                        mi = 0 if kt < 4 * qc else 1 + (kt - 4 * qc)
                        bcol = h * 20 + (kt - 4 * qc + 16)
                        if mi == 0:
                            p.op("tensor", lambda t, o=pp.t[:, :],
                                 l=ks[s_i][hs, kt * 128:(kt + 1) * 128],
                                 r_=qs[s_i][hs, qsl], tp=tpos:
                                 t.matmul(o, l, r_, start=True, stop=False,
                                          tile_position=tp))
                            p.op("tensor", lambda t, o=pp.t[:, :],
                                 l=b3ls[64 * hh:64 * hh + 3, h * 128:(h + 1) * 128],
                                 r_=b3rs[64 * hh:64 * hh + 3, :], tp=tpos:
                                 t.matmul(o, l, r_, start=False, stop=True,
                                          tile_position=tp), inc=p.sT)
                            pp.ready = (p.sT, p.sT.n)
                            p.wait("scalar", *pp.ready)
                            p.op("scalar", lambda s, o=et[s_i * 16 + kt][:],
                                 i=pp.t[:, :], b=btb[:, bcol:bcol + 1]:
                                 s.activation(out=o, in_=i, func=AF.Exp, bias=b),
                                 inc=p.sS)
                            pp.free = (p.sS, p.sS.n)
                            continue
                        p.op("tensor", lambda t, o=pp.t[:, :],
                             l=ks[s_i][hs, kt * 128:(kt + 1) * 128],
                             r_=qs[s_i][hs, qsl], tp=tpos:
                             t.matmul(o, l, r_, start=True, stop=True, tile_position=tp),
                             inc=p.sT)
                        pp.ready = (p.sT, p.sT.n)
                        pb_ = (s_i * nkt + kt) % 2
                        scr_ = scrS[pb_]
                        p.wait("vector", *pp.ready)
                        if exp_done[pb_]:
                            p.wait("vector", p.sS, exp_done[pb_])
                        p.op("vector", lambda v, o=scr_[:],
                             d_=dmk[:, (mi - 1) * 512:mi * 512],
                             i=pp.t[:, :], s_=slope[h]:
                             v.scalar_tensor_tensor(out=o, in0=d_, scalar=s_, in1=i,
                                                    op0=ALU.mult, op1=ALU.add), inc=p.sV)
                        pp.free = (p.sV, p.sV.n)
                        p.wait("scalar", p.sV)
                        p.op("scalar", lambda s, o=et[s_i * 16 + kt][:], i=scr_[:],
                             b=btb[:, bcol:bcol + 1]:
                             s.activation(out=o, in_=i, func=AF.Exp, bias=b), inc=p.sS)
                        exp_done[pb_] = p.sS.n

            def AV_blk(hh):
                hs, tpos, h = _hdr(hh)
                kts = [kt for kt in range(nkt)
                       if (qc * 512 - (kt * 128 + 127)) * slope[h] <= 60.0]
                for s_i, po in ((0, st.psE), (1, st.psF)):
                    st.tfree(po)
                    p.wait("tensor", p.sS)
                    if hp == 0 and qc == 0 and hh == 0 and s_i == 0:
                        p.wait("tensor", p.sV, v_done)
                    for ki, kt in enumerate(kts):
                        p.op("tensor", lambda t, o=po.t[0:65, :],
                             l=vx[kt][:, h * 65:(h + 1) * 65], r_=et[s_i * 16 + kt][:],
                             f=(ki == 0), la=(ki == len(kts) - 1):
                             t.matmul(o, l, r_, start=f, stop=la),
                             inc=(p.sT if ki == len(kts) - 1 else None))
                    po.ready = (p.sT, p.sT.n)

            def U_blk(hh):
                hs, tpos, h = _hdr(hh)
                p.wait("vector", *st.psE.ready)
                p.wait("vector", *st.psF.ready)
                if hp == 0 and qc == 0 and hh == 0:
                    p.wait("vector", p.sV, lam_ready)
                p.op("vector", lambda v, o=o1b[:], i=st.psF.t[0:64, :]:
                     v.tensor_copy(o, i))
                p.op("vector", lambda v, o=zrow[0:1, 0:512], i=st.psF.t[64:65, :]:
                     v.tensor_copy(o, i), inc=p.sV)
                p.wait("scalar", p.sV)
                p.op("scalar", lambda s, o=zrow[0:1, 0:512], i=zrow[0:1, 0:512]:
                     s.activation(out=o, in_=i, func=AF.Ln))
                p.op("scalar", lambda s, o=zrow[0:1, 0:512], i=zrow[0:1, 0:512]:
                     s.activation(out=o, in_=i, func=AF.Exp, scale=-1.0), inc=p.sS)
                p.wait("vector", p.sS)
                p.op("vector", lambda v, o=zrow[0:1, 0:512], i=st.psE.t[64:65, :]:
                     v.scalar_tensor_tensor(out=o, in0=i, scalar=lamv[0:1, 0:1],
                                            in1=zrow[0:1, 0:512], op0=ALU.mult,
                                            op1=ALU.mult))
                p.op("vector", lambda v, o=rowsb[0:1, 0:512], i=zrow[0:1, 0:512]:
                     v.tensor_copy(o, i), inc=p.sV)
                crow_done = p.sV.n
                st.psF.free = (p.sV, crow_done)
                pc = st.psC
                st.tfree(pc)
                p.wait("tensor", p.sV, crow_done)
                p.op("tensor", lambda t, o=pc.t[0:64, :], l=onb[:, 0:64],
                     r_=rowsb[0:1, 0:512]:
                     t.matmul(o, l, r_, start=True, stop=True), inc=p.sT)
                pc.ready = (p.sT, p.sT.n)
                p.wait("vector", *pc.ready)
                p.op("vector", lambda v, o=scrU[0:64, :], a=o1b[:], bc=pc.t[0:64, :]:
                     v.tensor_tensor(o, a, bc, ALU.mult), inc=p.sV)
                pc.free = (p.sV, p.sV.n)
                p.op("vector", lambda v, o=upair[hs, :], a=st.psE.t[0:64, :],
                     b=scrU[0:64, :]:
                     v.tensor_tensor(o, a, b, ALU.subtract), inc=p.sV)
                st.psE.free = (p.sV, p.sV.n)

            S_blk(0)
            AV_blk(0)
            S_blk(1)
            U_blk(0)
            AV_blk(1)
            U_blk(1)
            u_done = p.sV.n
            pc, pd = st.psC, st.psD
            st.tfree(pc)
            p.wait("tensor", p.sV, u_done)
            p.op("tensor", lambda t, o=pc.t[0:2, :], l=hofA[:, :], r_=upair[:]:
                 t.matmul(o, l, r_, start=True, stop=True), inc=p.sT)
            pc.ready = (p.sT, p.sT.n)
            p.wait("scalar", p.sV, u_done)
            p.wait("scalar", p.sT)
            p.op("scalar", lambda s, o=sqb[:], i=upair[:]:
                 s.activation(out=o, in_=i, func=AF.Square), inc=p.sS)
            st.tfree(pd)
            p.wait("tensor", p.sS)
            p.op("tensor", lambda t, o=pd.t[0:2, :], l=hofA[:, :], r_=sqb[:]:
                 t.matmul(o, l, r_, start=True, stop=True), inc=p.sT)
            pd.ready = (p.sT, p.sT.n)
            mu = prows[:, 0:512]
            v2 = prows[:, 512:1024]
            Bv = prows[:, 1024:1536]
            p.wait("vector", *pc.ready)
            p.wait("vector", *pd.ready)
            p.op("vector", lambda v, o=mu, i=pc.t[0:2, :]:
                 v.tensor_scalar_mul(o, i, 1.0 / 64.0))
            p.op("vector", lambda v, o=v2, i=pd.t[0:2, :]:
                 v.tensor_scalar_mul(o, i, 1.0 / 64.0), inc=p.sV)
            pc.free = pd.free = (p.sV, p.sV.n)
            p.op("vector", lambda v, o=Bv, a=mu: v.tensor_tensor(o, a, a, ALU.mult))
            p.op("vector", lambda v, o=v2, a=Bv, b=v2: v.scalar_tensor_tensor(
                 out=o, in0=a, scalar=-1.0, in1=b, op0=ALU.mult, op1=ALU.add))
            p.op("vector", lambda v, o=v2, a=v2:
                 v.tensor_scalar_add(o, a, 1e-5), inc=p.sV)
            p.wait("scalar", p.sV)
            p.op("scalar", lambda s, o=v2, i=v2: s.activation(out=o, in_=i, func=AF.Ln))
            p.op("scalar", lambda s, o=v2, i=v2:
                 s.activation(out=o, in_=i, func=AF.Exp, scale=-0.5), inc=p.sS)
            p.wait("vector", p.sS)
            p.op("vector", lambda v, o=Bv, a=mu, b=v2: v.scalar_tensor_tensor(
                 out=o, in0=a, scalar=-1.0, in1=b, op0=ALU.mult, op1=ALU.mult))
            p.op("vector", lambda v, o=prb[:, 0:512], i=v2: v.tensor_copy(o, i))
            p.op("vector", lambda v, o=prb[:, 512:1024], i=Bv: v.tensor_copy(o, i),
                 inc=p.sV)
            prows_ready = p.sV.n
            pe, pf = st.psE, st.psF
            for ps, rbase in ((pe, 0), (pf, 512)):
                st.tfree(ps)
                p.wait("tensor", p.sV, prows_ready)
                p.op("tensor", lambda t, o=ps.t[:, :], l=st.ind2[:, :],
                     r_=prb[:, rbase:rbase + 512]:
                     t.matmul(o, l, r_, start=True, stop=True), inc=p.sT)
                ps.ready = (p.sT, p.sT.n)
            p.wait("vector", *pe.ready)
            p.wait("vector", *pf.ready)
            p.op("vector", lambda v, o=scrU[:], a=upair[:], bc=pe.t[:, :]:
                 v.tensor_tensor(o, a, bc, ALU.mult), inc=p.sV)
            pe.free = (p.sV, p.sV.n)
            p.op("vector", lambda v, o=scrU[:], a=scrU[:], bc=pf.t[:, :]:
                 v.tensor_tensor(o, a, bc, ALU.add), inc=p.sV)
            pf.free = (p.sV, p.sV.n)
            p.wait("scalar", p.sV)
            p.op("scalar", lambda s, o=oTs[hp][:, qsl], i=scrU[:], hp_=hp:
                 s.activation(out=o, in_=i, func=AF.Identity,
                              scale=lnw[:, hp_:hp_ + 1], bias=lnb[:, hp_:hp_ + 1]),
                 inc=p.sS)
    att_done = p.sS.n

    p.wait("gpsimd", p.sT)
    for dt in range(NDT):
        p.dma("gpsimd", wbuf[dt][:, 0:1024], wo[dt * 128:(dt + 1) * 128, :])
    wo_ready = p.sW.n
    xin, yout = resid_stream
    sOut = p.sem("b0sOut")
    n_out = 0
    for dch in range(NDT):
        for tc in range(NTC):
            sl = slice(tc * 512, (tc + 1) * 512)
            pp = st.psE if tc % 2 == 0 else st.psF
            xr = xres[tc % 2]
            p.wait("sync", p.sV)
            p.wait("sync", sOut, 16 * max(0, n_out - 1))
            n_out += 1
            p.dma("sync", xr[:], xin[dch * 128:(dch + 1) * 128, sl])
            xr_ready = p.sD.n
            st.tfree(pp)
            p.wait("tensor", p.sW, wo_ready)
            if dch == 0 and tc == 0:
                p.wait("tensor", p.sS, att_done)
            for ei in range(8):
                p.op("tensor", lambda t, o=pp.t[:, :],
                     l=wbuf[ei][:, dch * 128:(dch + 1) * 128],
                     r_=oTs[ei][:, sl], f=(ei == 0), la=(ei == 7):
                     t.matmul(o, l, r_, start=f, stop=la),
                     inc=(p.sT if ei == 7 else None))
            pp.ready = (p.sT, p.sT.n)
            p.wait("vector", *pp.ready)
            p.wait("vector", p.sD, xr_ready)
            p.op("vector", lambda v, o=xr[:], i=pp.t[:, :]:
                 v.tensor_tensor(o, i, o, ALU.add), inc=p.sV)
            pp.free = (p.sV, p.sV.n)
            p.wait("scalar", p.sV)
            p.dma("scalar", yout[dch * 128:(dch + 1) * 128, sl], xr[:], inc=sOut)
    p.wait("sync", sOut)
    return (sOut, sOut.n)


def prep_attn_b0_weights(wq0, wq1, wk0, wk1, wv, wo, l0, l1, l2, l3, ln_w, ln_b):
    slope = 2.0 ** (-8.0 * np.arange(1, H + 1) / H)
    kr = np.arange(128)[:, None].astype(np.float32)
    qf = np.arange(512)[None, :].astype(np.float32)
    dmask5 = np.zeros((5, 128, 512), np.float32)
    dmask5[0] = kr - qf
    for i, off in enumerate((0, 128, 256, 384)):
        m = (kr + off) <= qf
        dmask5[1 + i] = np.where(m, kr - qf, -1e33)
    slope = 2.0 ** (-8.0 * np.arange(1, H + 1) / H)
    btbl = np.zeros((128, 320), np.float32)
    for h in range(H):
        for j in range(20):
            btbl[:, h * 20 + j] = slope[h] * (j - 16) * 128.0
    lnwP = np.zeros((128, 8), np.float32)
    lnbP = np.zeros((128, 8), np.float32)
    for hp in range(8):
        lnwP[:, hp] = np.concatenate([ln_w[2 * hp], ln_w[2 * hp + 1]])
        lnbP[:, hp] = np.concatenate([ln_b[2 * hp], ln_b[2 * hp + 1]])
    b3l = np.zeros((3, 16 * 128), np.float32)
    b3r = np.zeros((3, 512), np.float32)
    nn = np.arange(512, dtype=np.float32)
    b3r[0] = 1.0
    b3r[1] = -np.floor(nn / 16.0)
    b3r[2] = -(nn % 16.0)
    for h in range(H):
        b3l[0, h * 128:(h + 1) * 128] = slope[h] * np.arange(128)
        b3l[1, h * 128:(h + 1) * 128] = slope[h] * 16.0
        b3l[2, h * 128:(h + 1) * 128] = slope[h]
    return {
        "b3l": b3l.astype(BF), "b3r": b3r.astype(BF),
        "wq0": np.ascontiguousarray(wq0).astype(BF),
        "wq1": np.ascontiguousarray(wq1).astype(BF),
        "wk0": np.ascontiguousarray(wk0).astype(BF),
        "wk1": np.ascontiguousarray(wk1).astype(BF),
        "wv": np.ascontiguousarray(wv).astype(BF),
        "wo": np.ascontiguousarray(wo).astype(BF),
        "lvec": np.stack([l0, l1, l2, l3], axis=1).astype(np.float32),
        "dmask5": dmask5, "btbl": btbl, "lnwP": lnwP, "lnbP": lnbP,
        "ones_bf": np.ones((1, 128), BF),
    }


def build_attn_b0_launch():
    p = Prog()
    xT = p.inp("xT", [D, T], F32)
    ws = {n: p.inp(n, [D, D], BF16) for n in ("wq0", "wq1", "wk0", "wk1", "wv", "wo")}
    lvec = p.inp("lvec", [64, 4], F32)
    b3l = p.inp("b3l", [3, 16 * 128], BF16)
    b3r = p.inp("b3r", [3, 512], BF16)
    dmask5 = p.inp("dmask5", [5, 128, 512], F32)
    btbl = p.inp("btbl", [128, 320], F32)
    lnwP = p.inp("lnwP", [128, 8], F32)
    lnbP = p.inp("lnbP", [128, 8], F32)
    ones_bf = p.inp("ones_bf", [1, 128], BF16)
    consts = declare_const_inputs(p)
    yT = p.outp("yT", [D, T], F32)
    stt = St(p, with_x=False)
    stt.load_consts(*consts)
    xstr = [p.sb(f"xstr{i}", [128, 512], F32) for i in range(NDT)]
    xn_tcs = []
    emit_preln(stt, x_hbm=xT, xstr=xstr, xn_tcs=xn_tcs)
    emit_attn_b0(stt, ws["wq0"], ws["wq1"], ws["wk0"], ws["wk1"], ws["wv"], ws["wo"],
                 lvec, b3l, b3r, dmask5, btbl, lnwP, lnbP, ones_bf, xn_tcs,
                 resid_stream=(xT, yT))
    return p.finish()



def emit_attn_b1(st: St, wqP, wkP, wv, woT, mask4, ones_bf, xn_ready, resid_stream):
    xn_tc = xn_ready if isinstance(xn_ready, list) else [xn_ready] * 4
    """wqP/wkP: [1024, 512] f32 (cols = (h,32) zero-padded). wv: [1024,1024] f32 natural.
    woT: [1024, 1024] f32 = Wo^T (contract-e lhsT)."""
    p = st.p
    R2I = float(1.0 / np.sqrt(2.0))
    qT = [p.sb(f"b1q{i}", [128, T], BF16) for i in range(4)]
    kT = [p.sb(f"b1k{i}", [128, T], BF16) for i in range(4)]
    vx = [p.sb(f"b1v{i}", [128, H * 65], BF16) for i in range(16)]
    oTs = [p.sb(f"b1oT{i}", [128, T], BF16) for i in range(8)]
    wbuf = [p.sb(f"b1w{i}", [128, 2048], BF16) for i in range(NDT)]
    et = [wbuf[i // 4][:, (i % 4) * 512:(i % 4 + 1) * 512] for i in range(16)]
    msk = p.sb("b1msk", [128, 4 * 512], BF16)
    onb = p.sb("b1onb", [1, 128], BF16)
    recb = p.sb("b1recb", [1, 2 * 512], BF16)
    rec_prev = [0]
    r2col = p.sb("b1r2", [128, 1], F32)
    xres = [p.sb(f"b1xr{i}", [128, 512], F32) for i in range(2)]
    scrS1 = [st.scr, p.sb("b1scr2", [128, 512], F32)]
    et_dve = [0, 0]
    p.op("vector", lambda v: v.memset(r2col[:, :], R2I))
    p.dma("sync", onb[:], ones_bf)
    for m in range(4):
        p.dma("sync", msk[:, m * 512:(m + 1) * 512], mask4[m])
    consts_ld = p.sD.n

    for wsrc, dstl in ((wqP, qT), (wkP, kT)):
        p.wait("gpsimd", p.sT)
        for dt in range(NDT):
            p.dma("gpsimd", wbuf[dt][:, 0:512], wsrc[dt * 128:(dt + 1) * 128, :])
        w_ready = p.sW.n
        for ec in range(4):
            for tc in range(NTC):
                sl = slice(tc * 512, (tc + 1) * 512)
                pp = st.psA if (ec * NTC + tc) % 2 == 0 else st.psB
                st.tfree(pp)
                p.wait("tensor", p.sW, w_ready)
                if ec == 0:
                    p.wait("tensor", p.sV, xn_tc[tc])
                for dt in range(NDT):
                    p.op("tensor", lambda t, o=pp.t[:, :], l=wbuf[dt][:, ec * 128:(ec + 1) * 128],
                         r_=st.XN[dt][:, sl], f=(dt == 0), la=(dt == NDT - 1):
                         t.matmul(o, l, r_, start=f, stop=la),
                         inc=(p.sT if dt == NDT - 1 else None))
                pp.ready = (p.sT, p.sT.n)
                p.wait("vector", *pp.ready)
                p.op("vector", lambda v, o=dstl[ec][:, sl], i=pp.t[:, :]:
                     v.tensor_copy(o, i), inc=p.sV)
                pp.free = (p.sV, p.sV.n)
    qk_done = p.sV.n

    p.wait("gpsimd", p.sT)
    for dt in range(NDT):
        p.dma("gpsimd", wbuf[dt][:, 0:1024], wv[dt * 128:(dt + 1) * 128, :])
    wv_ready = p.sW.n
    for tt in range(16):
        p.op("vector", lambda v, o=vx[tt]: v.memset(
            o[:].rearrange("p (h j) -> p h j", h=H)[:, :, 64:65], 1.0),
            inc=(p.sV if tt == 15 else None))
    vones_ready = p.sV.n
    for tt in range(16):
        for ec in range(2):
            pp = st.psA if (tt * 2 + ec) % 2 == 0 else st.psB
            st.tfree(pp)
            p.wait("tensor", p.sW, wv_ready)
            for dt in range(NDT):
                p.op("tensor", lambda t, o=pp.t[:, :],
                     l=st.XN[dt][:, tt * 128:(tt + 1) * 128],
                     r_=wbuf[dt][:, ec * 512:(ec + 1) * 512],
                     f=(dt == 0), la=(dt == NDT - 1):
                     t.matmul(o, l, r_, start=f, stop=la),
                     inc=(p.sT if dt == NDT - 1 else None))
            pp.ready = (p.sT, p.sT.n)
            p.wait("vector", *pp.ready)
            if tt == 0 and ec == 0:
                p.wait("vector", p.sV, vones_ready)
            p.op("vector", lambda v,
                 o=vx[tt][:].rearrange("p (h j) -> p h j", h=H)[:, ec * 8:(ec + 1) * 8, 0:64],
                 i=pp.t[:, :].rearrange("p (h j) -> p h j", j=64):
                 v.tensor_copy(o, i), inc=p.sV)
            pp.free = (p.sV, p.sV.n)
    v_done = p.sV.n

    for qc in range(NTC):
        qsl = slice(qc * 512, (qc + 1) * 512)
        nkt = 4 * qc + 4
        for hp in range(8):
            for hh, po in ((0, st.psE), (1, st.psF)):
                h = hp * 2 + hh
                til = h // 4
                hs = slice((h % 4) * 32, (h % 4) * 32 + 32)
                tpos = ((h % 4) * 32, 0)
                for kt in range(nkt):
                    pp = [st.psA, st.psB, st.psC, st.psD][kt % 4]
                    st.tfree(pp)
                    if qc == 0 and hp == 0 and hh == 0 and kt == 0:
                        p.wait("tensor", p.sV, qk_done)
                        p.wait("tensor", p.sD, consts_ld)
                    p.op("tensor", lambda t, o=pp.t[:, :], l=kT[til][hs, kt * 128:(kt + 1) * 128],
                         r_=qT[til][hs, qsl], tp=tpos:
                         t.matmul(o, l, r_, start=True, stop=True, tile_position=tp),
                         inc=p.sT)
                    pp.ready = (p.sT, p.sT.n)
                    scr_ = scrS1[kt % 2]
                    p.wait("scalar", *pp.ready)
                    if et_dve[kt % 2]:
                        p.wait("scalar", p.sV, et_dve[kt % 2])
                    p.op("scalar", lambda s, o=scr_[:], i=pp.t[:, :]:
                         s.activation(out=o, in_=i, func=AF.Square, scale=R2I / 4.0,
                                      bias=r2col[:, 0:1]), inc=p.sS)
                    pp.free = (p.sS, p.sS.n)
                    mi = kt - 4 * qc
                    p.wait("vector", p.sS)
                    if mi >= 0:
                        p.op("vector", lambda v, o=et[kt][:], i=scr_[:],
                             m=msk[:, mi * 512:(mi + 1) * 512]:
                             v.scalar_tensor_tensor(out=o, in0=i, scalar=0.5, in1=m,
                                                    op0=ALU.add, op1=ALU.mult), inc=p.sV)
                    else:
                        p.op("vector", lambda v, o=et[kt][:], i=scr_[:]:
                             v.tensor_scalar_add(o, i, 0.5), inc=p.sV)
                    et_dve[kt % 2] = p.sV.n
                av_done_v = p.sV.n
                st.tfree(po)
                p.wait("tensor", p.sV, av_done_v)
                if qc == 0 and hp == 0 and hh == 0:
                    p.wait("tensor", p.sV, v_done)
                for kt in range(nkt):
                    p.op("tensor", lambda t, o=po.t[0:65, :],
                         l=vx[kt][:, h * 65:(h + 1) * 65], r_=et[kt][:],
                         f=(kt == 0), la=(kt == nkt - 1):
                         t.matmul(o, l, r_, start=f, stop=la),
                         inc=(p.sT if kt == nkt - 1 else None))
                po.ready = (p.sT, p.sT.n)
            p.wait("vector", *st.psE.ready)
            p.wait("vector", *st.psF.ready)
            if rec_prev[0]:
                p.wait("vector", p.sS, rec_prev[0])
            for hh, po in ((0, st.psE), (1, st.psF)):
                p.op("vector", lambda v, o=st.rows[0:1, hh * 512:(hh + 1) * 512],
                     i=po.t[64:65, :]: v.tensor_scalar_add(o, i, 1e-12))
                p.op("vector", lambda v, o=oTs[hp][hh * 64:(hh + 1) * 64, qsl],
                     i=po.t[0:64, :]: v.tensor_copy(o, i), inc=p.sV)
            recs_done = p.sV.n
            p.wait("scalar", p.sV, recs_done)
            p.op("scalar", lambda s, o=st.rows[0:1, 0:1024], i=st.rows[0:1, 0:1024]:
                 s.activation(out=o, in_=i, func=AF.Ln))
            p.op("scalar", lambda s, o=recb[0:1, 0:1024], i=st.rows[0:1, 0:1024]:
                 s.activation(out=o, in_=i, func=AF.Exp, scale=-1.0), inc=p.sS)
            rec_prev[0] = p.sS.n
            st.psE.free = st.psF.free = (p.sV, recs_done)
            pr = st.psO
            st.tfree(pr)
            p.wait("tensor", p.sS, rec_prev[0])
            p.op("tensor", lambda t, o=pr.t[0:64, 0:512], l=onb[:, 0:64],
                 r_=recb[0:1, 0:512]: t.matmul(o, l, r_, start=True, stop=True))
            p.op("tensor", lambda t, o=pr.t[64:128, 0:512], l=onb[:, 0:64],
                 r_=recb[0:1, 512:1024]: t.matmul(o, l, r_, start=True, stop=True),
                 inc=p.sT)
            pr.ready = (p.sT, p.sT.n)
            p.wait("vector", *pr.ready)
            p.op("vector", lambda v, o=oTs[hp][:, qsl], bc=pr.t[:, 0:512]:
                 v.tensor_tensor(o, o, bc, ALU.mult), inc=p.sV)
            pr.free = (p.sV, p.sV.n)
    att_done = p.sV.n

    p.wait("gpsimd", p.sT)
    for dt in range(NDT):
        p.dma("gpsimd", wbuf[dt][:, 0:1024], woT[dt * 128:(dt + 1) * 128, :])
    wo_ready = p.sW.n
    xin, yout = resid_stream
    sOut = p.sem("b1sOut")
    n_out = 0
    for dch in range(NDT):
        for tc in range(NTC):
            sl = slice(tc * 512, (tc + 1) * 512)
            pp = st.psE if tc % 2 == 0 else st.psF
            xr = xres[tc % 2]
            p.wait("sync", p.sV)
            p.wait("sync", sOut, 16 * max(0, n_out - 1))
            n_out += 1
            p.dma("sync", xr[:], xin[dch * 128:(dch + 1) * 128, sl])
            xr_ready = p.sD.n
            st.tfree(pp)
            p.wait("tensor", p.sW, wo_ready)
            if dch == 0 and tc == 0:
                p.wait("tensor", p.sV, att_done)
            for ei in range(8):
                p.op("tensor", lambda t, o=pp.t[:, :],
                     l=wbuf[ei][:, dch * 128:(dch + 1) * 128],
                     r_=oTs[ei][:, sl], f=(ei == 0), la=(ei == 7):
                     t.matmul(o, l, r_, start=f, stop=la),
                     inc=(p.sT if ei == 7 else None))
            pp.ready = (p.sT, p.sT.n)
            p.wait("vector", *pp.ready)
            p.wait("vector", p.sD, xr_ready)
            p.op("vector", lambda v, o=xr[:], i=pp.t[:, :]:
                 v.tensor_tensor(o, i, o, ALU.add), inc=p.sV)
            pp.free = (p.sV, p.sV.n)
            p.wait("scalar", p.sV)
            p.dma("scalar", yout[dch * 128:(dch + 1) * 128, sl], xr[:], inc=sOut)
    p.wait("sync", sOut)
    return (sOut, sOut.n)


def prep_attn_b1_weights(Wq, Wk, Wv, Wo):
    wqP = np.zeros((1024, 512), np.float32)
    wkP = np.zeros((1024, 512), np.float32)
    for h in range(H):
        wqP[:, h * 32:h * 32 + 16] = Wq[h * 16:(h + 1) * 16, :].T
        wkP[:, h * 32:h * 32 + 16] = Wk[h * 16:(h + 1) * 16, :].T
    kr = np.arange(128)[:, None]
    qf = np.arange(512)[None, :]
    mask4 = np.stack([((kr + off) <= qf).astype(BF) for off in (0, 128, 256, 384)])
    return {"wqP": wqP.astype(BF), "wkP": wkP.astype(BF),
            "wv": np.ascontiguousarray(Wv.T).astype(BF),
            "woT": np.ascontiguousarray(Wo.T).astype(BF), "mask4": mask4,
            "ones_bf": np.ones((1, 128), BF)}


def build_attn_b1_launch():
    p = Prog()
    xT = p.inp("xT", [D, T], F32)
    wqP = p.inp("wqP", [D, 512], BF16)
    wkP = p.inp("wkP", [D, 512], BF16)
    wv = p.inp("wv", [D, D], BF16)
    woT = p.inp("woT", [D, D], BF16)
    mask4 = p.inp("mask4", [4, 128, 512], BF16)
    ones_bf = p.inp("ones_bf", [1, 128], BF16)
    consts = declare_const_inputs(p)
    yT = p.outp("yT", [D, T], F32)
    stt = St(p, with_x=False)
    stt.load_consts(*consts)
    xstr = [p.sb(f"xstr{i}", [128, 512], F32) for i in range(NDT)]
    xn_tcs = []
    emit_preln(stt, x_hbm=xT, xstr=xstr, xn_tcs=xn_tcs)
    emit_attn_b1(stt, wqP, wkP, wv, woT, mask4, ones_bf, xn_tcs,
                 resid_stream=(xT, yT))
    return p.finish()



def emit_ffn1(st: St, rwT, rngt, rnbt, w1, lngt, lnbt, w2t, ongt, onbt,
              mut, vart, x_src, idf, ones_bf, xn_ready, resid_stream,
              ubf=None):
    p = st.p
    wbuf = [p.sb(f"f1w{i}", [128, 2048], BF16) for i in range(NDT)]
    w2bb = [p.sb("f1wB", [128, 16 * 128], BF16), p.sb("f1wB2", [128, 16 * 128], BF16)]
    if ubf is None:
        ubf = [p.sb(f"f1u{i}", [128, 1024], BF16) for i in range(16)]
    outac = [p.sb(f"f1oa{i}", [128, 1024], F32) for i in range(NDT)]
    sqb = p.sb("f1sq", [128, 1024], BF16)
    obf = p.sb("f1obf", [128, 1024], BF16)
    sqb2 = [sqb, p.sb("f1sqB", [128, 1024], BF16)]
    obf2 = [obf, p.sb("f1obB", [128, 1024], BF16)]
    hrl = p.sb("f1hrl", [128, 512], BF16)
    hrl2 = [hrl, st.scr_bf]
    scr2 = [st.scr, p.sb("f1scr2", [128, 512], F32)]
    sq_done = [0, 0]
    hr_done = [0, 0]
    mubc = p.sb("f1mu", [128, 1024], F32)
    w1m = p.sb("f1w1m", [128, 32], BF16)
    w1m_done = [0, 0, 0, 0]
    rws = [p.sb(f"f1rw{i}", [128, 4], F32) for i in range(NDT)]
    rwtmp = p.sb("f1rwtmp", [128, 4], F32)
    rnbF = p.sb("f1rnbF", [128, NDT], F32)
    onesc = p.sb("f1onec", [128, 1], F32)
    srow = p.sb("f1srow", [1, 4], F32)
    sbc = p.sb("f1sbc", [128, 4], F32)
    ltok = p.sb("f1lt", [128, 64], F32)
    wtok = p.sb("f1wt", [128, 64], F32)
    mrow = p.sb("f1mr", [128, 32], F32)
    b4bc = p.sb("f1b4", [128, 4], F32)
    wrow = p.sb("f1wr", [1, 4 * T], BF16)
    rowf = p.sb("f1rowf", [1, 6 * 512], F32)
    rowb2 = p.sb("f1rowb", [1, 2 * 512], BF16)
    onb = p.sb("f1onb", [1, 128], BF16)
    lng = p.sb("f1lng", [128, 64], F32)
    lnb_ = p.sb("f1lnb", [128, 64], F32)
    ong = p.sb("f1ong", [128, NDT], F32)
    onbv = p.sb("f1onbv", [128, NDT], F32)
    rng_ = p.sb("f1rng", [128, NDT], F32)
    xres = [p.sb(f"f1xr{i}", [128, 512], F32) for i in range(2)]
    for dst_, src_ in ((onb, ones_bf), (lng, lngt), (lnb_, lnbt),
                       (ong, ongt), (onbv, onbt), (rng_, rngt)):
        p.dma("sync", dst_[:], src_)
    for dt in range(NDT):
        p.dma("sync", rws[dt][:], rwT[dt * 128:(dt + 1) * 128, :])
    p.dma("sync", rnbF[:], rnbt)
    consts_ld = p.sD.n
    w_ld = p.sW.n

    def tiny(eng, fn):
        sem = {"vector": p.sV, "scalar": p.sS}[eng]
        p.wait(eng, sem)
        p.op(eng, fn, inc=sem)

    p.op("vector", lambda v: v.memset(onesc[:, :], 1.0), inc=p.sV)
    p.wait("vector", p.sD, consts_ld)
    for dt in range(NDT):
        p.op("vector", lambda v, o=rws[dt][:], i=rws[dt][:], g=rng_[:, dt:dt + 1]:
             v.tensor_scalar_mul(o, i, g), inc=(p.sV if dt == NDT - 1 else None))
    rws_ready = p.sV.n
    pb = st.psA
    st.tfree(pb)
    for dt in range(NDT):
        p.wait("sync", p.sT)
        p.dma("sync", rwtmp[:], rwT[dt * 128:(dt + 1) * 128, :])
        p.wait("tensor", p.sD)
        p.op("tensor", lambda t, o=pb.t[0:1, 0:4], l=rnbF[:, dt:dt + 1],
             r_=rwtmp[:], f=(dt == 0), la=(dt == NDT - 1):
             t.matmul(o, l, r_, start=f, stop=la), inc=p.sT)
    pb.ready = (p.sT, p.sT.n)
    pb2 = st.psB
    st.tfree(pb2)
    p.wait("tensor", p.sV, rws_ready)
    for dt in range(NDT):
        p.op("tensor", lambda t, o=pb2.t[0:1, 0:4], l=onesc[:, :], r_=rws[dt][:, :],
             f=(dt == 0), la=(dt == NDT - 1):
             t.matmul(o, l, r_, start=f, stop=la), inc=p.sT)
    pb2.ready = (p.sT, p.sT.n)
    p.wait("vector", *pb.ready)
    p.op("vector", lambda v: v.tensor_copy(rowf[0:1, 0:4], pb.t[0:1, 0:4]), inc=p.sV)
    pb.free = (p.sV, p.sV.n)
    p.wait("vector", *pb2.ready)
    p.op("vector", lambda v: v.tensor_copy(srow[0:1, :], pb2.t[0:1, 0:4]), inc=p.sV)
    pb2.free = (p.sV, p.sV.n)
    rows4_done = p.sV.n
    for ps, rsrc, dst_ in ((st.psA, rowf[0:1, 0:4], b4bc), (st.psB, srow[0:1, :], sbc)):
        st.tfree(ps)
        p.wait("tensor", p.sV, rows4_done)
        p.op("tensor", lambda t, o=ps.t[:, 0:4], l=st.ones_row[:], r_=rsrc:
             t.matmul(o, l, r_, start=True, stop=True), inc=p.sT)
        ps.ready = (p.sT, p.sT.n)
        p.wait("vector", *ps.ready)
        p.op("vector", lambda v, o=dst_[:], i=ps.t[:, 0:4]: v.tensor_copy(o, i),
             inc=p.sV)
        ps.free = (p.sV, p.sV.n)
    for tc in range(NTC):
        ldfn, xaps = x_src(tc)
        if ldfn is not None:
            ldfn()
        pl = st.psC
        st.tfree(pl)
        p.wait("tensor", p.sV, rws_ready)
        p.wait("tensor", p.sD)
        for dt in range(NDT):
            p.op("tensor", lambda t, o=pl.t[0:4, :], l=rws[dt][:, :], r_=xaps[dt],
                 f=(dt == 0), la=(dt == NDT - 1):
                 t.matmul(o, l, r_, start=f, stop=la),
                 inc=(p.sT if dt == NDT - 1 else None))
        pl.ready = (p.sT, p.sT.n)
        p.wait("vector", *pl.ready)
        p.op("vector", lambda v, i=pl.t[0:4, :]: v.tensor_copy(st.scr[0:4, :], i),
             inc=p.sV)
        pl.free = (p.sV, p.sV.n)
        lrows_done = p.sV.n
        pd = st.psD
        st.tfree(pd)
        p.wait("tensor", p.sV, lrows_done)
        for c in range(4):
            p.op("tensor", lambda t, o=pd.t[:, c * 4:(c + 1) * 4],
                 i=st.scr[0:4, c * 128:(c + 1) * 128]:
                 t.transpose(o, i, idf[0:4, 0:4]), inc=(p.sT if c == 3 else None))
        pd.ready = (p.sT, p.sT.n)
        p.wait("vector", *pd.ready)
        p.op("vector", lambda v, o=ltok[:, tc * 16:(tc + 1) * 16], i=pd.t[:, 0:16]:
             v.tensor_copy(o, i), inc=p.sV)
        pd.free = (p.sV, p.sV.n)
    tiny("vector", lambda v: v.tensor_scalar_add(mrow[:, 0:16], vart[:], EPS_LN))
    tiny("vector", lambda v: v.reciprocal(mrow[:, 16:32], mrow[:, 0:16]))
    tiny("vector", lambda v: v.tensor_tensor(mrow[:, 16:32], vart[:], mrow[:, 16:32],
                                             ALU.mult))
    tiny("vector", lambda v: v.tensor_scalar_add(mrow[:, 16:32], mrow[:, 16:32],
                                                 EPS_LN))
    tiny("vector", lambda v: v.tensor_tensor(vart[:], mrow[:, 16:32], mrow[:, 0:16],
                                             ALU.mult))
    p.wait("scalar", p.sV)
    p.op("scalar", lambda s: s.activation(out=vart[:], in_=vart[:], func=AF.Ln),
         inc=p.sS)
    p.wait("scalar", p.sS)
    p.op("scalar", lambda s: s.activation(out=vart[:], in_=vart[:], func=AF.Exp,
                                          scale=-0.5), inc=p.sS)
    p.wait("vector", p.sS)
    lt3 = ltok[:].rearrange("p (c e) -> p c e", e=4)
    wt3 = wtok[:].rearrange("p (c e) -> p c e", e=4)
    bc16 = lambda t_: t_.unsqueeze(2).broadcast_to([128, 16, 4])
    tiny("vector", lambda v: v.tensor_tensor(
        wt3, bc16(mut[:]), sbc[:].unsqueeze(1).broadcast_to([128, 16, 4]), ALU.mult))
    tiny("vector", lambda v: v.tensor_tensor(ltok[:], ltok[:], wtok[:], ALU.subtract))
    tiny("vector", lambda v: v.tensor_tensor(lt3, lt3, bc16(vart[:]), ALU.mult))
    tiny("vector", lambda v: v.tensor_tensor(
        lt3, lt3, b4bc[:].unsqueeze(1).broadcast_to([128, 16, 4]), ALU.add))
    tiny("vector", lambda v: v.reduce_max(mrow[:, 0:16], lt3, axis=AX))
    tiny("vector", lambda v: v.tensor_tensor(lt3, lt3, bc16(mrow[:, 0:16]),
                                             ALU.subtract))
    p.wait("scalar", p.sV)
    p.op("scalar", lambda s: s.activation(out=ltok[:], in_=ltok[:], func=AF.Exp),
         inc=p.sS)
    p.wait("vector", p.sS)
    tiny("vector", lambda v: v.reduce_sum(mrow[:, 0:16], lt3, axis=AX))
    tiny("vector", lambda v: v.reciprocal(mrow[:, 0:16], mrow[:, 0:16]))
    tiny("vector", lambda v: v.tensor_tensor(lt3, lt3, bc16(mrow[:, 0:16]), ALU.mult))
    tiny("vector", lambda v: v.reduce_max(mrow[:, 0:16], lt3, axis=AX))
    tiny("vector", lambda v: v.tensor_tensor(wt3, lt3, bc16(mrow[:, 0:16]),
                                             ALU.is_equal))
    tiny("vector", lambda v: v.scalar_tensor_tensor(
        out=wt3, in0=wt3, scalar=-1e30, in1=lt3, op0=ALU.mult, op1=ALU.add))
    tiny("vector", lambda v: v.reduce_max(mrow[:, 16:32], wt3, axis=AX))
    tiny("vector", lambda v: v.tensor_tensor(wt3, lt3, bc16(mrow[:, 16:32]),
                                             ALU.is_ge))
    tiny("vector", lambda v: v.tensor_tensor(wt3, wt3, lt3, ALU.mult))
    tiny("vector", lambda v: v.tensor_tensor(mrow[:, 0:16], mrow[:, 0:16],
                                             mrow[:, 16:32], ALU.add))
    tiny("vector", lambda v: v.tensor_scalar_add(mrow[:, 0:16], mrow[:, 0:16], 1e-8))
    tiny("vector", lambda v: v.reciprocal(mrow[:, 0:16], mrow[:, 0:16]))
    tiny("vector", lambda v: v.tensor_tensor(wt3, wt3, bc16(mrow[:, 0:16]), ALU.mult))
    wt_done = p.sV.n
    import os
    if os.environ.get("F1DBG"):
        dbg = p.nc.dram_tensor("dbg", [128, 64 + 16 + 64], F32, kind="ExternalOutput").ap()
        sdbg = p.sem("f1dbg")
        p.wait("sync", p.sV, wt_done)
        p.dma("sync", dbg[:, 0:64], wtok[:], inc=sdbg)
        p.dma("sync", dbg[:, 64:80], vart[:], inc=sdbg)
        p.dma("sync", dbg[:, 80:144], ltok[:], inc=sdbg)
        p.wait("sync", sdbg)
    pw = st.psF
    for e in range(4):
        for cg in range(4):
            st.tfree(pw)
            p.wait("tensor", p.sV, wt_done)
            for j in range(4):
                c = cg * 4 + j
                p.op("tensor", lambda t, o=pw.t[0:1, j * 128:(j + 1) * 128],
                     i=wtok[:, c * 4 + e:c * 4 + e + 1]:
                     t.transpose(o, i, idf[:]), inc=(p.sT if j == 3 else None))
            pw.ready = (p.sT, p.sT.n)
            p.wait("vector", *pw.ready)
            p.op("vector", lambda v,
                 o=wrow[0:1, e * T + cg * 512:e * T + (cg + 1) * 512],
                 i=pw.t[0:1, 0:512]: v.tensor_copy(o, i), inc=p.sV)
            pw.free = (p.sV, p.sV.n)
    wrow_done = p.sV.n
    import os as _os2
    if _os2.environ.get("F1DBG"):
        dbg3 = p.nc.dram_tensor("dbg3", [1, 4 * T], BF16, kind="ExternalOutput").ap()
        sdbg3 = p.sem("f1dbg3")
        p.wait("sync", p.sV, wrow_done)
        p.dma("sync", dbg3[0:1, :], wrow[0:1, :], inc=sdbg3)
        p.wait("sync", sdbg3)

    import os as _os
    dbg2 = None
    if _os.environ.get("F1DBG2"):
        dbg2 = p.nc.dram_tensor("dbg2", [128, 4 * 1024], F32, kind="ExternalOutput").ap()
        sdbg2 = p.sem("f1dbg2")
        dbg2_parts = []
    sOut = p.sem("f1sOut")
    f1_nout = 0
    xin, yout = resid_stream
    groups = [(th, e) for th in range(2) for e in range(4)]
    w1_rdy = {}

    def fetch_w1(gi, free_val):
        th_, e_ = groups[gi]
        p.wait("sync", p.sT, free_val)
        for dt in range(NDT):
            p.dma("sync", wbuf[dt][:], w1[e_, dt * 128:(dt + 1) * 128, :])
        w1_rdy[gi] = p.sD.n

    import os as _os8
    _PF = int(_os8.environ.get("F1PF", "1"))
    _ROT = int(_os8.environ.get("F1ROT", "4"))
    fetch_w1(0, 0)
    w2bb_free = [0, 0]
    for gi, (th, e) in enumerate(groups):
        t0 = th * 1024
        if True:
            if gi > 0 and not _PF:
                fetch_w1(gi, p.sT.n)
            w1_ready = w1_rdy[gi]
            if th == 0:
                p.wait("vector", p.sD, w1_ready)
                for dt in range(NDT):
                    p.op("vector", lambda v, o=st.scr[:, 0:1], i=wbuf[dt][:]:
                         v.reduce_sum(o, i, axis=AX), inc=p.sV)
                    p.wait("vector", p.sV)
                    p.op("vector", lambda v,
                         o=w1m[:, e * 8 + dt:e * 8 + dt + 1], i=st.scr[:, 0:1]:
                         v.tensor_scalar_mul(o, i, 1.0 / 2048.0), inc=p.sV)
                    p.wait("vector", p.sV)
                w1m_done[e] = p.sV.n
            w1m_ready = w1m_done[e]
            for tc2 in range(2):
                sl = slice(t0 + tc2 * 512, t0 + (tc2 + 1) * 512)
                pm_ = st.psC if tc2 == 0 else st.psD
                st.tfree(pm_)
                p.wait("tensor", p.sV, w1m_ready)
                for dt in range(NDT):
                    p.op("tensor", lambda t, o=pm_.t[0:1, :],
                         l=w1m[:, e * 8 + dt:e * 8 + dt + 1],
                         r_=st.XN[dt][:, sl], f=(dt == 0), la=(dt == NDT - 1):
                         t.matmul(o, l, r_, start=f, stop=la),
                         inc=(p.sT if dt == NDT - 1 else None))
                pm_.ready = (p.sT, p.sT.n)
                p.wait("vector", *pm_.ready)
                p.op("vector", lambda v, o=rowb2[0:1, tc2 * 512:(tc2 + 1) * 512],
                     i=pm_.t[0:1, :]: v.tensor_copy(o, i), inc=p.sV)
                murow_done = p.sV.n
                pm_.free = (p.sV, murow_done)
                st.tfree(pm_)
                p.wait("tensor", p.sV, murow_done)
                p.op("tensor", lambda t, o=pm_.t[:, :], l=onb[:, :],
                     r_=rowb2[0:1, tc2 * 512:(tc2 + 1) * 512]:
                     t.matmul(o, l, r_, start=True, stop=True), inc=p.sT)
                pm_.ready = (p.sT, p.sT.n)
                p.wait("vector", *pm_.ready)
                p.op("vector", lambda v, o=mubc[:, tc2 * 512:(tc2 + 1) * 512],
                     i=pm_.t[:, :]: v.tensor_copy(o, i), inc=p.sV)
                pm_.free = (p.sV, p.sV.n)
            mubc_done = p.sV.n
            if dbg2 is not None and th == 0 and e == 0:
                p.wait("sync", p.sV, mubc_done)
                p.dma("sync", dbg2[:, 0:1024], mubc[:], inc=sdbg2)
            pstat = [st.psC, st.psD]
            st.tfree(pstat[0])
            st.tfree(pstat[1])
            stat_q = []
            stat_first = [True, True]
            stat_tfin = [0, 0]
            sq_sfin = [0, 0]
            prot = [st.psA, st.psB, st.psE, st.psF][:_ROT]
            for ht in range(16):
                for tc2 in range(2):
                    sl = slice(t0 + tc2 * 512, t0 + (tc2 + 1) * 512)
                    usl = slice(tc2 * 512, (tc2 + 1) * 512)
                    pp = prot[(ht * 2 + tc2) % _ROT]
                    st.tfree(pp)
                    if ht == 0 and tc2 == 0:
                        p.wait("tensor", p.sV, mubc_done)
                        p.wait("tensor", p.sD, w1_ready)
                    for dt in range(NDT):
                        p.op("tensor", lambda t, o=pp.t[:, :],
                             l=wbuf[dt][:, ht * 128:(ht + 1) * 128],
                             r_=st.XN[dt][:, sl], f=(dt == 0), la=(dt == NDT - 1):
                             t.matmul(o, l, r_, start=f, stop=la),
                             inc=(p.sT if dt == NDT - 1 else None))
                    pp.ready = (p.sT, p.sT.n)
                    if ht == 15 and tc2 == 1:
                        hid_mm_done = p.sT.n
                    p.wait("vector", *pp.ready)
                    p.op("vector", lambda v, o=ubf[ht][:, usl], i=pp.t[:, :],
                         m=mubc[:, usl]:
                         v.tensor_tensor(o, i, m, ALU.subtract), inc=p.sV)
                    u_done = p.sV.n
                    pp.free = (p.sV, u_done)
                    i1_ = ht * 2 + tc2
                    p.wait("scalar", p.sV, u_done)
                    if i1_ >= 2:
                        p.wait("scalar", p.sT, stat_tfin[i1_ % 2])
                    p.op("scalar", lambda s, o=sqb[:, usl], i=ubf[ht][:, usl]:
                         s.activation(out=o, in_=i, func=AF.Square), inc=p.sS)
                    sq_sfin[i1_ % 2] = p.sS.n
                    stat_q.append((tc2, usl, p.sS.n, ht == 15))
                    if len(stat_q) >= 2:
                        tcp, uslp, sqv, lst = stat_q.pop(0)
                        p.wait("tensor", p.sS, sqv)
                        p.op("tensor", lambda t, o=pstat[tcp].t[0:1, :],
                             l=st.ones_col[:, 0:1], r_=sqb[:, uslp],
                             f0=stat_first[tcp], la=lst:
                             t.matmul(o, l, r_, start=f0, stop=la), inc=p.sT)
                        stat_first[tcp] = False
                        stat_tfin[tcp] = p.sT.n
            while stat_q:
                tcp, uslp, sqv, lst = stat_q.pop(0)
                p.wait("tensor", p.sS, sqv)
                p.op("tensor", lambda t, o=pstat[tcp].t[0:1, :],
                     l=st.ones_col[:, 0:1], r_=sqb[:, uslp], f0=stat_first[tcp],
                     la=lst:
                     t.matmul(o, l, r_, start=f0, stop=la), inc=p.sT)
                stat_first[tcp] = False
                stat_tfin[tcp] = p.sT.n
            pstat[0].ready = pstat[1].ready = (p.sT, p.sT.n)
            if _PF and gi + 1 < len(groups):
                fetch_w1(gi + 1, hid_mm_done)
            if dbg2 is not None and th == 0 and e == 0:
                p.wait("sync", p.sV)
                p.op("sync", lambda s_: s_.dma_start(
                    out=dbg2[:, 1024:2048],
                    in_=ubf[0][:].bitcast(F32)) if False else None)
            p.wait("vector", *pstat[0].ready)
            for tc2 in range(2):
                rsl = slice(tc2 * 512, (tc2 + 1) * 512)
                p.op("vector", lambda v, o=rowf[0:1, rsl], i=pstat[tc2].t[0:1, :]:
                     v.tensor_scalar_mul(o, i, 1.0 / 2048.0),
                     inc=(p.sV if tc2 == 1 else None))
            pstat[0].free = pstat[1].free = (p.sV, p.sV.n)
            tiny("vector", lambda v: v.tensor_scalar_add(rowf[0:1, 0:1024],
                                                         rowf[0:1, 0:1024], EPS_LN))
            p.wait("scalar", p.sV)
            p.op("scalar", lambda s: s.activation(out=rowf[0:1, 0:1024],
                                                  in_=rowf[0:1, 0:1024], func=AF.Ln),
                 inc=p.sS)
            p.wait("scalar", p.sS)
            p.op("scalar", lambda s: s.activation(out=rowf[0:1, 0:1024],
                                                  in_=rowf[0:1, 0:1024], func=AF.Exp,
                                                  scale=-0.5), inc=p.sS)
            p.wait("vector", p.sS)
            p.op("vector", lambda v: v.tensor_copy(rowb2[0:1, 0:1024],
                                                   rowf[0:1, 0:1024]), inc=p.sV)
            rrows_done = p.sV.n
            bcs = [st.psC, st.psD, st.psE, st.psF]
            for i2, (ps, rsrc) in enumerate((
                    (st.psC, rowb2[0:1, 0:512]), (st.psD, rowb2[0:1, 512:1024]),
                    (st.psE, wrow[0:1, e * T + t0:e * T + t0 + 512]),
                    (st.psF, wrow[0:1, e * T + t0 + 512:e * T + t0 + 1024]))):
                st.tfree(ps)
                p.wait("tensor", p.sV, rrows_done)
                if i2 >= 2:
                    p.wait("tensor", p.sV, wrow_done)
                p.op("tensor", lambda t, o=ps.t[:, :], l=onb[:, :], r_=rsrc:
                     t.matmul(o, l, r_, start=True, stop=True), inc=p.sT)
                ps.ready = (p.sT, p.sT.n)
            p.wait("vector", *bcs[3].ready)
            ph2 = [(ht, tc2) for tc2 in range(2) for ht in range(16)]
            hw_half = [0, 0]
            m1v = {}
            m2v = {}
            actS = {}
            pend = []

            def emit_mult2(j):
                htj, tcj = ph2[j]
                uslj = slice(tcj * 512, (tcj + 1) * 512)
                p.wait("vector", p.sS, actS[j])
                p.op("vector", lambda v, o=ubf[htj][:, uslj], a=hrl2[j % 2][:],
                     bc=bcs[2 + tcj].t[:, :]:
                     v.tensor_tensor(o, a, bc, ALU.mult), inc=p.sV)
                m2v[j] = p.sV.n
                if htj == 15:
                    hw_half[tcj] = p.sV.n

            for i2_, (ht, tc2) in enumerate(ph2):
                usl = slice(tc2 * 512, (tc2 + 1) * 512)
                pb_ = i2_ % 2
                scr_ = scr2[pb_]
                hrl_ = hrl2[pb_]
                if i2_ >= 2:
                    p.wait("vector", p.sS, actS[i2_ - 2])
                p.op("vector", lambda v, o=scr_[:], a=ubf[ht][:, usl],
                     bc=bcs[tc2].t[:, :]:
                     v.tensor_tensor(o, a, bc, ALU.mult), inc=p.sV)
                m1v[i2_] = p.sV.n
                p.wait("scalar", p.sV, m1v[i2_])
                if i2_ >= 2:
                    p.wait("scalar", p.sV, m2v[i2_ - 2])
                p.op("scalar", lambda s, o=hrl_[:], i=scr_[:], col=e * 16 + ht:
                     s.activation(out=o, in_=i, func=AF.Relu,
                                  scale=lng[:, col:col + 1],
                                  bias=lnb_[:, col:col + 1]), inc=p.sS)
                actS[i2_] = p.sS.n
                pend.append(i2_)
                if len(pend) >= 2:
                    emit_mult2(pend.pop(0))
            while pend:
                emit_mult2(pend.pop(0))
            hw_done = p.sV.n
            if dbg2 is not None and th == 0 and e == 0:
                p.wait("sync", p.sV, hw_done)
                p.dma("sync", dbg2[:, 2048:2560], ubf[0][:].bitcast(F32), inc=sdbg2)
                p.wait("vector", sdbg2)
                p.wait("tensor", sdbg2)
            for ps in bcs:
                ps.free = (p.sV, hw_done)
            w2_rdy = [0, 0]

            def fetch_w2(dch_, e_=e):
                p.wait("gpsimd", p.sT, w2bb_free[dch_ % 2])
                p.dma("gpsimd", w2bb[dch_ % 2][:].rearrange("p (h c) -> p h c", h=16),
                      w2t[e_, dch_].rearrange("h p c -> p h c"))
                w2_rdy[dch_ % 2] = p.sW.n

            fetch_w2(0)
            fetch_w2(1)
            for dch in range(NDT):
                w2c = w2bb[dch % 2]
                w2_ready = w2_rdy[dch % 2]
                for tc2 in range(2):
                    usl = slice(tc2 * 512, (tc2 + 1) * 512)
                    pp = st.psA if tc2 % 2 == 0 else st.psB
                    st.tfree(pp)
                    p.wait("tensor", p.sW, w2_ready)
                    if dch == 0:
                        p.wait("tensor", p.sV, hw_half[tc2])
                    for ht in range(16):
                        p.op("tensor", lambda t, o=pp.t[:, :],
                             l=w2c[:, ht * 128:(ht + 1) * 128], r_=ubf[ht][:, usl],
                             f=(ht == 0), la=(ht == 15):
                             t.matmul(o, l, r_, start=f, stop=la),
                             inc=(p.sT if ht == 15 else None))
                    pp.ready = (p.sT, p.sT.n)
                    p.wait("vector", *pp.ready)
                    if e == 0:
                        p.op("vector", lambda v, o=outac[dch][:, usl], i=pp.t[:, :]:
                             v.tensor_copy(o, i), inc=p.sV)
                    else:
                        p.op("vector", lambda v, o=outac[dch][:, usl], i=pp.t[:, :]:
                             v.tensor_tensor(o, i, o, ALU.add), inc=p.sV)
                    pp.free = (p.sV, p.sV.n)
                w2bb_free[dch % 2] = p.sT.n
                if dch + 2 < NDT:
                    fetch_w2(dch + 2)
            if dbg2 is not None and th == 0 and e == 0:
                p.wait("sync", p.sV)
                p.dma("sync", dbg2[:, 1024:2048], outac[0][:], inc=sdbg2)
                p.wait("vector", sdbg2)
                p.wait("tensor", sdbg2)
        if e != 3:
            continue
        if dbg2 is not None and th == 0:
            p.wait("sync", p.sV)
            p.dma("sync", dbg2[:, 3072:4096], outac[0][:], inc=sdbg2)
            p.wait("vector", sdbg2)
            p.wait("tensor", sdbg2)
        pS1, pS2, pQ1, pQ2 = st.psC, st.psD, st.psE, st.psF
        for ps in (pS1, pS2, pQ1, pQ2):
            st.tfree(ps)
        oln_cp = {}
        oln_sq = {}
        oln_tfin = [0, 0]
        oln_pend = []

        def oln_stats(dc_):
            par_ = dc_ % 2
            ob_ = obf2[par_]
            sq_ = sqb2[par_]
            p.wait("tensor", p.sV, oln_cp[dc_])
            p.wait("tensor", p.sS, oln_sq[dc_])
            for tc2 in range(2):
                usl = slice(tc2 * 512, (tc2 + 1) * 512)
                p.op("tensor", lambda t, o=(pS1 if tc2 == 0 else pS2).t[0:1, :],
                     l=st.ones_col[:, 0:1], r_=ob_[:, usl],
                     f=(dc_ == 0), la=(dc_ == NDT - 1):
                     t.matmul(o, l, r_, start=f, stop=la))
                p.op("tensor", lambda t, o=(pQ1 if tc2 == 0 else pQ2).t[0:1, :],
                     l=st.ones_col[:, 0:1], r_=sq_[:, usl],
                     f=(dc_ == 0), la=(dc_ == NDT - 1):
                     t.matmul(o, l, r_, start=f, stop=la),
                     inc=(p.sT if tc2 == 1 else None))
            oln_tfin[par_] = p.sT.n

        for dch in range(NDT):
            par = dch % 2
            if dch >= 2:
                p.wait("vector", p.sT, oln_tfin[par])
            else:
                p.wait("vector", p.sT)
            p.op("vector", lambda v, o=obf2[par][:], i=outac[dch][:]:
                 v.tensor_copy(o, i), inc=p.sV)
            oln_cp[dch] = p.sV.n
            p.wait("scalar", p.sV, oln_cp[dch])
            if dch >= 2:
                p.wait("scalar", p.sT, oln_tfin[par])
            else:
                p.wait("scalar", p.sT)
            p.op("scalar", lambda s, o=sqb2[par][:], i=outac[dch][:]:
                 s.activation(out=o, in_=i, func=AF.Square), inc=p.sS)
            oln_sq[dch] = p.sS.n
            oln_pend.append(dch)
            if len(oln_pend) >= 2:
                oln_stats(oln_pend.pop(0))
        while oln_pend:
            oln_stats(oln_pend.pop(0))
        stats_done = p.sT.n
        for ps in (pS1, pS2, pQ1, pQ2):
            ps.ready = (p.sT, stats_done)
        p.wait("vector", p.sT, stats_done)
        for tc2, (psS, psQ) in enumerate(((pS1, pQ1), (pS2, pQ2))):
            mu_ = rowf[0:1, tc2 * 512:(tc2 + 1) * 512]
            v2_ = rowf[0:1, 1024 + tc2 * 512:1024 + (tc2 + 1) * 512]
            ms_ = rowf[0:1, 2048 + tc2 * 512:2048 + (tc2 + 1) * 512]
            p.op("vector", lambda v, o=mu_, i=psS.t[0:1, :]:
                 v.tensor_scalar_mul(o, i, 1.0 / D))
            p.op("vector", lambda v, o=v2_, i=psQ.t[0:1, :]:
                 v.tensor_scalar_mul(o, i, 1.0 / D), inc=p.sV)
            p.op("vector", lambda v, o=ms_, a=mu_: v.tensor_tensor(o, a, a, ALU.mult))
            p.op("vector", lambda v, o=v2_, a=ms_, b=v2_: v.scalar_tensor_tensor(
                 out=o, in0=a, scalar=-1.0, in1=b, op0=ALU.mult, op1=ALU.add))
            p.op("vector", lambda v, o=v2_, a=v2_:
                 v.tensor_scalar_add(o, a, EPS_LN), inc=p.sV)
            p.wait("scalar", p.sV)
            p.op("scalar", lambda s, o=v2_, i=v2_:
                 s.activation(out=o, in_=i, func=AF.Ln), inc=p.sS)
            p.wait("scalar", p.sS)
            p.op("scalar", lambda s, o=v2_, i=v2_:
                 s.activation(out=o, in_=i, func=AF.Exp, scale=-0.5), inc=p.sS)
            p.wait("vector", p.sS)
            p.op("vector", lambda v, o=ms_, a=mu_, b=v2_: v.scalar_tensor_tensor(
                 out=o, in0=a, scalar=-1.0, in1=b, op0=ALU.mult, op1=ALU.mult))
            p.op("vector", lambda v, o=rowb2[0:1, 0:512], i=v2_: v.tensor_copy(o, i))
            p.op("vector", lambda v, o=rowb2[0:1, 512:1024], i=ms_:
                 v.tensor_copy(o, i), inc=p.sV)
            orow_done = p.sV.n
            import os as _os3
            if _os3.environ.get("F1DBG4") and th == 0 and tc2 == 0:
                dbg4 = p.nc.dram_tensor("dbg4", [1, 3 * 512], F32, kind="ExternalOutput").ap()
                sdbg4 = p.sem("f1dbg4")
                p.wait("sync", p.sV, orow_done)
                p.dma("sync", dbg4[0:1, 0:512], mu_, inc=sdbg4)
                p.dma("sync", dbg4[0:1, 512:1024], v2_, inc=sdbg4)
                p.dma("sync", dbg4[0:1, 1024:1536], ms_, inc=sdbg4)
                p.wait("sync", sdbg4)
            psS.free = psQ.free = (p.sV, orow_done)
            pA, pB = st.psA, st.psB
            for ps, rsrc in ((pA, rowb2[0:1, 0:512]), (pB, rowb2[0:1, 512:1024])):
                st.tfree(ps)
                p.wait("tensor", p.sV, orow_done)
                p.op("tensor", lambda t, o=ps.t[:, :], l=onb[:, :], r_=rsrc:
                     t.matmul(o, l, r_, start=True, stop=True), inc=p.sT)
                ps.ready = (p.sT, p.sT.n)
            p.wait("vector", *pA.ready)
            p.wait("vector", *pB.ready)
            for dch in range(NDT):
                usl = slice(tc2 * 512, (tc2 + 1) * 512)
                gsl = slice(t0 + tc2 * 512, t0 + (tc2 + 1) * 512)
                xr = xres[dch % 2]
                p.wait("sync", p.sV)
                p.wait("sync", sOut, 16 * max(0, f1_nout - 1))
                f1_nout += 1
                p.dma("sync", xr[:], xin[dch * 128:(dch + 1) * 128, gsl])
                xr_ready = p.sD.n
                p.op("vector", lambda v, o=st.scr[:], a=outac[dch][:, usl],
                     bc=pA.t[:, :]: v.tensor_tensor(o, a, bc, ALU.mult))
                p.op("vector", lambda v, o=st.scr[:], a=st.scr[:], bc=pB.t[:, :]:
                     v.tensor_tensor(o, a, bc, ALU.add))
                p.op("vector", lambda v, o=st.scr[:], i=st.scr[:], dc=dch:
                     v.tensor_scalar(out=o, in0=i, scalar1=ong[:, dc:dc + 1],
                                     scalar2=onbv[:, dc:dc + 1], op0=ALU.mult,
                                     op1=ALU.add))
                p.wait("vector", p.sD, xr_ready)
                p.op("vector", lambda v, o=xr[:], a=st.scr[:], b=xr[:]:
                     v.tensor_tensor(o, a, b, ALU.add), inc=p.sV)
                p.wait("scalar", p.sV)
                p.dma("scalar", yout[dch * 128:(dch + 1) * 128, gsl], xr[:], inc=sOut)
            pA.free = pB.free = (p.sV, p.sV.n)
    p.wait("sync", sOut)
    return (sOut, sOut.n)


def prep_ffn1_weights(router_w, rn_g, rn_b, w1, ln_g, ln_b, w2, on_g, on_b):
    lngt = np.zeros((128, 64), np.float32)
    lnbt = np.zeros((128, 64), np.float32)
    for e in range(4):
        for ht in range(16):
            lngt[:, e * 16 + ht] = ln_g[e, ht * 128:(ht + 1) * 128]
            lnbt[:, e * 16 + ht] = ln_b[e, ht * 128:(ht + 1) * 128]
    tile8 = lambda vec: np.ascontiguousarray(vec.reshape(8, 128).T)
    return {
        "rwT": np.ascontiguousarray(router_w.T),
        "rngt": tile8(rn_g), "rnbt": tile8(rn_b),
        "w1": np.ascontiguousarray(w1).astype(BF),
        "lngt": lngt, "lnbt": lnbt,
        "w2t": np.ascontiguousarray(
            w2.reshape(4, 16, 128, 8, 128).transpose(0, 3, 1, 2, 4)).astype(BF),
        "ongt": tile8(on_g), "onbt": tile8(on_b),
        "idf32": np.eye(128, dtype=np.float32),
        "ones_bf": np.ones((1, 128), BF),
    }


def build_ffn1_launch():
    p = Prog()
    xT = p.inp("xT", [D, T], F32)
    rwT = p.inp("rwT", [D, 4], F32)
    rngt = p.inp("rngt", [128, NDT], F32)
    rnbt = p.inp("rnbt", [128, NDT], F32)
    w1 = p.inp("w1", [4, D, 2048], BF16)
    lngt = p.inp("lngt", [128, 64], F32)
    lnbt = p.inp("lnbt", [128, 64], F32)
    w2t = p.inp("w2t", [4, 8, 16, 128, 128], BF16)
    ongt = p.inp("ongt", [128, NDT], F32)
    onbt = p.inp("onbt", [128, NDT], F32)
    idf32 = p.inp("idf32", [128, 128], F32)
    ones_bf = p.inp("ones_bf", [1, 128], BF16)
    consts = declare_const_inputs(p)
    yT = p.outp("yT", [D, T], F32)
    stt = St(p, with_x=False)
    stt.load_consts(*consts)
    ubf = [p.sb(f"f1u{i}", [128, 1024], BF16) for i in range(16)]
    xstr = [ubf[i][:].bitcast(F32) for i in range(NDT)]
    mut = p.sb("f1mut", [128, 16], F32)
    vart = p.sb("f1vt", [128, 16], F32)
    idfs = p.sb("f1id", [128, 128], F32)
    p.dma("sync", idfs[:], idf32)
    idf_ld = p.sD.n
    pF = stt.psF

    def row_stash(tc, rows_val):
        p.wait("tensor", p.sD, idf_ld)
        p.wait("tensor", p.sV, rows_val)
        stt.tfree(pF)
        for c in range(4):
            p.op("tensor", lambda t, o=pF.t[:, c:c + 1],
                 i=stt.rows[0:1, c * 128:(c + 1) * 128]:
                 t.transpose(o, i, idfs[0:1, 0:1]))
        for c in range(4):
            p.op("tensor", lambda t, o=pF.t[:, 8 + c:9 + c],
                 i=stt.rows[0:1, 3 * 512 + c * 128:3 * 512 + (c + 1) * 128]:
                 t.transpose(o, i, idfs[0:1, 0:1]),
                 inc=(p.sT if c == 3 else None))
        tdone = p.sT.n
        pF.ready = (p.sT, tdone)
        p.wait("vector", *pF.ready)
        p.op("vector", lambda v, o=mut[:, tc * 4:(tc + 1) * 4], i=pF.t[:, 0:4]:
             v.tensor_copy(o, i))
        p.op("vector", lambda v, o=vart[:, tc * 4:(tc + 1) * 4], i=pF.t[:, 8:12]:
             v.tensor_copy(o, i), inc=p.sV)
        pF.free = (p.sV, p.sV.n)
        return tdone
    xn_ready = emit_preln(stt, x_hbm=xT, xstr=xstr, row_stash=row_stash)

    def x_src(tc):
        sl = slice(tc * 512, (tc + 1) * 512)

        def ld():
            p.wait("sync", p.sT)
            for dt in range(NDT):
                p.dma("sync", xstr[dt][:, 0:512], xT[dt * 128:(dt + 1) * 128, sl])
        return ld, [xstr[dt][:, 0:512] for dt in range(NDT)]

    emit_ffn1(stt, rwT, rngt, rnbt, w1, lngt, lnbt, w2t, ongt, onbt,
              mut, vart, x_src, idfs, ones_bf, xn_ready,
              resid_stream=(xT, yT), ubf=ubf)
    return p.finish()



def _a0_dummy_out(p, st, resid_stream):
    sOutD = p.sem("a0dum")
    p.wait("sync", p.sV)
    p.wait("sync", p.sS)
    p.wait("sync", p.sT)
    p.dma("sync", resid_stream[1][0:128, 0:512], st.scr[:], inc=sOutD)
    p.wait("sync", sOutD)


def emit_attn_a0(st: St, wrT, wkT, wvT, wwT, woT, tmt, ut, lnw64, lnb64, idbf,
                 mask_scan, mask32, xn_ready, resid_stream, vtok=None):
    import os
    PH = int(os.environ.get("A0PHASE", "9"))
    p = st.p
    rT = [p.sb(f"a0r{i}", [128, T], BF16) for i in range(NDT)]
    kT = [p.sb(f"a0k{i}", [128, T], BF16) for i in range(NDT)]
    if vtok is None:
        vtok = [p.sb(f"a0v{i}", [128, 1024], BF16) for i in range(16)]
    osb = vtok
    MIX = [p.sb(f"a0m{i}", [128, T], BF16) for i in range(NDT)]
    ktok = [MIX[i // 2][:, (i % 2) * 1024:(i % 2 + 1) * 1024] for i in range(16)]
    rtl = st.XN
    oT = st.XN
    wbuf = [p.sb(f"a0w{i}", [128, 1024], BF16) for i in range(NDT)]
    Ssb = p.sb("a0S", [128, 512], F32)
    Sbf = p.sb("a0Sb", [128, 512], BF16)
    Dsb = p.sb("a0D", [128, 512], BF16)
    csb = p.sb("a0c", [128, 256], F32)
    negw = p.sb("a0nw", [128, 512], F32)
    poscum = p.sb("a0pc", [128, 512], F32)
    grows = p.sb("a0gr", [128, 3 * 16], F32)
    pbf = p.sb("a0pb", [128, 512], BF16)
    tmv = p.sb("a0tm", [128, 32], F32)
    uv = p.sb("a0u", [128, NDT], F32)
    lnwv = p.sb("a0lw", [128, 64], F32)
    lnbv = p.sb("a0lb", [128, 64], F32)
    idb = p.sb("a0id", [128, 128], BF16)
    msc = p.sb("a0ms", [128, 512], BF16)
    m32 = p.sb("a0m32", [32, 32], BF16)
    hofF = p.sb("a0hF", [128, 2], F32)
    onec1 = p.sb("a0one1", [128, 1], F32)
    xres = [p.sb(f"a0xr{i}", [128, 512], F32) for i in range(2)]
    for dst_, src_ in ((tmv, tmt), (uv, ut), (lnwv, lnw64), (lnbv, lnb64),
                      (idb, idbf), (m32, mask32)):
        p.dma("sync", dst_[:], src_)
    consts_ld = p.sD.n
    p.op("vector", lambda v: v.memset(msc[:, :], 1.0))
    p.op("vector", lambda v: v.memset(
        msc[:].rearrange("p (c q) -> p c q", q=32)[:, :, 0:1], 0.0), inc=p.sV)
    p.op("vector", lambda v: v.memset(hofF[:, :], 0.0))
    p.op("vector", lambda v: v.memset(hofF[0:64, 0:1], 1.0))
    p.op("vector", lambda v: v.memset(hofF[64:128, 1:2], 1.0))
    p.op("vector", lambda v: v.memset(onec1[:, :], 1.0))
    p.op("vector", lambda v: v.memset(Ssb[:, :], 0.0))
    p.op("vector", lambda v: v.memset(Sbf[:, :], 0.0), inc=p.sV)
    p.nc.const_aps.register(F32, 1.0, onec1[:, 0:1])

    def build_mix(col):
        p.wait("vector", p.sT)
        p.wait("vector", p.sD, consts_ld)
        mix_tcs = []
        for tc in range(NTC):
            for dt in range(NDT):
                if tc == 0:
                    p.op("vector", lambda v, o=MIX[dt][:, 0:1], i=st.XN[dt][:, 0:1],
                         tm=tmv[:, 4 * dt + col:4 * dt + col + 1]:
                         v.tensor_scalar_mul(o, i, tm))
                lo1 = max(tc * 512, 1)
                n = 512 if tc > 0 else 511
                p.op("vector", lambda v, o=st.scr[:, 0:n], n=n,
                     a=st.XN[dt][:, lo1:lo1 + n], b=st.XN[dt][:, lo1 - 1:lo1 + n - 1]:
                     v.tensor_tensor(o, a, b, ALU.subtract))
                p.op("vector", lambda v, o=MIX[dt][:, lo1:lo1 + n], n=n,
                     a=st.scr[:, 0:n], b=st.XN[dt][:, lo1 - 1:lo1 + n - 1],
                     tm=tmv[:, 4 * dt + col:4 * dt + col + 1]:
                     v.scalar_tensor_tensor(out=o, in0=a, scalar=tm, in1=b,
                                            op0=ALU.mult, op1=ALU.add),
                     inc=(p.sV if dt == NDT - 1 else None))
            mix_tcs.append(p.sV.n)
        return mix_tcs

    def proj(wsrc, outmode, dsts, sig=False, mix_tcs=None):
        p.wait("gpsimd", p.sT)
        for dt in range(NDT):
            p.dma("gpsimd", wbuf[dt][:], wsrc[dt * 128:(dt + 1) * 128, :])
        w_ready = p.sW.n
        if outmode == "feat":
            for tc in range(NTC):
                for ec in range(NDT):
                    sl = slice(tc * 512, (tc + 1) * 512)
                    pp = st.psA if (tc * NDT + ec) % 2 == 0 else st.psB
                    st.tfree(pp)
                    p.wait("tensor", p.sW, w_ready)
                    if ec == 0:
                        p.wait("tensor", p.sV,
                               mix_tcs[tc] if mix_tcs else None)
                    for dt in range(NDT):
                        p.op("tensor", lambda t, o=pp.t[:, :],
                             l=wbuf[dt][:, ec * 128:(ec + 1) * 128],
                             r_=MIX[dt][:, sl], f=(dt == 0), la=(dt == NDT - 1):
                             t.matmul(o, l, r_, start=f, stop=la),
                             inc=(p.sT if dt == NDT - 1 else None))
                    pp.ready = (p.sT, p.sT.n)
                    if sig:
                        p.wait("scalar", *pp.ready)
                        p.op("scalar", lambda s, o=dsts[ec][:, sl], i=pp.t[:, :]:
                             s.activation(out=o, in_=i, func=AF.Sigmoid), inc=p.sS)
                        pp.free = (p.sS, p.sS.n)
                    else:
                        p.wait("vector", *pp.ready)
                        p.op("vector", lambda v, o=dsts[ec][:, sl], i=pp.t[:, :]:
                             v.tensor_copy(o, i), inc=p.sV)
                        pp.free = (p.sV, p.sV.n)
        else:
            for tt in range(16):
                for ec in range(2):
                    pp = st.psA if (tt * 2 + ec) % 2 == 0 else st.psB
                    st.tfree(pp)
                    p.wait("tensor", p.sW, w_ready)
                    if ec == 0:
                        p.wait("tensor", p.sV,
                               mix_tcs[tt // 4] if mix_tcs else None)
                    for dt in range(NDT):
                        p.op("tensor", lambda t, o=pp.t[:, :],
                             l=MIX[dt][:, tt * 128:(tt + 1) * 128],
                             r_=wbuf[dt][:, ec * 512:(ec + 1) * 512],
                             f=(dt == 0), la=(dt == NDT - 1):
                             t.matmul(o, l, r_, start=f, stop=la),
                             inc=(p.sT if dt == NDT - 1 else None))
                    pp.ready = (p.sT, p.sT.n)
                    p.wait("vector", *pp.ready)
                    p.op("vector", lambda v, o=dsts[tt][:, ec * 512:(ec + 1) * 512],
                         i=pp.t[:, :]: v.tensor_copy(o, i), inc=p.sV)
                    pp.free = (p.sV, p.sV.n)

    m0 = build_mix(0)
    proj(wrT, "feat", rT, sig=True, mix_tcs=m0)
    m1 = build_mix(1)
    proj(wkT, "feat", kT, mix_tcs=m1)
    m2 = build_mix(2)
    proj(wvT, "tok", vtok, mix_tcs=m2)
    build_mix(3)

    if PH < 2:
        _a0_dummy_out(p, st, resid_stream)
        return
    p.wait("gpsimd", p.sT)
    for dt in range(NDT):
        p.dma("gpsimd", wbuf[dt][:], wwT[dt * 128:(dt + 1) * 128, :])
    ww_ready = p.sW.n
    pcb = st.psC
    st.tfree(pcb)
    for ec in range(NDT):
        for tc in range(NTC):
            gsl = slice(tc * 512, (tc + 1) * 512)
            pp = st.psA if tc % 2 == 0 else st.psB
            st.tfree(pp)
            p.wait("tensor", p.sW, ww_ready)
            if ec == 0 and tc == 0:
                p.wait("tensor", p.sV)
            for dt in range(NDT):
                p.op("tensor", lambda t, o=pp.t[:, :],
                     l=wbuf[dt][:, ec * 128:(ec + 1) * 128],
                     r_=MIX[dt][:, gsl], f=(dt == 0), la=(dt == NDT - 1):
                     t.matmul(o, l, r_, start=f, stop=la),
                     inc=(p.sT if dt == NDT - 1 else None))
            pp.ready = (p.sT, p.sT.n)
            p.wait("scalar", *pp.ready)
            p.wait("scalar", p.sV)
            p.op("scalar", lambda s, o=poscum[:, 0:512], i=pp.t[:, :]:
                 s.activation(out=o, in_=i, func=AF.Abs), inc=p.sS)
            abs_done = p.sS.n
            p.op("scalar", lambda s, o=poscum[:, 0:512], i=poscum[:, 0:512]:
                 s.activation(out=o, in_=i, func=AF.Exp, scale=-1.0))
            p.op("scalar", lambda s, o=poscum[:, 0:512], i=poscum[:, 0:512]:
                 s.activation(out=o, in_=i, func=AF.Ln, bias=onec1[:, 0:1]),
                 inc=p.sS)
            p.wait("vector", p.sS, abs_done)
            p.op("vector", lambda v, o=negw[:, 0:512], i=pp.t[:, :]:
                 v.tensor_scalar_mul(o, i, -1.0), inc=p.sV)
            pp.free = (p.sV, p.sV.n)
            p.op("vector", lambda v, o=negw[:, 0:512], i=negw[:, 0:512]:
                 v.tensor_relu(o, i))
            p.wait("vector", p.sS)
            p.op("vector", lambda v, o=negw[:, 0:512], a=negw[:, 0:512],
                 b=poscum[:, 0:512]:
                 v.scalar_tensor_tensor(out=o, in0=a, scalar=1e-4, in1=b,
                                        op0=ALU.add, op1=ALU.add), inc=p.sV)
            p.wait("vector", p.sV)
            p.op("vector", lambda v: v.tensor_tensor_scan(
                out=poscum[:, 0:512], data0=msc[:, 0:512], data1=negw[:, 0:512],
                initial=0.0, op0=ALU.mult, op1=ALU.add), inc=p.sV)
            pc_done = p.sV.n
            p.wait("scalar", p.sV, pc_done)
            p.op("scalar", lambda s,
                 o=Dsb[:, ec * 64 + tc * 16:ec * 64 + tc * 16 + 16],
                 i=poscum[:, 0:512].rearrange("p (c q) -> p c q", q=32)[:, :, 31:32]:
                 s.activation(out=o, in_=i, func=AF.Exp, scale=-1.0))
            p.op("scalar", lambda s,
                 o=negw[:, 0:512].rearrange("p (c q) -> p c q", q=32)[:, :, 1:32],
                 i=poscum[:, 0:512].rearrange("p (c q) -> p c q", q=32)[:, :, 0:31]:
                 s.activation(out=o, in_=i, func=AF.Exp, scale=-1.0), inc=p.sS)
            p.wait("vector", p.sS)
            p.op("vector", lambda v,
                 o=negw[:, 0:512].rearrange("p (c q) -> p c q", q=32)[:, :, 0:1]:
                 v.memset(o, 1.0), inc=p.sV)
            p.wait("vector", p.sV)
            p.op("vector", lambda v, o=rtl[ec][:, gsl], a=rT[ec][:, gsl],
                 b=negw[:, 0:512]:
                 v.tensor_tensor(o, a, b, ALU.mult), inc=p.sV)
            rt_done = p.sV.n
            p.wait("scalar", p.sV, rt_done)
            p.op("scalar", lambda s, o=negw[:, 0:512], i=kT[ec][:, gsl],
                 b=uv[:, ec:ec + 1]:
                 s.activation(out=o, in_=i, func=AF.Exp, bias=b), inc=p.sS)
            p.wait("vector", p.sS)
            p.op("vector", lambda v, o=negw[:, 0:512], a=rT[ec][:, gsl],
                 b=negw[:, 0:512]:
                 v.tensor_tensor(o, b, a, ALU.mult), inc=p.sV)
            re_done = p.sV.n
            p.wait("tensor", p.sV, re_done)
            for tl in range(4):
                tt = tc * 4 + tl
                p.op("tensor", lambda t,
                     o=pcb.t[:, tt * 16 + 2 * ec:tt * 16 + 2 * ec + 2],
                     l=negw[:, tl * 128:(tl + 1) * 128], r_=hofF[:, :]:
                     t.matmul(o, l, r_, start=True, stop=True),
                     inc=(p.sT if tl == 3 else None))
            cmm_done = p.sT.n
            p.wait("scalar", p.sT, cmm_done)
            p.op("scalar", lambda s, o=negw[:, 0:512], i=poscum[:, 0:512]:
                 s.activation(out=o, in_=i, func=AF.Exp), inc=p.sS)
            p.wait("vector", p.sS)
            p.op("vector", lambda v, o=kT[ec][:, gsl], a=kT[ec][:, gsl],
                 b=negw[:, 0:512]:
                 v.tensor_tensor(o, a, b, ALU.mult), inc=p.sV)
    prep_done = p.sV.n
    wo_can_load = p.sT.n
    pcb.ready = (p.sT, p.sT.n)
    p.wait("vector", *pcb.ready)
    p.op("vector", lambda v: v.tensor_copy(csb[:], pcb.t[:, 0:256]), inc=p.sV)
    pcb.free = (p.sV, p.sV.n)

    if PH < 3:
        _a0_dummy_out(p, st, resid_stream)
        return
    pt = st.psD
    ptb = pt.t[:].bitcast(BF16)
    for dt in range(NDT):
        for tt in range(16):
            st.tfree(pt)
            if dt == 0 and tt == 0:
                p.wait("tensor", p.sV, prep_done)
            p.op("tensor", lambda t, o=ptb[:, 0:128],
                 i=kT[dt][:, tt * 128:(tt + 1) * 128]:
                 t.transpose(o, i, idb[:]), inc=p.sT)
            pt.ready = (p.sT, p.sT.n)
            p.wait("vector", *pt.ready)
            p.op("vector", lambda v, o=ktok[tt][:, dt * 128:(dt + 1) * 128],
                 i=ptb[:, 0:128]: v.tensor_copy(o, i), inc=p.sV)
            pt.free = (p.sV, p.sV.n)
    ktok_done = p.sV.n

    if PH < 4:
        _a0_dummy_out(p, st, resid_stream)
        return
    SUB = int(os.environ.get("A0SUB", "15"))
    NTT = int(os.environ.get("A0TT", "16"))
    NCC = int(os.environ.get("A0CC", "4"))
    pG = st.psC
    pP = st.psD
    poI = st.psO
    for tt in range(NTT):
        st.tfree(poI)
        st.tfree(st.psE)
        st.tfree(st.psF)
        for cc in range(NCC):
            c = tt * 4 + cc
            rb = cc * 32
            st.tfree(pG)
            if c == 0:
                p.wait("tensor", p.sV, ktok_done)
            if SUB & 1:
                for h in range(16):
                    dt, ldt = h // 2, h % 2
                    p.op("tensor", lambda t, o=pG.t[ldt * 64:(ldt + 1) * 64,
                                                    dt * 64:(dt + 1) * 64],
                         l=ktok[tt][rb:rb + 32, dt * 128 + ldt * 64:dt * 128 + ldt * 64 + 64],
                         r_=vtok[tt][rb:rb + 32, h * 64:(h + 1) * 64],
                         tp=(rb, ldt * 64):
                         t.matmul(o, l, r_, start=True, stop=True, tile_position=tp),
                         inc=(p.sT if h == 15 else None))
            else:
                p.op("tensor", lambda t: t.matmul(
                    pG.t[0:64, 0:64], ktok[tt][0:32, 0:64], vtok[tt][0:32, 0:64],
                    start=True, stop=True), inc=p.sT)
            pG.ready = (p.sT, p.sT.n)
            if SUB & 2:
                p.wait("tensor", p.sT)
                for ldt in range(2):
                    if ldt == 1:
                        p.wait("tensor", p.sT)
                    for dt in range(8):
                        h = dt * 2 + ldt
                        p.op("tensor", lambda t,
                             o=poI.t[rb:rb + 32, h * 64:(h + 1) * 64],
                             l=rtl[dt][ldt * 64:(ldt + 1) * 64, c * 32:(c + 1) * 32],
                             r_=Sbf[ldt * 64:(ldt + 1) * 64, dt * 64:(dt + 1) * 64],
                             tp=(ldt * 64, rb):
                             t.matmul(o, l, r_, start=True, stop=True,
                                      tile_position=tp),
                             inc=(p.sT if dt == 7 else None))
            else:
                p.op("tensor", lambda t: t.matmul(
                    poI.t[0:64, 0:64], ktok[tt][0:32, 0:64], vtok[tt][0:32, 0:64],
                    start=True, stop=True), inc=p.sT)
            ointer_done = p.sT.n
            st.tfree(pP)
            if SUB & 4:
                p.wait("tensor", p.sT)
                for ldt in range(2):
                    if ldt == 1:
                        p.wait("tensor", p.sT)
                    for dt in range(8):
                        h = dt * 2 + ldt
                        p.op("tensor", lambda t, o=pP.t[0:32, h * 32:(h + 1) * 32],
                             l=kT[dt][ldt * 64:(ldt + 1) * 64, c * 32:(c + 1) * 32],
                             r_=rtl[dt][ldt * 64:(ldt + 1) * 64, c * 32:(c + 1) * 32],
                             tp=(ldt * 64, 0):
                             t.matmul(o, l, r_, start=True, stop=True,
                                      tile_position=tp),
                             inc=(p.sT if dt == 7 else None))
            else:
                p.op("tensor", lambda t: t.matmul(
                    pP.t[0:32, 0:32], ktok[tt][0:32, 0:32], vtok[tt][0:32, 0:32],
                    start=True, stop=True), inc=p.sT)
            pP.ready = (p.sT, p.sT.n)
            p.wait("vector", p.sT, ointer_done)
            p.op("vector", lambda v, o=Ssb[:], a=Ssb[:], b=pG.t[:, :]:
                 v.tensor_tensor(o, a, b, ALU.add), inc=p.sV)
            p.wait("vector", p.sV)
            p.op("vector", lambda v,
                 o=Ssb[:].rearrange("p (d j) -> p d j", j=64),
                 a=Ssb[:].rearrange("p (d j) -> p d j", j=64), c_=c:
                 v.tensor_tensor(o, a, Dsb[:].rearrange("p (d c2) -> p d c2", c2=64)
                                 [:, :, c_:c_ + 1].broadcast_to([128, 8, 64]),
                                 ALU.mult), inc=p.sV)
            p.wait("vector", p.sV)
            p.op("vector", lambda v: v.tensor_copy(Sbf[:], Ssb[:]), inc=p.sV)
            pG.free = (p.sV, p.sV.n)
            p.wait("vector", *pP.ready)
            p.op("vector", lambda v,
                 o=pbf[rb:rb + 32, :].rearrange("p (h q) -> p h q", q=32),
                 i=pP.t[0:32, :].rearrange("p (h q) -> p h q", q=32),
                 m=m32[:, :].unsqueeze(1).broadcast_to([32, 16, 32]):
                 v.tensor_tensor(o, i, m, ALU.mult), inc=p.sV)
            pP.free = (p.sV, p.sV.n)
            pmask_done = p.sV.n
            p.wait("tensor", p.sV, pmask_done)
            p.wait("tensor", p.sT)
            if SUB & 8:
                for h in range(16):
                    po_ = st.psE if h < 8 else st.psF
                    hc = h if h < 8 else h - 8
                    p.op("tensor", lambda t, o=po_.t[rb:rb + 32, hc * 64:(hc + 1) * 64],
                         l=pbf[rb:rb + 32, h * 32:(h + 1) * 32],
                         r_=vtok[tt][rb:rb + 32, h * 64:(h + 1) * 64],
                         tp=(rb, rb):
                         t.matmul(o, l, r_, start=True, stop=True, tile_position=tp),
                         inc=(p.sT if h == 15 else None))
            else:
                p.op("tensor", lambda t: t.matmul(
                    st.psE.t[0:64, 0:64], ktok[tt][0:32, 0:64], vtok[tt][0:32, 0:64],
                    start=True, stop=True), inc=p.sT)
        tile_done = p.sT.n
        poI.ready = st.psE.ready = st.psF.ready = (p.sT, tile_done)
        p.wait("vector", p.sT, tile_done)
        for half in range(2):
            hs = slice(half * 512, (half + 1) * 512)
            pix = st.psE if half == 0 else st.psF
            p.op("vector", lambda v,
                 o=st.scr[:].rearrange("p (h j) -> p h j", j=64),
                 a=vtok[tt][:, hs].rearrange("p (h j) -> p h j", j=64),
                 b=csb[:, tt * 16 + half * 8:tt * 16 + (half + 1) * 8]
                 .unsqueeze(2).broadcast_to([128, 8, 64]):
                 v.tensor_tensor(o, a, b, ALU.mult), inc=p.sV)
            p.wait("vector", p.sV)
            p.op("vector", lambda v, o=st.scr[:], a=poI.t[:, hs], b=st.scr[:]:
                 v.tensor_tensor(o, a, b, ALU.add), inc=p.sV)
            p.wait("vector", p.sV)
            p.op("vector", lambda v, o=osb[tt][:, hs], a=pix.t[:, 0:512], b=st.scr[:]:
                 v.tensor_tensor(o, a, b, ALU.add), inc=p.sV)
        poI.free = st.psE.free = st.psF.free = (p.sV, p.sV.n)
    state_done = p.sV.n

    if PH < 5:
        _a0_dummy_out(p, st, resid_stream)
        return
    pr = st.psA
    prb = pr.t[:].bitcast(BF16)
    for tt in range(16):
        o3 = osb[tt][:].rearrange("p (h j) -> p h j", j=64)
        p.wait("vector", p.sV)
        p.op("vector", lambda v, o=grows[:, 0:16], i=o3:
             v.reduce_sum(o, i, axis=AX), inc=p.sV)
        for half in range(2):
            hs = slice(half * 512, (half + 1) * 512)
            p.wait("scalar", p.sV)
            p.op("scalar", lambda s, o=st.scr_bf[:], i=osb[tt][:, hs]:
                 s.activation(out=o, in_=i, func=AF.Square), inc=p.sS)
            p.wait("vector", p.sS)
            p.op("vector", lambda v, o=grows[:, 16 + half * 8:24 + half * 8],
                 i=st.scr_bf[:].rearrange("p (h j) -> p h j", j=64):
                 v.reduce_sum(o, i, axis=AX), inc=p.sV)
        p.wait("vector", p.sV)
        p.op("vector", lambda v: v.tensor_scalar_mul(grows[:, 0:16], grows[:, 0:16],
                                                     1.0 / 64.0), inc=p.sV)
        p.wait("vector", p.sV)
        p.op("vector", lambda v: v.tensor_scalar_mul(grows[:, 16:32], grows[:, 16:32],
                                                     1.0 / 64.0), inc=p.sV)
        p.wait("vector", p.sV)
        p.op("vector", lambda v: v.tensor_tensor(grows[:, 32:48], grows[:, 0:16],
                                                 grows[:, 0:16], ALU.mult), inc=p.sV)
        p.wait("vector", p.sV)
        p.op("vector", lambda v: v.scalar_tensor_tensor(
            out=grows[:, 16:32], in0=grows[:, 32:48], scalar=-1.0,
            in1=grows[:, 16:32], op0=ALU.mult, op1=ALU.add), inc=p.sV)
        p.wait("vector", p.sV)
        p.op("vector", lambda v: v.tensor_scalar_add(grows[:, 16:32], grows[:, 16:32],
                                                     EPS_LN), inc=p.sV)
        p.wait("scalar", p.sV)
        p.op("scalar", lambda s: s.activation(out=grows[:, 16:32], in_=grows[:, 16:32],
                                              func=AF.Ln), inc=p.sS)
        p.wait("scalar", p.sS)
        p.op("scalar", lambda s: s.activation(out=grows[:, 16:32], in_=grows[:, 16:32],
                                              func=AF.Exp, scale=-0.5), inc=p.sS)
        p.wait("vector", p.sS)
        p.op("vector", lambda v: v.scalar_tensor_tensor(
            out=grows[:, 32:48], in0=grows[:, 0:16], scalar=-1.0,
            in1=grows[:, 16:32], op0=ALU.mult, op1=ALU.mult), inc=p.sV)
        p.wait("vector", p.sV)
        p.op("vector", lambda v, o=o3, a=o3,
             b=grows[:, 16:32].unsqueeze(2).broadcast_to([128, 16, 64]):
             v.tensor_tensor(o, a, b, ALU.mult), inc=p.sV)
        p.wait("vector", p.sV)
        p.op("vector", lambda v, o=o3, a=o3,
             b=grows[:, 32:48].unsqueeze(2).broadcast_to([128, 16, 64]):
             v.tensor_tensor(o, a, b, ALU.add), inc=p.sV)
        p.wait("vector", p.sV)
        p.op("vector", lambda v, o=o3, a=o3,
             b=lnwv[:, 0:64].unsqueeze(1).broadcast_to([128, 16, 64]):
             v.tensor_tensor(o, a, b, ALU.mult), inc=p.sV)
        p.wait("vector", p.sV)
        p.op("vector", lambda v, o=o3, a=o3,
             b=lnbv[:, 0:64].unsqueeze(1).broadcast_to([128, 16, 64]):
             v.tensor_tensor(o, a, b, ALU.add), inc=p.sV)
        ln_done = p.sV.n
        for dt in range(NDT):
            st.tfree(pr)
            p.wait("tensor", p.sV, ln_done)
            p.op("tensor", lambda t, o=prb[:, 0:128],
                 i=rT[dt][:, tt * 128:(tt + 1) * 128]:
                 t.transpose(o, i, idb[:]), inc=p.sT)
            pr.ready = (p.sT, p.sT.n)
            p.wait("vector", *pr.ready)
            p.op("vector", lambda v, o=osb[tt][:, dt * 128:(dt + 1) * 128],
                 a=osb[tt][:, dt * 128:(dt + 1) * 128], b=prb[:, 0:128]:
                 v.tensor_tensor(o, a, b, ALU.mult), inc=p.sV)
            pr.free = (p.sV, p.sV.n)
    on_done = p.sV.n
    pt2 = st.psB
    ptb2 = pt2.t[:].bitcast(BF16)
    for dt in range(NDT):
        for tt in range(16):
            st.tfree(pt2)
            if dt == 0 and tt == 0:
                p.wait("tensor", p.sV, on_done)
            p.op("tensor", lambda t, o=ptb2[:, 0:128],
                 i=osb[tt][:, dt * 128:(dt + 1) * 128]:
                 t.transpose(o, i, idb[:]), inc=p.sT)
            pt2.ready = (p.sT, p.sT.n)
            p.wait("vector", *pt2.ready)
            p.op("vector", lambda v, o=oT[dt][:, tt * 128:(tt + 1) * 128],
                 i=ptb2[:, 0:128]: v.tensor_copy(o, i), inc=p.sV)
            pt2.free = (p.sV, p.sV.n)
    oT_done = p.sV.n

    p.wait("gpsimd", p.sT, wo_can_load)
    for dt in range(NDT):
        p.dma("gpsimd", wbuf[dt][:], woT[dt * 128:(dt + 1) * 128, :])
    wo_ready = p.sW.n
    xin, yout = resid_stream
    sOut = p.sem("a0sOut")
    n_out = 0
    for dch in range(NDT):
        for tc in range(NTC):
            sl = slice(tc * 512, (tc + 1) * 512)
            pp = st.psE if tc % 2 == 0 else st.psF
            xr = xres[tc % 2]
            p.wait("sync", p.sV)
            p.wait("sync", sOut, 16 * max(0, n_out - 1))
            n_out += 1
            p.dma("sync", xr[:], xin[dch * 128:(dch + 1) * 128, sl])
            xr_ready = p.sD.n
            st.tfree(pp)
            p.wait("tensor", p.sW, wo_ready)
            if dch == 0 and tc == 0:
                p.wait("tensor", p.sV, oT_done)
            for ei in range(NDT):
                p.op("tensor", lambda t, o=pp.t[:, :],
                     l=wbuf[ei][:, dch * 128:(dch + 1) * 128],
                     r_=oT[ei][:, sl], f=(ei == 0), la=(ei == 7):
                     t.matmul(o, l, r_, start=f, stop=la),
                     inc=(p.sT if ei == 7 else None))
            pp.ready = (p.sT, p.sT.n)
            p.wait("vector", *pp.ready)
            p.wait("vector", p.sD, xr_ready)
            p.op("vector", lambda v, o=xr[:], i=pp.t[:, :], b=xr[:]:
                 v.tensor_tensor(o, i, b, ALU.add), inc=p.sV)
            pp.free = (p.sV, p.sV.n)
            p.wait("scalar", p.sV)
            p.dma("scalar", yout[dch * 128:(dch + 1) * 128, sl], xr[:], inc=sOut)
    p.wait("sync", sOut)
    return (sOut, sOut.n)


def prep_attn_a0_weights(W_r, W_k, W_v, W_w, W_o, u, tm_r, tm_k, tm_v, tm_w,
                         ln_w, ln_b):
    tmt = np.zeros((128, 32), np.float32)
    for dt in range(8):
        for ci, vec in enumerate((tm_r, tm_k, tm_v, tm_w)):
            tmt[:, 4 * dt + ci] = vec[dt * 128:(dt + 1) * 128]
    ut = np.zeros((128, 8), np.float32)
    for dt in range(8):
        ut[:, dt] = np.concatenate([u[2 * dt], u[2 * dt + 1]])
    msk = np.ones((128, 1024), BF)
    msk[:, 0::32] = 0.0
    m32 = (np.arange(32)[:, None] < np.arange(32)[None, :]).astype(BF)
    return {
        "wrT": np.ascontiguousarray(W_r.T).astype(BF),
        "wkT": np.ascontiguousarray(W_k.T).astype(BF),
        "wvT": np.ascontiguousarray(W_v.T).astype(BF),
        "wwT": np.ascontiguousarray(W_w.T).astype(BF),
        "woT": np.ascontiguousarray(W_o.T).astype(BF),
        "tmt": tmt, "ut": ut,
        "lnw64": np.tile(ln_w[None, :], (128, 1)).astype(np.float32),
        "lnb64": np.tile(ln_b[None, :], (128, 1)).astype(np.float32),
        "idbf": np.eye(128, dtype=BF),
        "mask_scan": msk, "mask32": m32,
    }


def build_attn_a0_launch():
    p = Prog()
    xT = p.inp("xT", [D, T], F32)
    ws = {n: p.inp(n, [D, D], BF16) for n in ("wrT", "wkT", "wvT", "wwT", "woT")}
    tmt = p.inp("tmt", [128, 32], F32)
    ut = p.inp("ut", [128, NDT], F32)
    lnw64 = p.inp("lnw64", [128, 64], F32)
    lnb64 = p.inp("lnb64", [128, 64], F32)
    idbf = p.inp("idbf", [128, 128], BF16)
    mask_scan = p.inp("mask_scan", [128, 1024], BF16)
    mask32 = p.inp("mask32", [32, 32], BF16)
    consts = declare_const_inputs(p)
    yT = p.outp("yT", [D, T], F32)
    stt = St(p, with_x=False)
    stt.load_consts(*consts)
    vtok_pre = [p.sb(f"a0v{i}", [128, 1024], BF16) for i in range(16)]
    xstr = [vtok_pre[i][:].bitcast(F32) for i in range(NDT)]
    xn_ready = emit_preln(stt, x_hbm=xT, xstr=xstr)
    emit_attn_a0(stt, ws["wrT"], ws["wkT"], ws["wvT"], ws["wwT"], ws["woT"],
                 tmt, ut, lnw64, lnb64, idbf, mask_scan, mask32, xn_ready,
                 resid_stream=(xT, yT), vtok=vtok_pre)
    return p.finish()


def prep_attn_a1_weights(W_qk, W_v, W_out, q_scale, q_bias, k_scale, k_bias):
    perm = []
    for hp in range(8):
        h0, h1 = 2 * hp, 2 * hp + 1
        perm += list(range(h0 * 128, h0 * 128 + 64))
        perm += list(range(h1 * 128, h1 * 128 + 64))
    qcols = []
    cols = []
    for hp in range(8):
        h0, h1 = 2 * hp, 2 * hp + 1
        cols += list(range(h0 * 128, h0 * 128 + 64))
        cols += list(range(h1 * 128, h1 * 128 + 64))
        cols += list(range(h0 * 128 + 64, h0 * 128 + 128))
        cols += list(range(h1 * 128 + 64, h1 * 128 + 128))
    wqkP = np.ascontiguousarray(W_qk[:, cols])
    kr = np.arange(128)[:, None]
    qf = np.arange(512)[None, :]
    mask4 = np.stack([((kr + off) <= qf).astype(BF) for off in (0, 128, 256, 384)])
    return {
        "wqkP": wqkP.astype(BF),
        "wv": np.ascontiguousarray(W_v).astype(BF),
        "wout": np.ascontiguousarray(W_out).astype(BF),
        "qsb": np.stack([np.concatenate([q_scale, q_scale]),
                         np.concatenate([q_bias, q_bias])], axis=1).astype(np.float32),
        "ksb": np.stack([np.concatenate([k_scale, k_scale]),
                         np.concatenate([k_bias, k_bias])], axis=1).astype(np.float32),
        "mask4": mask4,
        "ones_bf": np.ones((1, 128), BF),
    }


def build_attn_a1_launch():
    p = Prog()
    xT = p.inp("xT", [D, T], F32)
    wqkP = p.inp("wqkP", [D, 2048], BF16)
    wv = p.inp("wv", [D, D], BF16)
    wout = p.inp("wout", [D, D], BF16)
    qsb = p.inp("qsb", [128, 2], F32)
    ksb = p.inp("ksb", [128, 2], F32)
    mask4 = p.inp("mask4", [4, 128, 512], BF16)
    ones_bf = p.inp("ones_bf", [1, 128], BF16)
    consts = declare_const_inputs(p)
    yT = p.outp("yT", [D, T], F32)
    stt = St(p, with_x=False)
    stt.load_consts(*consts)
    xstr = [p.sb(f"xstr{i}", [128, 512], F32) for i in range(NDT)]
    xn_tcs = []
    emit_preln(stt, x_hbm=xT, xstr=xstr, xn_tcs=xn_tcs)
    emit_attn_a1(stt, wqkP, wv, wout, qsb, ksb, mask4, ones_bf, xn_tcs,
                 resid_stream=(xT, yT))
    return p.finish()



def _barrier(p: Prog, done):
    for e in ENGS:
        p.wait(e, done[0], done[1])


def build_fused_launch():
    p = Prog()
    xT = p.inp("xT", [D, T], F32)
    wi = {}
    for pfx, specs in (
        ("a0_", [("wrT", [D, D], BF16), ("wkT", [D, D], BF16),
                 ("wvT", [D, D], BF16),
                 ("wwT", [D, D], BF16), ("woT", [D, D], BF16), ("tmt", [128, 32], F32),
                 ("ut", [128, NDT], F32), ("lnw64", [128, 64], F32),
                 ("lnb64", [128, 64], F32), ("idbf", [128, 128], BF16),
                 ("mask_scan", [128, 1024], BF16), ("mask32", [32, 32], BF16)]),
        ("f0_", [("wgT", [D, 3072], BF16), ("wuT", [D, 3072], BF16),
                 ("wdTt", [24, 8, 128, 128], BF16)]),
        ("b0_", [("wq0", [D, D], BF16), ("wq1", [D, D], BF16),
                 ("wk0", [D, D], BF16),
                 ("wk1", [D, D], BF16), ("wv", [D, D], BF16), ("wo", [D, D], BF16),
                 ("lvec", [64, 4], F32), ("b3l", [3, 16 * 128], BF16),
                 ("b3r", [3, 512], BF16), ("dmask5", [5, 128, 512], F32),
                 ("btbl", [128, 320], F32), ("lnwP", [128, 8], F32),
                 ("lnbP", [128, 8], F32), ("ones_bf", [1, 128], BF16)]),
        ("a1_", [("wqkP", [D, 2048], BF16), ("wv", [D, D], BF16),
                 ("wout", [D, D], BF16),
                 ("qsb", [128, 2], F32), ("ksb", [128, 2], F32),
                 ("mask4", [4, 128, 512], BF16), ("ones_bf", [1, 128], BF16)]),
        ("f1_", [("rwT", [D, 4], F32), ("rngt", [128, NDT], F32),
                 ("rnbt", [128, NDT], F32), ("w1", [4, D, 2048], BF16),
                 ("lngt", [128, 64], F32), ("lnbt", [128, 64], F32),
                 ("w2t", [4, 8, 16, 128, 128], BF16), ("ongt", [128, NDT], F32),
                 ("onbt", [128, NDT], F32), ("idf32", [128, 128], F32),
                 ("ones_bf", [1, 128], BF16)]),
        ("b1_", [("wqP", [D, 512], BF16), ("wkP", [D, 512], BF16),
                 ("wv", [D, D], BF16),
                 ("woT", [D, D], BF16), ("mask4", [4, 128, 512], BF16),
                 ("ones_bf", [1, 128], BF16)]),
    ):
        for nm, shp, dt in specs:
            wi[pfx + nm] = p.inp(pfx + nm, shp, dt)
    consts = declare_const_inputs(p)
    yT = p.outp("yT", [D, T], F32)
    h = [p.nc.dram_tensor(f"hbuf{i}", [D, T], F32, kind="Internal").ap()
         for i in range(2)]
    stt = St(p, with_x=False)
    stt.load_consts(*consts)

    def stage_a0(cur, nxt):
        p.push_scope()
        vtok_pre = [p.sb(f"a0v{i}", [128, 1024], BF16) for i in range(16)]
        xstr = [vtok_pre[i][:].bitcast(F32) for i in range(NDT)]
        xn_ready = emit_preln(stt, x_hbm=cur, xstr=xstr)
        done = emit_attn_a0(stt, wi["a0_wrT"], wi["a0_wkT"], wi["a0_wvT"],
                            wi["a0_wwT"], wi["a0_woT"], wi["a0_tmt"], wi["a0_ut"],
                            wi["a0_lnw64"], wi["a0_lnb64"], wi["a0_idbf"],
                            wi["a0_mask_scan"], wi["a0_mask32"], xn_ready,
                            resid_stream=(cur, nxt), vtok=vtok_pre)
        p.pop_scope()
        return done

    def stage_f0(cur, nxt):
        p.push_scope()
        stt.X = [p.sb(f"X{i}", [128, T], F32) for i in range(NDT)]
        launch_prologue(p, stt, cur)
        xn_tcs = []
        emit_preln(stt, xn_tcs=xn_tcs)
        emit_ffn0(stt, wi["f0_wgT"], wi["f0_wuT"], wi["f0_wdTt"], xn_tcs)
        done = launch_epilogue(p, stt, nxt, (p.sV, p.sV.n))
        stt.X = None
        p.pop_scope()
        return done

    def stage_b0(cur, nxt):
        p.push_scope()
        xstr = [p.sb(f"xstr{i}", [128, 512], F32) for i in range(NDT)]
        xn_tcs = []
        emit_preln(stt, x_hbm=cur, xstr=xstr, xn_tcs=xn_tcs)
        done = emit_attn_b0(stt, wi["b0_wq0"], wi["b0_wq1"], wi["b0_wk0"],
                            wi["b0_wk1"], wi["b0_wv"], wi["b0_wo"], wi["b0_lvec"],
                            wi["b0_b3l"], wi["b0_b3r"],
                            wi["b0_dmask5"], wi["b0_btbl"], wi["b0_lnwP"],
                            wi["b0_lnbP"], wi["b0_ones_bf"], xn_tcs,
                            resid_stream=(cur, nxt))
        p.pop_scope()
        return done

    def stage_a1(cur, nxt):
        p.push_scope()
        xstr = [p.sb(f"xstr{i}", [128, 512], F32) for i in range(NDT)]
        xn_tcs = []
        emit_preln(stt, x_hbm=cur, xstr=xstr, xn_tcs=xn_tcs)
        done = emit_attn_a1(stt, wi["a1_wqkP"], wi["a1_wv"], wi["a1_wout"],
                            wi["a1_qsb"], wi["a1_ksb"], wi["a1_mask4"],
                            wi["a1_ones_bf"], xn_tcs, resid_stream=(cur, nxt))
        p.pop_scope()
        return done

    def stage_f1(cur, nxt):
        p.push_scope()
        ubf = [p.sb(f"f1u{i}", [128, 1024], BF16) for i in range(16)]
        xstr = [ubf[i][:].bitcast(F32) for i in range(NDT)]
        mut = p.sb("f1mut", [128, 16], F32)
        vart = p.sb("f1vt", [128, 16], F32)
        idfs = p.sb("f1id", [128, 128], F32)
        p.dma("sync", idfs[:], wi["f1_idf32"])
        idf_ld = p.sD.n
        pF = stt.psF

        def row_stash(tc, rows_val):
            p.wait("tensor", p.sD, idf_ld)
            p.wait("tensor", p.sV, rows_val)
            stt.tfree(pF)
            for c in range(4):
                p.op("tensor", lambda t, o=pF.t[:, c:c + 1],
                     i=stt.rows[0:1, c * 128:(c + 1) * 128]:
                     t.transpose(o, i, idfs[0:1, 0:1]))
            for c in range(4):
                p.op("tensor", lambda t, o=pF.t[:, 8 + c:9 + c],
                     i=stt.rows[0:1, 3 * 512 + c * 128:3 * 512 + (c + 1) * 128]:
                     t.transpose(o, i, idfs[0:1, 0:1]),
                     inc=(p.sT if c == 3 else None))
            tdone = p.sT.n
            pF.ready = (p.sT, tdone)
            p.wait("vector", *pF.ready)
            p.op("vector", lambda v, o=mut[:, tc * 4:(tc + 1) * 4], i=pF.t[:, 0:4]:
                 v.tensor_copy(o, i))
            p.op("vector", lambda v, o=vart[:, tc * 4:(tc + 1) * 4], i=pF.t[:, 8:12]:
                 v.tensor_copy(o, i), inc=p.sV)
            pF.free = (p.sV, p.sV.n)
            return tdone
        xn_ready = emit_preln(stt, x_hbm=cur, xstr=xstr, row_stash=row_stash)

        def x_src(tc):
            sl = slice(tc * 512, (tc + 1) * 512)

            def ld():
                p.wait("sync", p.sT)
                for dt in range(NDT):
                    p.dma("sync", xstr[dt][:, 0:512],
                          cur[dt * 128:(dt + 1) * 128, sl])
            return ld, [xstr[dt][:, 0:512] for dt in range(NDT)]

        done = emit_ffn1(stt, wi["f1_rwT"], wi["f1_rngt"], wi["f1_rnbt"],
                         wi["f1_w1"], wi["f1_lngt"], wi["f1_lnbt"], wi["f1_w2t"],
                         wi["f1_ongt"], wi["f1_onbt"], mut, vart, x_src,
                         idfs, wi["f1_ones_bf"], xn_ready,
                         resid_stream=(cur, nxt), ubf=ubf)
        p.pop_scope()
        return done

    def stage_b1(cur, nxt):
        p.push_scope()
        xstr = [p.sb(f"xstr{i}", [128, 512], F32) for i in range(NDT)]
        xn_tcs = []
        emit_preln(stt, x_hbm=cur, xstr=xstr, xn_tcs=xn_tcs)
        done = emit_attn_b1(stt, wi["b1_wqP"], wi["b1_wkP"], wi["b1_wv"],
                            wi["b1_woT"], wi["b1_mask4"], wi["b1_ones_bf"],
                            xn_tcs, resid_stream=(cur, nxt))
        p.pop_scope()
        return done

    stages = [stage_a0, stage_f0, stage_b0, stage_f0, stage_a1, stage_f1,
              stage_b1, stage_f1]
    cur = xT
    for i, fn in enumerate(stages):
        nxt = yT if i == len(stages) - 1 else h[i % 2]
        done = fn(cur, nxt)
        if i < len(stages) - 1:
            _barrier(p, done)
        cur = nxt
    return p.finish()


def kernel(**inputs):
    from concourse.bass_utils import run_bass_kernel_spmd
    g = {k: np.asarray(v) for k, v in inputs.items()}
    x = np.asarray(g["L0_attn_a_x"], np.float32)
    cst = const_inputs()
    wa0 = prep_attn_a0_weights(
        g["L0_attn_a_W_r"], g["L0_attn_a_W_k"], g["L0_attn_a_W_v"],
        g["L0_attn_a_W_w"], g["L0_attn_a_W_o"], g["L0_attn_a_u"],
        g["L0_attn_a_tm_r"], g["L0_attn_a_tm_k"], g["L0_attn_a_tm_v"],
        g["L0_attn_a_tm_w"], g["L0_attn_a_ln_w"], g["L0_attn_a_ln_b"])
    wf0 = prep_ffn0_weights(g["L0_ffn_w_gate"], g["L0_ffn_w_up"],
                            g["L0_ffn_w_down"])
    wb0 = prep_attn_b0_weights(
        g["L0_attn_b_wq0"], g["L0_attn_b_wq1"], g["L0_attn_b_wk0"],
        g["L0_attn_b_wk1"], g["L0_attn_b_wv"], g["L0_attn_b_wo"],
        g["L0_attn_b_l0"], g["L0_attn_b_l1"], g["L0_attn_b_l2"],
        g["L0_attn_b_l3"], g["L0_attn_b_ln_w"], g["L0_attn_b_ln_b"])
    wa1 = prep_attn_a1_weights(
        g["L1_attn_a_W_qk"], g["L1_attn_a_W_v"], g["L1_attn_a_W_out"],
        g["L1_attn_a_q_scale"], g["L1_attn_a_q_bias"],
        g["L1_attn_a_k_scale"], g["L1_attn_a_k_bias"])
    wf1 = prep_ffn1_weights(
        g["L1_ffn_router_w"], g["L1_ffn_rn_g"], g["L1_ffn_rn_b"],
        g["L1_ffn_w1"], g["L1_ffn_ln_g"], g["L1_ffn_ln_b"], g["L1_ffn_w2"],
        g["L1_ffn_on_g"], g["L1_ffn_on_b"])
    wb1 = prep_attn_b1_weights(g["L1_attn_b_Wq"], g["L1_attn_b_Wk"],
                               g["L1_attn_b_Wv"], g["L1_attn_b_Wo"])
    wmap = {}
    for pfx, wd in (("a0_", wa0), ("f0_", wf0), ("b0_", wb0), ("a1_", wa1),
                    ("f1_", wf1), ("b1_", wb1)):
        for k, v in wd.items():
            wmap[pfx + k] = v
    nc = build_fused_launch()
    in_maps = [{**wmap, **cst, "xT": np.ascontiguousarray(x[b].T)}
               for b in range(8)]
    res = run_bass_kernel_spmd(nc, in_maps, core_ids=list(range(8)))
    return np.stack([res.results[b]["yT"].T for b in range(8)]).astype(np.float32)
